# version 04
# baseline (speedup 1.0000x reference)
import numpy as np
import ml_dtypes
from contextlib import ExitStack

import concourse.bass as bass
import concourse.mybir as mybir
from concourse.bass_utils import run_bass_kernel_spmd

F32 = mybir.dt.float32
BF16 = mybir.dt.bfloat16
AF = mybir.ActivationFunctionType
ALU = mybir.AluOpType
AX = mybir.AxisListType

NCORE = 8
T = 2048
D = 1024
NQT = T // 128
NCB = D // 128
BF = ml_dtypes.bfloat16

ENGS = ("mm", "vv", "ss", "pp", "sy", "h", "g")
MS = {"tensor": "mm", "vector": "vv", "scalar": "ss", "gpsimd": "pp", "sync": "sy"}


_NOREC = {"dma_start", "wait_ge", "nop", "drain", "print"}


class _Rec:

    def __init__(self, e):
        self._e = e
        self.last = None

    def __getattr__(self, n):
        a = getattr(self._e, n)
        if callable(a) and not n.startswith("_"):
            def w(*args, __a=a, __n=n, **kw):
                r = __a(*args, **kw)
                if __n not in _NOREC and r is not None and hasattr(r, "then_inc"):
                    self.last = r
                return r
            return w
        return a


class PB:
    def __init__(self):
        self.nc = bass.Bass("TRN2", target_bir_lowering=False)
        self.ctx = ExitStack()
        self._sems = {}
        self.c = {}
        for s in ENGS:
            self.sem(s)

    def sem(self, name):
        if name not in self._sems:
            self._sems[name] = self.ctx.enter_context(self.nc.semaphore(name))
            self.c[name] = 0
        return self._sems[name]

    def sb(self, name, p, f, dt, side="left"):
        return self.ctx.enter_context(self.nc.sbuf_tensor(name, [p, f], dt, side=side))

    def sbm(self, name, p, f, dt, side="left"):
        cm = self.nc.sbuf_tensor(name, [p, f], dt, side=side)
        t = cm.__enter__()
        if not hasattr(self, "_stack"):
            self._stack = []
        self._stack.append((name, cm))
        return t

    def free(self, name):
        n, cm = self._stack.pop()
        assert n == name, f"free order: want {name} got {n}"
        cm.__exit__(None, None, None)

    def ps(self, name, f, dt=F32):
        return self.ctx.enter_context(self.nc.psum_tensor(name, [128, f], dt))

    def begin(self):
        self.block = self.ctx.enter_context(self.nc.Block())

    def step(self, **fns):
        snap = dict(self.c)
        S = self._sems
        if not hasattr(self, "_lastw"):
            self._lastw = {}
        for name, fn in fns.items():
            ms = MS[name]
            lw = self._lastw.setdefault(name, {s: 0 for s in ENGS})

            def body(e, fn=fn, ms=ms, lw=lw):
                for s in ENGS:
                    if snap[s] > lw[s]:
                        e.wait_ge(S[s], snap[s])
                        lw[s] = snap[s]
                rec = _Rec(e)
                fn(rec)
                if rec.last is not None:
                    rec.last.then_inc(S[ms], 1)
                else:
                    e.nop().then_inc(S[ms], 1)

            getattr(self.block, name)(body)
            self.c[ms] += 1

    def dma(self, e, out, in_, q):
        sem = "h" if q == "h" else "g"
        e.dma_start(out=out, in_=in_).then_inc(self._sems[sem], 16)
        self.c[sem] += 16

    def done(self):
        for n, cm in reversed(getattr(self, "_stack", [])):
            cm.__exit__(None, None, None)
        self._stack = []
        self.ctx.close()
        return self.nc


LIMIT = 99
H_LIMIT = 99

def build_launchA():
    pb = PB()
    nc = pb.nc

    def din(name, shape):
        return nc.dram_tensor(name, list(shape), F32, kind="ExternalInput").ap()

    x0 = din("x0", (T, D))
    x0T_d = din("x0Td", (D, T))
    WqT = din("WqT", (D, D)); WkT = din("WkT", (D, D))
    WvT = din("WvT", (D, D)); WoT = din("WoT", (D, D))
    bo = din("bo", (1, D))
    lq1 = din("lq1", (1, 64)); lk1 = din("lk1", (1, 64))
    lq2 = din("lq2", (1, 64)); lk2 = din("lk2", (1, 64))
    dgr = din("dgr", (1, 8)); chd = din("chd", (1, 8)); raw = din("raw", (1, 1))
    subw = din("subw", (128, 1)); pm64 = din("pm64", (128, 1))
    bdo = din("bdo", (128, 128))
    bdm = din("bdm", (32, 128, 128))
    uts = din("uts", (128, 128, 128))
    wcT = din("wcT", (16 * D, D))
    xout = nc.dram_tensor("xout", [T, D], F32, kind="ExternalOutput").ap()
    x1sp = nc.dram_tensor("x1sp", [T, D], F32).ap()

    ident_b = pb.sb("ident_b", 128, 128, BF16)
    ones_b = pb.sb("ones_b", 128, 128, BF16)
    ones_f = pb.sb("ones_f", 128, 128, F32)
    bdo_s = pb.sb("bdo_s", 128, 128, F32)
    bo_s = pb.sb("bo_s", 1, D, BF16)
    lrow = pb.sb("lrow", 1, 512, F32)
    trow = pb.sb("trow", 1, 64, F32)
    crow = pb.sb("crow", 1, 16, F32)
    cs_cols = pb.sb("cs_cols", 128, 8, F32)
    dg_cols = pb.sb("dg_cols", 128, 8, F32)
    lamn_col = pb.sb("lamn_col", 128, 1, F32)
    subw_c = pb.sb("subw_c", 128, 1, F32)
    pm_c = pb.sb("pm_c", 128, 1, F32)
    scrA = pb.sb("scrA", 128, T, F32)
    scrB = pb.sb("scrB", 128, T, F32)
    scrC = pb.sb("scrC", 128, T, F32)
    e1c = pb.sb("e1c", 128, 512, BF16)
    e2c = pb.sb("e2c", 128, 512, BF16)
    gstr = pb.sb("gstr", 128, 512, BF16)
    xt_l = pb.sb("xt_l", 128, D, F32)
    k2s = pb.sb("k2s", 64, T, BF16)
    q2s = pb.sb("q2s", 64, T, BF16)
    xbf = pb.sb("xbf", 128, D, BF16)
    acol = pb.sb("acol", 128, 16, F32)

    kTn = pb.sbm("kTn", 128, NCB * T, BF16, side="right")
    v_b = pb.sbm("v_b", 128, NQT * D, BF16, side="right")
    qTn = pb.sbm("qTn", 128, NCB * T, BF16, side="right")
    wts = pb.sbm("wts", 128, NCB * D, BF16, side="right")
    x0T = pb.sbm("x0T", 128, NCB * T, BF16, side="right")

    P = [pb.ps(f"P{i}", 512) for i in range(7)]
    PT = pb.ps("PT", 1024, BF16)

    pb.begin()

    def l_sync(e):
        pb.dma(e, bdo_s[:, :], bdo, "h")
        pb.dma(e, subw_c[:, :], subw, "h")
        pb.dma(e, pm_c[:, :], pm64, "h")
        pb.dma(e, lrow[0:1, 0:64], lq1, "h")
        pb.dma(e, lrow[0:1, 64:128], lk1, "h")
        pb.dma(e, lrow[0:1, 128:192], lq2, "h")
        pb.dma(e, lrow[0:1, 192:256], lk2, "h")
        pb.dma(e, lrow[0:1, 256:264], dgr, "h")
        pb.dma(e, lrow[0:1, 264:272], chd, "h")
        pb.dma(e, lrow[0:1, 272:273], raw, "h")

    def l_g(e):
        for cb in range(NCB):
            pb.dma(e, x0T[:, cb * T:(cb + 1) * T], x0T_d[cb * 128:(cb + 1) * 128, :], "g")
        pb.dma(e, bo_s[0:1, :], bo, "g")
        e.memset(ones_b[:, :], 1.0)
        e.memset(ones_f[:, :], 1.0)
        e.drain()
        e.affine_select(ident_b[:, :], ones_b[:, :], [[-1, 128]], ALU.is_equal, 0.0,
                        channel_multiplier=1)

    pb.step(sync=l_sync, gpsimd=l_g)

    def p1v(e):
        e.tensor_tensor(trow[0:1, 0:64], lrow[0:1, 0:64], lrow[0:1, 64:128], ALU.mult)
        e.drain()
        e.reduce_sum(crow[0:1, 0:1], trow[0:1, 0:64], AX.X)
        e.drain()
        e.tensor_tensor(trow[0:1, 0:64], lrow[0:1, 128:192], lrow[0:1, 192:256], ALU.mult)
        e.drain()
        e.reduce_sum(crow[0:1, 1:2], trow[0:1, 0:64], AX.X)
        e.drain()
        e.reduce_sum(crow[0:1, 2:3], lrow[0:1, 264:272], AX.X)

    pb.step(vector=p1v)

    def p1s(e):
        e.activation(crow[0:1, 3:4], crow[0:1, 0:1], AF.Exp)
        e.drain()
        e.activation(crow[0:1, 4:5], crow[0:1, 1:2], AF.Exp)
        e.drain()
        e.activation(crow[0:1, 5:6], lrow[0:1, 272:273], AF.Sigmoid)

    pb.step(scalar=p1s)

    def p1v2(e):
        e.tensor_tensor(crow[0:1, 6:7], crow[0:1, 4:5], crow[0:1, 3:4], ALU.subtract)
        e.drain()
        e.tensor_scalar(crow[0:1, 6:7], crow[0:1, 6:7], -0.5, None, ALU.add)
        e.drain()
        e.tensor_scalar(crow[0:1, 7:8], crow[0:1, 5:6], 15.0, None, ALU.mult)
        e.drain()
        e.tensor_scalar(crow[0:1, 2:3], crow[0:1, 2:3], 0.125, None, ALU.mult)
        e.drain()
        e.tensor_scalar(trow[0:1, 0:8], lrow[0:1, 264:272], crow[0:1, 2:3], None,
                        ALU.subtract)

    pb.step(vector=p1v2)

    def p1s2(e):
        e.activation(trow[0:1, 8:16], trow[0:1, 0:8], AF.Tanh)

    pb.step(scalar=p1s2)

    def p1v3(e):
        e.tensor_scalar(trow[0:1, 16:24], trow[0:1, 8:16], 0.5, 1.0, ALU.mult, ALU.add)
        e.drain()
        e.tensor_scalar(trow[0:1, 16:24], trow[0:1, 16:24], crow[0:1, 7:8], None,
                        ALU.mult)
        e.drain()
        e.tensor_scalar(subw_c[:, 0:1], subw_c[:, 0:1], 0.5, None, ALU.mult)

    pb.step(vector=p1v3)

    def p1t(e):
        e.matmul(P[0][:, 0:8], ones_f[0:1, 0:128], trow[0:1, 16:24], start=True, stop=True)
        e.matmul(P[1][:, 0:8], ones_f[0:1, 0:128], lrow[0:1, 256:264], start=True, stop=True)
        e.matmul(P[2][:, 0:1], ones_f[0:1, 0:128], crow[0:1, 6:7], start=True, stop=True)

    pb.step(tensor=p1t)

    def p1v4(e):
        e.tensor_copy(cs_cols[:, :], P[0][:, 0:8])
        e.drain()
        e.tensor_copy(dg_cols[:, :], P[1][:, 0:8])
        e.drain()
        e.tensor_copy(lamn_col[:, :], P[2][:, 0:1])

    pb.step(vector=p1v4)

    if LIMIT < 2:
        return _finishA(pb)
    def load_w(wdr):
        def f(e):
            for cb in range(NCB):
                pb.dma(e, wts[:, cb * D:(cb + 1) * D],
                       wdr[cb * 128:(cb + 1) * 128, :], "g")
        return f

    for side, wdr, dstn in (("q", WqT, qTn), ("k", WkT, kTn)):
        pb.step(gpsimd=load_w(wdr))
        for h in range(NCB):
            def mmf(e, h=h):
                for ch in range(4):
                    for r in range(NCB):
                        e.matmul(P[ch][:, :],
                                 wts[:, r * D + h * 128: r * D + (h + 1) * 128],
                                 x0T[:, r * T + ch * 512: r * T + (ch + 1) * 512],
                                 start=(r == 0), stop=(r == NCB - 1))
            pb.step(tensor=mmf)

            def evacf(e):
                for ch in range(4):
                    e.tensor_copy(scrA[:, ch * 512:(ch + 1) * 512], P[ch][:, :])
            pb.step(vector=evacf)

            def sqf(e):
                e.activation(scrB[:, :], scrA[:, :], AF.Square)
            pb.step(scalar=sqf)

            def nrm_mm(e):
                for ch in range(4):
                    e.matmul(P[ch][:, :], bdo_s[:, :], scrB[:, ch * 512:(ch + 1) * 512],
                             start=True, stop=True)
            pb.step(tensor=nrm_mm)

            def nrm_max(e):
                for ch in range(4):
                    e.tensor_scalar(scrB[:, ch * 512:(ch + 1) * 512], P[ch][:, :],
                                    1e-12, None, ALU.max)
            pb.step(vector=nrm_max)

            def nrm_inv(e):
                e.activation(scrC[:, :], scrB[:, :], AF.Ln)
                e.drain()
                e.activation(scrC[:, :], scrC[:, :], AF.Exp, scale=-0.5)
            pb.step(scalar=nrm_inv)

            if side == "q":
                def mulq(e, h=h):
                    e.scalar_tensor_tensor(qTn[:, h * T:(h + 1) * T], scrA[:, :],
                                           cs_cols[:, h:h + 1], scrC[:, :],
                                           ALU.mult, ALU.mult)
                pb.step(vector=mulq)
            else:
                def mulk(e, h=h):
                    e.scalar_tensor_tensor(kTn[:, h * T:(h + 1) * T], scrA[:, :],
                                           pm_c[:, 0:1], scrC[:, :],
                                           ALU.mult, ALU.mult)
                pb.step(vector=mulk)

    pb.step(gpsimd=load_w(WvT))
    for qp in range(NQT // 2):
        def vmm(e, qp=qp):
            for half in range(2):
                qt = qp * 2 + half
                for oc in range(2):
                    for r in range(NCB):
                        e.matmul(P[half * 2 + oc][:, :],
                                 x0T[:, r * T + qt * 128: r * T + (qt + 1) * 128],
                                 wts[:, r * D + oc * 512: r * D + (oc + 1) * 512],
                                 start=(r == 0), stop=(r == NCB - 1))
        pb.step(tensor=vmm)

        def vev(e, qp=qp):
            for half in range(2):
                qt = qp * 2 + half
                for oc in range(2):
                    e.tensor_copy(v_b[:, qt * D + oc * 512: qt * D + (oc + 1) * 512],
                                  P[half * 2 + oc][:, :])
        pb.step(vector=vev)

    pb.free("x0T")
    if LIMIT < 3:
        return _finishA(pb)

    for h in range(min(NCB, H_LIMIT)):
        def cph(e, h=h):
            e.tensor_copy(k2s[:, :], kTn[64:128, h * T:(h + 1) * T])
            e.drain()
            e.tensor_copy(q2s[:, :], qTn[64:128, h * T:(h + 1) * T])
        pb.step(vector=cph)
        for qc in range(4):
            for kt in range(NQT):
                k1 = kTn[0:64, h * T + kt * 128: h * T + (kt + 1) * 128]
                k2 = k2s[:, kt * 128:(kt + 1) * 128]
                q1 = qTn[0:64, h * T + qc * 512: h * T + (qc + 1) * 512]
                q2 = q2s[:, qc * 512:(qc + 1) * 512]

                def lgmm(e, k1=k1, k2=k2, q1=q1, q2=q2):
                    e.matmul(P[0][:, :], k1, q1, start=True, stop=False)
                    e.matmul(P[2][:, :], k1, q1, start=True, stop=False)
                    e.matmul(P[1][:, :], k2, q2, start=True, stop=False)
                    e.matmul(P[2][:, :], k2, q2, start=False, stop=True)
                pb.step(tensor=lgmm)

                def gtan(e, h=h):
                    e.activation(gstr[:, :], P[2][:, :], AF.Tanh,
                                 scale=dg_cols[:, h:h + 1])
                pb.step(scalar=gtan)

                def gadd(e):
                    e.matmul(P[0][:, :], ident_b[:, :], gstr[:, :], start=False, stop=True)
                    e.matmul(P[1][:, :], ident_b[:, :], gstr[:, :], start=False, stop=True)
                pb.step(tensor=gadd)

                def eexp(e):
                    e.activation(e1c[:, :], P[0][:, :], AF.Exp)
                    e.drain()
                    e.activation(e2c[:, :], P[1][:, :], AF.Exp, scale=-1.0)
                pb.step(scalar=eexp)

                vt = v_b[:, kt * D + h * 128: kt * D + (h + 1) * 128]

                def avmm(e, vt=vt, kt=kt):
                    st = (kt == 0); sp = (kt == NQT - 1)
                    e.matmul(P[3][:, :], vt, e1c[:, :], start=st, stop=sp)
                    e.matmul(P[4][:, :], vt, e2c[:, :], start=st, stop=sp)
                    e.matmul(P[5][:, :], ones_b[:, :], e1c[:, :], start=st, stop=sp)
                    e.matmul(P[6][:, :], ones_b[:, :], e2c[:, :], start=st, stop=sp)
                pb.step(tensor=avmm)

            def fin_v(e):
                e.reciprocal(scrB[:, 0:512], P[5][:, :])
                e.drain()
                e.reciprocal(scrB[:, 512:1024], P[6][:, :])
                e.drain()
                e.tensor_tensor(scrA[:, 0:512], P[3][:, :], scrB[:, 0:512], ALU.mult)
                e.drain()
                e.tensor_tensor(scrA[:, 512:1024], P[4][:, :], scrB[:, 512:1024], ALU.mult)
                e.drain()
                e.scalar_tensor_tensor(scrB[:, 1024:1536], scrA[:, 512:1024],
                                       lamn_col[:, 0:1], scrA[:, 0:512],
                                       ALU.mult, ALU.add)
            pb.step(vector=fin_v)

            def fin_sq(e):
                e.activation(scrC[:, 0:512], scrB[:, 1024:1536], AF.Square)
            pb.step(scalar=fin_sq)

            def fin_mm(e):
                e.matmul(P[0][:, :], ones_f[:, :], scrC[:, 0:512], start=True, stop=True)
            pb.step(tensor=fin_mm)

            def fin_ms(e):
                e.tensor_scalar(scrC[:, 512:1024], P[0][:, :], 1.0 / 128, 1e-5,
                                ALU.mult, ALU.add)
            pb.step(vector=fin_ms)

            def fin_ln(e):
                e.activation(scrC[:, 1024:1536], scrC[:, 512:1024], AF.Ln)
                e.drain()
                e.activation(scrC[:, 512:1024], scrC[:, 1024:1536], AF.Exp, scale=-0.5)
            pb.step(scalar=fin_ln)

            def fin_out(e, h=h, qc=qc):
                e.tensor_tensor(scrA[:, 0:512], scrB[:, 1024:1536], scrC[:, 512:1024],
                                ALU.mult)
                e.drain()
                e.tensor_scalar(qTn[:, h * T + qc * 512: h * T + (qc + 1) * 512],
                                scrA[:, 0:512], subw_c[:, 0:1], None, ALU.mult)
            pb.step(vector=fin_out)

    ysubT = qTn
    if LIMIT < 4:
        return _finishA(pb)

    pb.step(gpsimd=load_w(WoT))
    x1T = kTn
    for qt in range(NQT):
        def ldx(e, qt=qt):
            pb.dma(e, xt_l[:, :], x0[qt * 128:(qt + 1) * 128, :], "h")
        pb.step(sync=ldx)

        def womm2(e, qt=qt):
            for oc in range(2):
                for r in range(NCB):
                    e.matmul(P[oc][:, :],
                             ysubT[:, r * T + qt * 128: r * T + (qt + 1) * 128],
                             wts[:, r * D + oc * 512: r * D + (oc + 1) * 512],
                             start=(r == 0), stop=False)
                e.matmul(P[oc][:, :], ones_b[0:1, 0:128],
                         bo_s[0:1, oc * 512:(oc + 1) * 512], start=False, stop=True)
        pb.step(tensor=womm2)

        def ev(e):
            e.scalar_tensor_tensor(scrA[:, 0:512], P[0][:, :], 1.0, xt_l[:, 0:512],
                                   ALU.mult, ALU.add, accum_out=acol[:, 0:1])
            e.drain()
            e.scalar_tensor_tensor(scrA[:, 512:1024], P[1][:, :], 1.0, xt_l[:, 512:1024],
                                   ALU.mult, ALU.add, accum_out=acol[:, 1:2])
            e.drain()
            e.tensor_tensor(acol[:, 2:3], acol[:, 0:1], acol[:, 1:2], ALU.add)
        pb.step(vector=ev)

        def sq(e):
            e.activation(scrB[:, 0:1024], scrA[:, 0:1024], AF.Square,
                         accum_out=acol[:, 3:4])
        pb.step(scalar=sq)

        def stats(e):
            e.tensor_scalar(acol[:, 4:5], acol[:, 2:3], 1.0 / D, None, ALU.mult)
            e.drain()
            e.tensor_tensor(acol[:, 5:6], acol[:, 4:5], acol[:, 4:5], ALU.mult)
            e.drain()
            e.scalar_tensor_tensor(acol[:, 6:7], acol[:, 3:4], 1.0 / D, acol[:, 5:6],
                                   ALU.mult, ALU.subtract)
            e.drain()
            e.tensor_scalar(acol[:, 6:7], acol[:, 6:7], 1e-5, None, ALU.add)
        pb.step(vector=stats)

        def rstd(e):
            e.activation(acol[:, 7:8], acol[:, 6:7], AF.Ln)
            e.drain()
            e.activation(acol[:, 8:9], acol[:, 7:8], AF.Exp, scale=-0.5)
        pb.step(scalar=rstd)

        def nmr(e):
            e.scalar_tensor_tensor(acol[:, 9:10], acol[:, 4:5], -1.0, acol[:, 8:9],
                                   ALU.mult, ALU.mult)
        pb.step(vector=nmr)

        def norm(e):
            e.activation(xt_l[:, 0:1024], scrA[:, 0:1024], AF.Identity,
                         bias=acol[:, 9:10], scale=acol[:, 8:9])
            e.drain()
            e.activation(xbf[:, 0:1024], scrA[:, 0:1024], AF.Identity,
                         bias=acol[:, 9:10], scale=acol[:, 8:9])
        pb.step(scalar=norm)

        def spill(e, qt=qt):
            pb.dma(e, x1sp[qt * 128:(qt + 1) * 128, :], xt_l[:, 0:1024], "h")
        pb.step(sync=spill)

        def tr(e):
            for cb in range(NCB):
                e.transpose(PT[:, cb * 128:(cb + 1) * 128],
                            xbf[:, cb * 128:(cb + 1) * 128], ident_b[:, :])
        pb.step(tensor=tr)

        def trev(e, qt=qt):
            for cb in range(NCB):
                e.tensor_copy(x1T[:, cb * T + qt * 128: cb * T + (qt + 1) * 128],
                              PT[:, cb * 128:(cb + 1) * 128])
                e.drain()
        pb.step(vector=trev)

    if LIMIT < 5:
        return _finishA(pb)
    pb.free("wts")
    pb.free("qTn")
    bdm_s = pb.sbm("bdm_s", 128, 32 * 128, BF16, side="right")
    uts_s = pb.sbm("uts_s", 128, 16 * 128, BF16, side="right")
    yts = pb.sbm("yts", 128, 512, BF16, side="right")
    wsup = pb.sbm("wsup", 128, 16 * D, BF16, side="right")
    of_s = pb.sbm("of_s", 128, 4 * D, F32, side="right")

    def ldbdm(e):
        for i in range(32):
            pb.dma(e, bdm_s[:, i * 128:(i + 1) * 128], bdm[i, :, :], "g")
    pb.step(gpsimd=ldbdm)

    hts = v_b
    for hf in range(4):
        tof = hf * 512
        for s in range(8):
            def ldu(e, s=s):
                for m in range(16):
                    pb.dma(e, uts_s[:, m * 128:(m + 1) * 128], uts[s * 16 + m, :, :], "g")
            pb.step(gpsimd=ldu)

            hh = s // 2
            for gl in range(4):
                g8 = (s % 2) * 4 + gl
                idx = hh * 8 + g8

                def ymm(e, idx=idx, g8=g8, tof=tof):
                    e.matmul(P[0][:, :], bdm_s[:, idx * 128:(idx + 1) * 128],
                             x1T[:, g8 * T + tof: g8 * T + tof + 512],
                             start=True, stop=True)
                pb.step(tensor=ymm)

                def yev(e):
                    e.tensor_copy(yts[:, 0:512], P[0][:, :])
                pb.step(vector=yev)

                def zmm(e, gl=gl):
                    for mg in range(4):
                        m = gl * 4 + mg
                        e.matmul(P[2 + mg][:, :], uts_s[:, m * 128:(m + 1) * 128],
                                 yts[:, 0:512], start=True, stop=True)
                pb.step(tensor=zmm)

                def gel(e, gl=gl):
                    for mg in range(4):
                        m = gl * 4 + mg
                        e.activation(hts[:, m * 512: (m + 1) * 512],
                                     P[2 + mg][:, :], AF.Gelu)
                pb.step(scalar=gel)

            def ldw0(e, s=s):
                for fb in range(16):
                    pb.dma(e, wsup[:, fb * D:(fb + 1) * D],
                           wcT[(s * 16 + fb) * 128:(s * 16 + fb + 1) * 128, :], "g")
            pb.step(gpsimd=ldw0)

            for q8 in range(4):
                qt = hf * 4 + q8

                def cpmm(e, q8=q8, s=s):
                    for fb in range(16):
                        for oc in range(2):
                            e.matmul(P[4 + oc][:, :],
                                     hts[:, fb * 512 + q8 * 128:fb * 512 + (q8 + 1) * 128],
                                     wsup[:, fb * D + oc * 512: fb * D + (oc + 1) * 512],
                                     start=(fb == 0), stop=(fb == 15))
                pb.step(tensor=cpmm)
                def cpadd(e, q8=q8, s=s):
                    for oc in range(2):
                        if s == 0:
                            e.tensor_copy(of_s[:, q8 * D + oc * 512: q8 * D + (oc + 1) * 512],
                                          P[4 + oc][:, :])
                        else:
                            e.tensor_tensor(of_s[:, q8 * D + oc * 512: q8 * D + (oc + 1) * 512],
                                            P[4 + oc][:, :],
                                            of_s[:, q8 * D + oc * 512: q8 * D + (oc + 1) * 512],
                                            ALU.add)
                        e.drain()
                pb.step(vector=cpadd)

        for q8 in range(4):
            qt = hf * 4 + q8

            def ldx1(e, qt=qt):
                pb.dma(e, xt_l[:, :], x1sp[qt * 128:(qt + 1) * 128, :], "h")
            pb.step(sync=ldx1)

            def ev2(e, q8=q8):
                e.scalar_tensor_tensor(scrA[:, 0:512], of_s[:, q8 * D:q8 * D + 512],
                                       1.0, xt_l[:, 0:512], ALU.mult, ALU.add,
                                       accum_out=acol[:, 0:1])
                e.drain()
                e.scalar_tensor_tensor(scrA[:, 512:1024], of_s[:, q8 * D + 512:(q8 + 1) * D],
                                       1.0, xt_l[:, 512:1024], ALU.mult, ALU.add,
                                       accum_out=acol[:, 1:2])
                e.drain()
                e.tensor_tensor(acol[:, 2:3], acol[:, 0:1], acol[:, 1:2], ALU.add)
            pb.step(vector=ev2)

            def sq2(e):
                e.activation(scrB[:, 0:1024], scrA[:, 0:1024], AF.Square,
                             accum_out=acol[:, 3:4])
            pb.step(scalar=sq2)

            def stats2(e):
                e.tensor_scalar(acol[:, 4:5], acol[:, 2:3], 1.0 / D, None, ALU.mult)
                e.drain()
                e.tensor_tensor(acol[:, 5:6], acol[:, 4:5], acol[:, 4:5], ALU.mult)
                e.drain()
                e.scalar_tensor_tensor(acol[:, 6:7], acol[:, 3:4], 1.0 / D, acol[:, 5:6],
                                       ALU.mult, ALU.subtract)
                e.drain()
                e.tensor_scalar(acol[:, 6:7], acol[:, 6:7], 1e-5, None, ALU.add)
            pb.step(vector=stats2)

            def rstd2(e):
                e.activation(acol[:, 7:8], acol[:, 6:7], AF.Ln)
                e.drain()
                e.activation(acol[:, 8:9], acol[:, 7:8], AF.Exp, scale=-0.5)
            pb.step(scalar=rstd2)

            def nmr2(e):
                e.scalar_tensor_tensor(acol[:, 9:10], acol[:, 4:5], -1.0, acol[:, 8:9],
                                       ALU.mult, ALU.mult)
            pb.step(vector=nmr2)

            def norm2(e):
                e.activation(xt_l[:, 0:1024], scrA[:, 0:1024], AF.Identity,
                             bias=acol[:, 9:10], scale=acol[:, 8:9])
            pb.step(scalar=norm2)

            def st2(e, qt=qt):
                pb.dma(e, xout[qt * 128:(qt + 1) * 128, :], xt_l[:, 0:1024], "h")
            pb.step(sync=st2)

    return _finishA(pb)


def _finishA(pb):
    hf_target = pb.c["h"]
    gf_target = pb.c["g"]

    def fence(e):
        if hf_target:
            e.wait_ge(pb._sems["h"], hf_target)
        if gf_target:
            e.wait_ge(pb._sems["g"], gf_target)
    pb.block.sync(fence)
    return pb.done()



FF = 4096
NE = 8
NH = 16

LIMITB = 99


def build_launchB():
    pb = PB()
    nc = pb.nc

    def din(name, shape):
        return nc.dram_tensor(name, list(shape), F32, kind="ExternalInput").ap()

    x2 = din("x2", (T, D))
    x2rT = din("x2rT", (NCORE, D, T))
    wcvT = din("wcvT", (8, 8))
    ldt_bc = din("ldt_bc", (128, 128))
    iwr_a = din("iwr_a", (128, 128)); wim_a = din("wim_a", (128, 128))
    Br_a = din("Br_a", (128, 128)); Bi_a = din("Bi_a", (128, 128))
    Pr_a = din("Pr_a", (128, 128)); Pi_a = din("Pi_a", (128, 128))
    Cr_a = din("Cr_a", (128, 128)); Ci_a = din("Ci_a", (128, 128))
    Dd_c = din("Dd_c", (128, 8))
    peT = din("peT", (D, T))
    Wqkv = din("Wqkv", (D, 3 * D))
    Wo1 = din("Wo1", (D, D))
    Wr = din("Wr", (D, D))
    ubT = din("ubT", (64, 16)); vbT = din("vbT", (64, 16))
    ln1g = din("ln1g", (1, D)); ln1b = din("ln1b", (1, D))
    ln2g = din("ln2g", (1, D)); ln2b = din("ln2b", (1, D))
    ln3g = din("ln3g", (1, D)); ln3b = din("ln3b", (1, D))
    Wff1 = din("Wff1", (D, FF)); bff1c = din("bff1c", (FF, 1))
    Wff2 = din("Wff2", (FF, D)); bff2r = din("bff2r", (1, D))
    Wg = din("Wg", (D, NE))
    We1 = din("We1", (NE, D, FF)); be1c = din("be1c", (NE * FF, 1))
    We2 = din("We2", (NE, FF, D)); be2m = din("be2m", (NE, D))
    Wout = din("Wout", (D, D))
    xo = nc.dram_tensor("xo", [T, D], F32, kind="ExternalOutput").ap()
    x3sp = nc.dram_tensor("x3sp", [T, D], F32).ap()
    h1sp = nc.dram_tensor("h1sp", [T, D], F32).ap()
    h2sp = nc.dram_tensor("h2sp", [T, D], F32).ap()
    rTsp = nc.dram_tensor("rTsp", [D, T], F32).ap()
    kTsp = nc.dram_tensor("kTsp", [D, T], BF16).ap()
    sc0 = nc.dram_tensor("sc0", [T * 2049 + 128], BF16).ap()
    sc1 = nc.dram_tensor("sc1", [T * 2049 + 128], BF16).ap()

    ident_b = pb.sb("ident_b", 128, 128, BF16)
    ident_f = pb.sb("ident_f", 128, 128, F32)
    ones_b = pb.sb("ones_b", 128, 128, BF16)
    ones_f = pb.sb("ones_f", 128, 128, F32)
    scrA = pb.sb("scrA", 128, T, F32)
    xt_l = pb.sb("xt_l", 128, D, F32)
    xbf = pb.sb("xbf", 128, D, BF16)
    acol = pb.sb("acol", 128, 16, F32)
    rowb = pb.sb("rowb", 1, 256, F32)
    ubc = pb.sb("ubc", 64, 16, F32)
    vbc = pb.sb("vbc", 64, 16, F32)
    mskT = pb.sb("mskT", 8, T, BF16)
    msks = pb.sb("msks", 128, 8 * NQT, F32)
    gate_c = pb.sb("gate_c", 128, NQT, F32)
    wgsb = pb.sb("wgsb", 128, 8 * 8, BF16)
    be2s = pb.sb("be2s", 8, D, BF16)
    wcvs = pb.sb("wcvs", 8, 8, F32)
    ksel = pb.sb("ksel", 128, 64, F32)
    kblk = pb.sb("kblk", 128, 80, F32)
    ebuf = pb.sb("ebuf", 128, T, BF16)
    bdr = pb.sb("bdr", 128, T, BF16)
    bdsh = pb.sb("bdsh", 128, T, BF16)
    etil = pb.sb("etil", 128, 1024, BF16)
    dmsk = pb.sb("dmsk", 128, 128, F32)
    bcol = pb.sb("bcol", 128, 1, F32)
    pebuf = pb.sb("pebuf", 128, 1024, BF16)

    x3T = pb.sbm("x3T", 128, NCB * T, BF16, side="right")
    accc = pb.sbm("accc", 128, NCB * T, F32, side="right")
    xjt = pb.sbm("xjt", 128, 2 * T, F32, side="right")
    s4a = pb.sbm("s4a", 128, 128, F32, side="right")
    s4b = pb.sbm("s4b", 128, 128, F32, side="right")
    s4c = pb.sbm("s4c", 128, 128, F32, side="right")
    s4d = pb.sbm("s4d", 128, 128, F32, side="right")
    s4e = pb.sbm("s4e", 128, 128, F32, side="right")
    s4f = pb.sbm("s4f", 128, 320, F32, side="right")

    P = [pb.ps(f"P{i}", 512) for i in range(7)]
    PT = pb.ps("PT", 1024, BF16)

    S = pb._sems
    pb.begin()

    def b0g(e):
        e.memset(ones_b[:, :], 1.0)
        e.memset(ones_f[:, :], 1.0)
        e.drain()
        e.affine_select(ident_b[:, :], ones_b[:, :], [[-1, 128]], ALU.is_equal, 0.0,
                        channel_multiplier=1)
        e.drain()
        e.affine_select(ident_f[:, :], ones_f[:, :], [[-1, 128]], ALU.is_equal, 0.0,
                        channel_multiplier=1)
        e.drain()
        e.memset(dmsk[:, :], 0.0)
        e.drain()
        e.affine_select(dmsk[:, :], dmsk[:, :], [[-1, 128]], ALU.is_ge, -1e30,
                        channel_multiplier=1)

    def b0s(e):
        pb.dma(e, wcvs[:, :], wcvT, "h")
        pb.dma(e, ubc[:, :], ubT, "h")
        pb.dma(e, vbc[:, :], vbT, "h")
        pb.dma(e, s4a[:, :], ldt_bc, "h")
        pb.dma(e, kblk[:, 0:8], Dd_c, "h")

    def b0g2(e):
        pb.dma(e, be2s[:, :], be2m, "g")
        for cb in range(NCB):
            pb.dma(e, wgsb[:, cb * 8:(cb + 1) * 8], Wg[cb * 128:(cb + 1) * 128, :], "g")

    pb.step(gpsimd=b0g, sync=b0s)
    pb.step(gpsimd=b0g2)

    def b0v(e):
        e.tensor_scalar(ubc[:, :], ubc[:, :], 0.125, None, ALU.mult)
        e.drain()
        e.tensor_scalar(vbc[:, :], vbc[:, :], 0.125, None, ALU.mult)
    pb.step(vector=b0v)

    Z, W2, IRW = _s4_consts()

    def b1s(e):
        e.activation(s4b[:, :], s4a[:, :], AF.Exp)
    pb.step(scalar=b1s)

    def b1sy(e):
        pb.dma(e, s4a[:, :], iwr_a, "h")
    pb.step(sync=b1sy)

    def b1s2(e):
        e.activation(s4c[:, :], s4a[:, :], AF.Exp)
    pb.step(scalar=b1s2)

    def b1sy2(e):
        pb.dma(e, s4a[:, :], wim_a, "h")
    pb.step(sync=b1sy2)

    def b1v(e):
        e.tensor_tensor(s4c[:, :], s4c[:, :], s4b[:, :], ALU.mult)
        e.drain()
        e.tensor_scalar(s4c[:, :], s4c[:, :], -1.0, None, ALU.mult)
        e.drain()
        e.tensor_tensor(s4d[:, :], s4a[:, :], s4b[:, :], ALU.mult)
    pb.step(vector=b1v)

    rcols = {}
    colidx = [0]

    def alloc_col():
        c = colidx[0]; colidx[0] += 1
        return c

    for ab, (Xr_d, Xi_d, Yr_d, Yi_d, conjY) in enumerate((
            (Br_a, Bi_a, Cr_a, Ci_a, False),
            (Br_a, Bi_a, Pr_a, Pi_a, True),
            (Pr_a, Pi_a, Cr_a, Ci_a, False),
            (Pr_a, Pi_a, Pr_a, Pi_a, True))):
        def ldv(e, Xr_d=Xr_d, Xi_d=Xi_d, Yr_d=Yr_d, Yi_d=Yi_d):
            pb.dma(e, s4a[:, :], Xr_d, "h")
            pb.dma(e, xjt[:, 1664:1792], Xi_d, "h")
            pb.dma(e, s4e[:, :], Yr_d, "h")
            pb.dma(e, xjt[:, 0:128], Yi_d, "h")
        pb.step(sync=ldv)

        sgn = -1.0 if conjY else 1.0

        def vab(e, sgn=sgn):
            e.tensor_tensor(xjt[:, 128:256], s4a[:, :], s4e[:, :], ALU.mult)
            e.drain()
            e.tensor_tensor(xjt[:, 256:384], xjt[:, 1664:1792], xjt[:, 0:128], ALU.mult)
            e.drain()
            e.tensor_scalar(xjt[:, 256:384], xjt[:, 256:384], sgn, None, ALU.mult)
            e.drain()
            e.tensor_tensor(xjt[:, 384:512], xjt[:, 128:256], xjt[:, 256:384],
                            ALU.subtract)
            e.drain()
            e.tensor_tensor(xjt[:, 512:640], s4a[:, :], xjt[:, 0:128], ALU.mult)
            e.drain()
            e.tensor_scalar(xjt[:, 512:640], xjt[:, 512:640], sgn, None, ALU.mult)
            e.drain()
            e.tensor_tensor(xjt[:, 640:768], xjt[:, 1664:1792], s4e[:, :], ALU.mult)
            e.drain()
            e.tensor_tensor(xjt[:, 512:640], xjt[:, 512:640], xjt[:, 640:768],
                            ALU.add)
        pb.step(vector=vab)

        for m in range(5):
            zr, zi = Z[m]
            cr_ = alloc_col(); ci_ = alloc_col()
            rcols[(ab, m)] = (cr_, ci_)

            def cch(e, zr=zr, zi=zi, cr_=cr_, ci_=ci_):
                e.tensor_scalar(xjt[:, 768:896], s4c[:, :], -1.0, zr, ALU.mult, ALU.add)
                e.drain()
                e.tensor_scalar(xjt[:, 896:1024], s4d[:, :], -1.0, zi, ALU.mult, ALU.add)
                e.drain()
                e.tensor_tensor(xjt[:, 1024:1152], xjt[:, 768:896], xjt[:, 768:896], ALU.mult)
                e.drain()
                e.tensor_tensor(xjt[:, 1152:1280], xjt[:, 896:1024], xjt[:, 896:1024], ALU.mult)
                e.drain()
                e.tensor_tensor(xjt[:, 1024:1152], xjt[:, 1024:1152], xjt[:, 1152:1280], ALU.add)
                e.drain()
                e.reciprocal(xjt[:, 1152:1280], xjt[:, 1024:1152])
                e.drain()
                e.tensor_tensor(xjt[:, 1280:1408], xjt[:, 384:512], xjt[:, 768:896], ALU.mult)
                e.drain()
                e.tensor_tensor(xjt[:, 1408:1536], xjt[:, 512:640], xjt[:, 896:1024], ALU.mult)
                e.drain()
                e.tensor_tensor(xjt[:, 1280:1408], xjt[:, 1280:1408], xjt[:, 1408:1536], ALU.add)
                e.drain()
                e.tensor_tensor(xjt[:, 1408:1536], xjt[:, 512:640], xjt[:, 768:896], ALU.mult)
                e.drain()
                e.tensor_tensor(xjt[:, 1536:1664], xjt[:, 384:512], xjt[:, 896:1024], ALU.mult)
                e.drain()
                e.tensor_tensor(xjt[:, 1408:1536], xjt[:, 1408:1536], xjt[:, 1536:1664], ALU.subtract)
                e.drain()
                e.tensor_tensor(xjt[:, 1280:1408], xjt[:, 1280:1408], xjt[:, 1152:1280], ALU.mult)
                e.drain()
                e.tensor_tensor(xjt[:, 1408:1536], xjt[:, 1408:1536], xjt[:, 1152:1280], ALU.mult)
                e.drain()
                e.reduce_sum(s4f[:, cr_ * 8:(cr_ + 1) * 8],
                             xjt[:, 1280:1408].rearrange("p (b n) -> p b n", n=16), AX.X)
                e.drain()
                e.reduce_sum(s4f[:, ci_ * 8:(ci_ + 1) * 8],
                             xjt[:, 1408:1536].rearrange("p (b n) -> p b n", n=16), AX.X)
            pb.step(vector=cch)

    def rdt(e):
        dtb = s4b[:, 0:128:16]
        for idx in range(40):
            e.tensor_tensor(s4f[:, idx * 8:(idx + 1) * 8],
                            s4f[:, idx * 8:(idx + 1) * 8], dtb, ALU.mult)
            e.drain()
    pb.step(vector=rdt)

    def kfm(e):
        e.memset(kblk[:, 8:72], 0.0)
        e.drain()
        for m in range(5):
            c00r, c00i = rcols[(0, m)]; c01r, c01i = rcols[(1, m)]
            c10r, c10i = rcols[(2, m)]; c11r, c11i = rcols[(3, m)]
            def col(c): return s4f[:, c * 8:(c + 1) * 8]
            e.tensor_scalar(s4a[:, 0:8], col(c11r), 1.0, None, ALU.add)
            e.drain()
            e.tensor_tensor(s4a[:, 8:16], col(c01r), col(c10r), ALU.mult)
            e.drain()
            e.tensor_tensor(s4a[:, 16:24], col(c01i), col(c10i), ALU.mult)
            e.drain()
            e.tensor_tensor(s4a[:, 8:16], s4a[:, 8:16], s4a[:, 16:24], ALU.subtract)
            e.drain()
            e.tensor_tensor(s4a[:, 24:32], col(c01r), col(c10i), ALU.mult)
            e.drain()
            e.tensor_tensor(s4a[:, 32:40], col(c01i), col(c10r), ALU.mult)
            e.drain()
            e.tensor_tensor(s4a[:, 24:32], s4a[:, 24:32], s4a[:, 32:40], ALU.add)
            e.drain()
            e.tensor_tensor(s4a[:, 40:48], s4a[:, 0:8], s4a[:, 0:8], ALU.mult)
            e.drain()
            e.tensor_tensor(s4a[:, 48:56], col(c11i), col(c11i), ALU.mult)
            e.drain()
            e.tensor_tensor(s4a[:, 40:48], s4a[:, 40:48], s4a[:, 48:56], ALU.add)
            e.drain()
            e.reciprocal(s4a[:, 48:56], s4a[:, 40:48])
            e.drain()
            e.tensor_tensor(s4a[:, 56:64], s4a[:, 8:16], s4a[:, 0:8], ALU.mult)
            e.drain()
            e.tensor_tensor(s4a[:, 64:72], s4a[:, 24:32], col(c11i), ALU.mult)
            e.drain()
            e.tensor_tensor(s4a[:, 56:64], s4a[:, 56:64], s4a[:, 64:72], ALU.add)
            e.drain()
            e.tensor_tensor(s4a[:, 56:64], s4a[:, 56:64], s4a[:, 48:56], ALU.mult)
            e.drain()
            e.tensor_tensor(s4a[:, 64:72], s4a[:, 24:32], s4a[:, 0:8], ALU.mult)
            e.drain()
            e.tensor_tensor(s4a[:, 72:80], s4a[:, 8:16], col(c11i), ALU.mult)
            e.drain()
            e.tensor_tensor(s4a[:, 64:72], s4a[:, 64:72], s4a[:, 72:80], ALU.subtract)
            e.drain()
            e.tensor_tensor(s4a[:, 64:72], s4a[:, 64:72], s4a[:, 48:56], ALU.mult)
            e.drain()
            e.tensor_tensor(s4a[:, 80:88], col(c00r), s4a[:, 56:64], ALU.subtract)
            e.drain()
            e.tensor_tensor(s4a[:, 88:96], col(c00i), s4a[:, 64:72], ALU.subtract)
            e.drain()
            w2r, w2i = W2[m]
            e.tensor_scalar(s4a[:, 96:104], s4a[:, 80:88], w2r, None, ALU.mult)
            e.drain()
            e.tensor_scalar(s4a[:, 104:112], s4a[:, 88:96], w2i, None, ALU.mult)
            e.drain()
            e.tensor_tensor(s4a[:, 96:104], s4a[:, 96:104], s4a[:, 104:112], ALU.subtract)
            e.drain()
            e.tensor_scalar(s4a[:, 104:112], s4a[:, 80:88], w2i, None, ALU.mult)
            e.drain()
            e.tensor_scalar(s4a[:, 112:120], s4a[:, 88:96], w2r, None, ALU.mult)
            e.drain()
            e.tensor_tensor(s4a[:, 104:112], s4a[:, 104:112], s4a[:, 112:120], ALU.add)
            e.drain()
            for l in range(8):
                wr_, wi_ = IRW[m][l]
                if wr_ != 0.0:
                    e.scalar_tensor_tensor(kblk[:, 8 + l * 8:16 + l * 8],
                                           s4a[:, 96:104], wr_, kblk[:, 8 + l * 8:16 + l * 8],
                                           ALU.mult, ALU.add)
                    e.drain()
                if wi_ != 0.0:
                    e.scalar_tensor_tensor(kblk[:, 8 + l * 8:16 + l * 8],
                                           s4a[:, 104:112], wi_, kblk[:, 8 + l * 8:16 + l * 8],
                                           ALU.mult, ALU.add)
                    e.drain()
    pb.step(vector=kfm)

    for cb in range(NCB):
        def ktr(e, cb=cb):
            for l in range(8):
                e.tensor_copy(s4a[:, 120 + l:121 + l], kblk[:, 8 + l * 8 + cb:9 + l * 8 + cb])
                e.drain()
            e.nop()
        pb.step(vector=ktr)

        def ktp(e):
            e.transpose(P[0][0:8, 0:128], s4a[:, 120:128], ident_f[:, :])
        pb.step(tensor=ktp)

        def kev(e):
            e.tensor_copy(s4b[0:8, 0:128], P[0][0:8, 0:128])
        pb.step(vector=kev)

        def ksl(e):
            e.matmul(P[1][:, 0:8], s4b[0:8, 0:128], wcvs[0:8, 0:8], start=True, stop=True)
        pb.step(tensor=ksl)

        def kse(e, cb=cb):
            e.tensor_copy(ksel[:, cb * 8:(cb + 1) * 8], P[1][:, 0:8])
        pb.step(vector=kse)

    for nm in ("s4f", "s4e", "s4d", "s4c", "s4b", "s4a"):
        pb.free(nm)

    if LIMITB < 2:
        return _finishB(pb)

    for cb in range(NCB):
        def cinit(e, cb=cb):
            pb.dma(e, xjt[:, 0:T], x2rT[0, cb * 128:(cb + 1) * 128, :], "h")
        pb.step(sync=cinit)

        def cacc0(e, cb=cb):
            e.tensor_scalar(accc[:, cb * T:(cb + 1) * T], xjt[:, 0:T],
                            kblk[:, cb:cb + 1], None, ALU.mult)
            e.drain()
            e.scalar_tensor_tensor(accc[:, cb * T:(cb + 1) * T], xjt[:, 0:T],
                                   ksel[:, cb * 8:cb * 8 + 1],
                                   accc[:, cb * T:(cb + 1) * T], ALU.mult, ALU.add)
        pb.step(vector=cacc0)

        for j in range(1, 8):
            par = j % 2

            def cld(e, j=j, cb=cb, par=par):
                pb.dma(e, xjt[:, par * T:(par + 1) * T],
                       x2rT[j, cb * 128:(cb + 1) * 128, :], "h")
            pb.step(sync=cld)

            def cac(e, j=j, cb=cb, par=par):
                e.scalar_tensor_tensor(accc[:, cb * T:(cb + 1) * T],
                                       xjt[:, par * T:(par + 1) * T],
                                       ksel[:, cb * 8 + j:cb * 8 + j + 1],
                                       accc[:, cb * T:(cb + 1) * T], ALU.mult, ALU.add)
            pb.step(vector=cac)

    for cb in range(NCB):
        def sil(e, cb=cb):
            e.activation(accc[:, cb * T:(cb + 1) * T], accc[:, cb * T:(cb + 1) * T],
                         AF.Silu)
        pb.step(scalar=sil)

    for qt in range(NQT):
        def ldx2(e, qt=qt):
            pb.dma(e, xt_l[:, :], x2[qt * 128:(qt + 1) * 128, :], "h")
        pb.step(sync=ldx2)

        def tr3a(e, qt=qt):
            for cb in range(4):
                e.transpose(P[4 + cb // 2][:, (cb % 2) * 128:(cb % 2) * 128 + 128],
                            accc[:, cb * T + qt * 128: cb * T + (qt + 1) * 128],
                            ident_f[:, :])
        pb.step(tensor=tr3a)

        def ev3a(e, qt=qt):
            for cb in range(4):
                e.scalar_tensor_tensor(scrA[:, cb * 128:(cb + 1) * 128],
                                       P[4 + cb // 2][:, (cb % 2) * 128:(cb % 2) * 128 + 128],
                                       1.0, xt_l[:, cb * 128:(cb + 1) * 128],
                                       ALU.mult, ALU.add, accum_out=acol[:, cb:cb + 1])
                e.drain()
        pb.step(vector=ev3a)

        def tr3b(e, qt=qt):
            for cb in range(4, 8):
                e.transpose(P[4 + (cb - 4) // 2][:, ((cb - 4) % 2) * 128:((cb - 4) % 2) * 128 + 128],
                            accc[:, cb * T + qt * 128: cb * T + (qt + 1) * 128],
                            ident_f[:, :])
        pb.step(tensor=tr3b)

        def ev3b(e, qt=qt):
            for cb in range(4, 8):
                e.scalar_tensor_tensor(scrA[:, cb * 128:(cb + 1) * 128],
                                       P[4 + (cb - 4) // 2][:, ((cb - 4) % 2) * 128:((cb - 4) % 2) * 128 + 128],
                                       1.0, xt_l[:, cb * 128:(cb + 1) * 128],
                                       ALU.mult, ALU.add, accum_out=acol[:, cb:cb + 1])
                e.drain()
            e.reduce_sum(acol[:, 8:9], acol[:, 0:8], AX.X)
        pb.step(vector=ev3b)

        _ln_emit(pb, scrA, acol, xt_l, xbf, None, None)

        def sp3(e, qt=qt):
            pb.dma(e, x3sp[qt * 128:(qt + 1) * 128, :], xt_l[:, 0:D], "h")
        pb.step(sync=sp3)

        def t3T(e):
            for cb in range(NCB):
                e.transpose(PT[:, cb * 128:(cb + 1) * 128],
                            xbf[:, cb * 128:(cb + 1) * 128], ident_b[:, :])
        pb.step(tensor=t3T)

        def t3Te(e, qt=qt):
            for cb in range(NCB):
                e.tensor_copy(x3T[:, cb * T + qt * 128: cb * T + (qt + 1) * 128],
                              PT[:, cb * 128:(cb + 1) * 128])
        pb.step(vector=t3Te)

    pb.free("xjt")
    pb.free("accc")

    if LIMITB < 3:
        return _finishB(pb)

    qT = pb.sbm("qT", 128, NCB * T, BF16, side="right")
    v_b = pb.sbm("v_b", 128, NQT * D, BF16, side="right")
    wts = pb.sbm("wts", 128, NCB * D, BF16, side="right")

    def ldw_sec(off):
        def f(e):
            for cb in range(NCB):
                pb.dma(e, wts[:, cb * D:(cb + 1) * D],
                       Wqkv[cb * 128:(cb + 1) * 128, off:off + D], "g")
        return f

    pb.step(gpsimd=ldw_sec(0))
    for dblk in range(NCB):
        def qmm(e, dblk=dblk):
            for ch in range(4):
                for r in range(NCB):
                    e.matmul(P[ch][:, :],
                             wts[:, r * D + dblk * 128: r * D + (dblk + 1) * 128],
                             x3T[:, r * T + ch * 512: r * T + (ch + 1) * 512],
                             start=(r == 0), stop=(r == NCB - 1))
        pb.step(tensor=qmm)

        def qev(e, dblk=dblk):
            for ch in range(4):
                e.tensor_scalar(qT[:, dblk * T + ch * 512: dblk * T + (ch + 1) * 512],
                                P[ch][:, :], 0.125, None, ALU.mult)
        pb.step(vector=qev)

    pb.step(gpsimd=ldw_sec(D))
    for dblk in range(NCB):
        def kmm(e, dblk=dblk):
            for ch in range(4):
                for r in range(NCB):
                    e.matmul(P[ch][:, :],
                             wts[:, r * D + dblk * 128: r * D + (dblk + 1) * 128],
                             x3T[:, r * T + ch * 512: r * T + (ch + 1) * 512],
                             start=(r == 0), stop=(r == NCB - 1))
        pb.step(tensor=kmm)

        def kev(e, dblk=dblk):
            for ch in range(4):
                e.tensor_copy(ebuf[:, ch * 512:(ch + 1) * 512], P[ch][:, :])
                e.drain()
        pb.step(vector=kev)

        def ksp(e, dblk=dblk):
            pb.dma(e, kTsp[dblk * 128:(dblk + 1) * 128, :], ebuf[:, 0:T], "h")
        pb.step(sync=ksp)

    pb.step(gpsimd=ldw_sec(2 * D))
    for qp in range(NQT // 2):
        def vmm(e, qp=qp):
            for half in range(2):
                qt = qp * 2 + half
                for oc in range(2):
                    for r in range(NCB):
                        e.matmul(P[half * 2 + oc][:, :],
                                 x3T[:, r * T + qt * 128: r * T + (qt + 1) * 128],
                                 wts[:, r * D + oc * 512: r * D + (oc + 1) * 512],
                                 start=(r == 0), stop=(r == NCB - 1))
        pb.step(tensor=vmm)

        def vev(e, qp=qp):
            for half in range(2):
                qt = qp * 2 + half
                for oc in range(2):
                    e.tensor_copy(v_b[:, qt * D + oc * 512: qt * D + (oc + 1) * 512],
                                  P[half * 2 + oc][:, :])
        pb.step(vector=vev)

    def ldwr(e):
        for cb in range(NCB):
            pb.dma(e, wts[:, cb * D:(cb + 1) * D], Wr[cb * 128:(cb + 1) * 128, :], "g")
    pb.step(gpsimd=ldwr)

    for ch in range(4):
        def peload2(e, ch=ch):
            for cb in range(4):
                pb.dma(e, ebuf[:, cb * 512:(cb + 1) * 512],
                       peT[cb * 128:(cb + 1) * 128, ch * 512:(ch + 1) * 512], "g")
            for cb in range(4, 8):
                pb.dma(e, bdr[:, (cb - 4) * 512:(cb - 3) * 512],
                       peT[cb * 128:(cb + 1) * 128, ch * 512:(ch + 1) * 512], "g")
        pb.step(gpsimd=peload2)

        def rmm(e, ch=ch):
            for dblk in range(4):
                for r in range(NCB):
                    mv = ebuf[:, r * 512:(r + 1) * 512] if r < 4 else bdr[:, (r - 4) * 512:(r - 3) * 512]
                    e.matmul(P[dblk][:, :],
                             wts[:, r * D + dblk * 128: r * D + (dblk + 1) * 128],
                             mv, start=(r == 0), stop=(r == NCB - 1))
        pb.step(tensor=rmm)

        def rev(e, ch=ch):
            for dblk in range(4):
                e.tensor_copy(scrA[:, dblk * 512:(dblk + 1) * 512], P[dblk][:, :])
                e.drain()
        pb.step(vector=rev)

        def rst(e, ch=ch):
            for dblk in range(4):
                pb.dma(e, rTsp[dblk * 128:(dblk + 1) * 128, ch * 512:(ch + 1) * 512],
                       scrA[:, dblk * 512:(dblk + 1) * 512], "h")
        pb.step(sync=rst)

        def rmm2(e, ch=ch):
            for dblk in range(4, 8):
                for r in range(NCB):
                    mv = ebuf[:, r * 512:(r + 1) * 512] if r < 4 else bdr[:, (r - 4) * 512:(r - 3) * 512]
                    e.matmul(P[dblk - 4][:, :],
                             wts[:, r * D + dblk * 128: r * D + (dblk + 1) * 128],
                             mv, start=(r == 0), stop=(r == NCB - 1))
        pb.step(tensor=rmm2)

        def rev2(e, ch=ch):
            for dblk in range(4, 8):
                e.tensor_copy(scrA[:, (dblk - 4) * 512:(dblk - 3) * 512], P[dblk - 4][:, :])
                e.drain()
        pb.step(vector=rev2)

        def rst2(e, ch=ch):
            for dblk in range(4, 8):
                pb.dma(e, rTsp[dblk * 128:(dblk + 1) * 128, ch * 512:(ch + 1) * 512],
                       scrA[:, (dblk - 4) * 512:(dblk - 3) * 512], "h")
        pb.step(sync=rst2)

    pb.free("wts")

    if LIMITB < 4:
        return _finishB(pb)

    def zf(e):
        e.memset(ebuf[:, :], 0.0)
        e.drain()
        for i in range(17):
            n = min(T * 2049 + 128 - i * 256 * 1024, 256 * 1024)
            if n <= 0:
                break
            pb.dma(e, sc0[i * 256 * 1024: i * 256 * 1024 + n].rearrange("(p f) -> p f", p=128),
                   ebuf[:, 0:n // 128], "g")
            pb.dma(e, sc1[i * 256 * 1024: i * 256 * 1024 + n].rearrange("(p f) -> p f", p=128),
                   ebuf[:, 0:n // 128], "g")
    pb.step(gpsimd=zf)

    att = pb.sbm("att", 64, 3 * T, BF16, side="right")
    rbuf = pb.sbm("rbuf", 64, 2 * T, BF16, side="right")

    scs = (sc0, sc1)
    for h in range(NH):
        dblk = h // 2
        rb = (h % 2) * 64

        def cph(e, h=h):
            pb.dma(e, att[:, 0:T], kTsp[h * 64:(h + 1) * 64, :], "h")
        pb.step(sync=cph)

        def cpq(e, h=h, dblk=dblk, rb=rb):
            e.tensor_scalar(att[:, T:2 * T], qT[rb:rb + 64, dblk * T:(dblk + 1) * T],
                            ubc[:, h:h + 1], None, ALU.add)
            e.tensor_scalar(att[:, 2 * T:3 * T], qT[rb:rb + 64, dblk * T:(dblk + 1) * T],
                            vbc[:, h:h + 1], None, ALU.add)
        pb.step(vector=cpq)

        sc = scs[h % 2]
        for a in range(NQT):
            cw = 128 * (a + 1)
            m_lo = T - cw
            nch = (cw + 511) // 512

            def rld(e, h=h, a=a, m_lo=m_lo, cw=cw):
                par = (a % 2) * T
                pb.dma(e, rbuf[:, par:par + cw],
                       rTsp[h * 64:(h + 1) * 64, m_lo:T], "g")
            pb.step(gpsimd=rld)

            def bmm(e, h=h, a=a, cw=cw, nch=nch):
                par = (a % 2) * T
                for bc in range(nch):
                    w = min(512, cw - bc * 512)
                    e.matmul(P[bc][:, 0:w],
                             att[:, 2 * T + a * 128: 2 * T + (a + 1) * 128],
                             rbuf[:, par + bc * 512: par + bc * 512 + w],
                             start=True, stop=True)
            pb.step(tensor=bmm)

            def bev(e, cw=cw, nch=nch):
                for bc in range(nch):
                    w = min(512, cw - bc * 512)
                    e.tensor_copy(bdr[:, bc * 512: bc * 512 + w], P[bc][:, 0:w])
            pb.step(vector=bev)

            def shw(e, h=h, a=a, m_lo=m_lo, cw=cw, sc=sc):
                pad = sc[0: T * 2049].rearrange("(i m) -> i m", m=2049)
                pb.dma(e, pad[a * 128:(a + 1) * 128, m_lo:T], bdr[:, 0:cw], "h")
            pb.step(sync=shw)

            def shr(e, h=h, a=a, cw=cw, sc=sc):
                shv = sc[2047: 2047 + T * T].rearrange("(i j) -> i j", j=T)
                pb.dma(e, bdsh[:, 0:cw], shv[a * 128:(a + 1) * 128, 0:cw], "h")
            pb.step(sync=shr)

            def amm(e, h=h, a=a, cw=cw, nch=nch):
                for bc in range(nch):
                    w = min(512, cw - bc * 512)
                    e.matmul(P[bc][:, 0:w],
                             att[:, T + a * 128: T + (a + 1) * 128],
                             att[:, bc * 512: bc * 512 + w],
                             start=True, stop=True)
            pb.step(tensor=amm)

            def sadd(e, cw=cw, nch=nch):
                for bc in range(nch):
                    w = min(512, cw - bc * 512)
                    e.tensor_tensor(scrA[:, bc * 512: bc * 512 + w], P[bc][:, 0:w],
                                    bdsh[:, bc * 512: bc * 512 + w], ALU.add)
            pb.step(vector=sadd)

            def dmask(e, a=a, cw=cw):
                e.tensor_tensor(scrA[:, cw - 128: cw], scrA[:, cw - 128: cw],
                                dmsk[:, :], ALU.add)
            pb.step(vector=dmask)

            def eexp(e, cw=cw):
                e.activation(ebuf[:, 0:cw], scrA[:, 0:cw], AF.Exp,
                             accum_out=acol[:, 0:1])
            pb.step(scalar=eexp)

            def esc(e, cw=cw):
                e.reciprocal(acol[:, 1:2], acol[:, 0:1])
                e.drain()
                e.tensor_scalar(ebuf[:, 0:cw], ebuf[:, 0:cw], acol[:, 1:2], None,
                                ALU.mult)
            pb.step(vector=esc)

            for kt0 in range(0, a + 1, 8):
                nb = min(8, a + 1 - kt0)

                def etp(e, kt0=kt0, nb=nb):
                    for i in range(nb):
                        e.transpose(PT[:, i * 128:(i + 1) * 128],
                                    ebuf[:, (kt0 + i) * 128:(kt0 + i + 1) * 128],
                                    ident_b[:, :])
                pb.step(tensor=etp)

                def ete(e, nb=nb):
                    e.tensor_copy(etil[:, 0:nb * 128], PT[:, 0:nb * 128])
                pb.step(vector=ete)

                def avm(e, kt0=kt0, nb=nb, h=h, a=a):
                    for i in range(nb):
                        kt = kt0 + i
                        e.matmul(P[6][0:64, 0:128],
                                 v_b[:, kt * D + h * 64: kt * D + (h + 1) * 64],
                                 etil[:, i * 128:(i + 1) * 128],
                                 start=(kt == 0), stop=(kt == a))
                pb.step(tensor=avm)

            def ctxe(e, h=h, a=a, dblk=dblk, rb=rb):
                e.tensor_copy(x3T[rb:rb + 64, dblk * T + a * 128: dblk * T + (a + 1) * 128],
                              P[6][0:64, 0:128])
            pb.step(vector=ctxe)

    ctxT = x3T

    if LIMITB < 5:
        return _finishB(pb)

    pb.free("rbuf")
    pb.free("att")
    pb.free("v_b")
    pb.free("qT")
    wts2 = pb.sbm("wts2", 128, NCB * D, BF16, side="right")
    hT = pb.sbm("hT", 128, NCB * T, BF16, side="right")
    accF = pb.sbm("accF", 128, NQT * D, F32, side="right")
    lnrow = pb.sbm("lnrow", 1, 2 * D, F32, side="right")
    gbG = pb.sbm("gbG", 128, D, F32, side="right")
    gbB = pb.sbm("gbB", 128, D, F32, side="right")

    def ldlp(e, gr, br):
        pb.dma(e, lnrow[0:1, 0:D], gr, "g")
        pb.dma(e, lnrow[0:1, D:2 * D], br, "g")

    def gb_build(tag):
        def t(e):
            for oc in range(2):
                e.matmul(P[0][:, :], ones_f[0:1, 0:128], lnrow[0:1, oc * 512:(oc + 1) * 512],
                         start=True, stop=True) if oc == 0 else None
            e.matmul(P[0][:, :], ones_f[0:1, 0:128], lnrow[0:1, 0:512], start=True, stop=True)
            e.matmul(P[1][:, :], ones_f[0:1, 0:128], lnrow[0:1, 512:1024], start=True, stop=True)
            e.matmul(P[2][:, :], ones_f[0:1, 0:128], lnrow[0:1, 1024:1536], start=True, stop=True)
            e.matmul(P[3][:, :], ones_f[0:1, 0:128], lnrow[0:1, 1536:2048], start=True, stop=True)
        def v(e):
            e.tensor_copy(gbG[:, 0:512], P[0][:, :])
            e.drain()
            e.tensor_copy(gbG[:, 512:1024], P[1][:, :])
            e.drain()
            e.tensor_copy(gbB[:, 0:512], P[2][:, :])
            e.drain()
            e.tensor_copy(gbB[:, 512:1024], P[3][:, :])
        return t, v

    def ldwo(e):
        for cb in range(NCB):
            pb.dma(e, wts2[:, cb * D:(cb + 1) * D], Wo1[cb * 128:(cb + 1) * 128, :], "g")
        ldlp(e, ln1g, ln1b)
    pb.step(gpsimd=ldwo)
    tfn, vfn = gb_build("ln1")
    pb.step(tensor=tfn)
    pb.step(vector=vfn)

    def wo_ln(spill, resid_dram, gb=True):
        for qt in range(NQT):
            def ldres(e, qt=qt):
                pb.dma(e, xt_l[:, :], resid_dram[qt * 128:(qt + 1) * 128, :], "h")
            pb.step(sync=ldres)

            def womm(e, qt=qt):
                for oc in range(2):
                    for r in range(NCB):
                        e.matmul(P[oc][:, :],
                                 ctxT[:, r * T + qt * 128: r * T + (qt + 1) * 128],
                                 wts2[:, r * D + oc * 512: r * D + (oc + 1) * 512],
                                 start=(r == 0), stop=(r == NCB - 1))
            pb.step(tensor=womm)

            def ev(e):
                e.scalar_tensor_tensor(scrA[:, 0:512], P[0][:, :], 1.0, xt_l[:, 0:512],
                                       ALU.mult, ALU.add, accum_out=acol[:, 6:7])
                e.drain()
                e.scalar_tensor_tensor(scrA[:, 512:1024], P[1][:, :], 1.0, xt_l[:, 512:1024],
                                       ALU.mult, ALU.add, accum_out=acol[:, 7:8])
                e.drain()
                e.tensor_tensor(acol[:, 8:9], acol[:, 6:7], acol[:, 7:8], ALU.add)
            pb.step(vector=ev)

            _ln_emit(pb, scrA, acol, xt_l, xbf, None, None)

            if gb:
                def gbap(e):
                    e.tensor_tensor(xt_l[:, 0:D], xt_l[:, 0:D], gbG[:, 0:D], ALU.mult)
                    e.drain()
                    e.tensor_tensor(xt_l[:, 0:D], xt_l[:, 0:D], gbB[:, 0:D], ALU.add)
                    e.drain()
                    e.tensor_copy(xbf[:, 0:D], xt_l[:, 0:D])
                pb.step(vector=gbap)

            def spl(e, qt=qt):
                pb.dma(e, spill[qt * 128:(qt + 1) * 128, :], xt_l[:, 0:D], "h")
            pb.step(sync=spl)

            def trh(e):
                for cb in range(NCB):
                    e.transpose(PT[:, cb * 128:(cb + 1) * 128],
                                xbf[:, cb * 128:(cb + 1) * 128], ident_b[:, :])
            pb.step(tensor=trh)

            def trhe(e, qt=qt):
                for cb in range(NCB):
                    e.tensor_copy(hT[:, cb * T + qt * 128: cb * T + (qt + 1) * 128],
                                  PT[:, cb * 128:(cb + 1) * 128])
            pb.step(vector=trhe)

    wo_ln(h1sp, x3sp, gb=True)

    if LIMITB < 6:
        return _finishB(pb)

    hidT = ctxT
    w2b = pb.sbm("w2b", 128, 2 * D, BF16, side="right")

    def emit_ffn(W1, W2d, b1_slice, nsup, e_idx, accF, init_mode):
        for s in range(nsup):
            def ldw1(e, s=s):
                for cb in range(NCB):
                    if e_idx is None:
                        pb.dma(e, wts2[:, cb * D:(cb + 1) * D],
                               W1[cb * 128:(cb + 1) * 128, s * 1024:(s + 1) * 1024], "g")
                    else:
                        pb.dma(e, wts2[:, cb * D:(cb + 1) * D],
                               W1[e_idx, cb * 128:(cb + 1) * 128, s * 1024:(s + 1) * 1024], "g")
            pb.step(gpsimd=ldw1)

            for fb in range(8):
                def bload(e, s=s, fb=fb):
                    off = (0 if e_idx is None else e_idx * FF) + s * 1024 + fb * 128
                    pb.dma(e, bcol[:, 0:1], b1_slice[off:off + 128, :], "h")
                pb.step(sync=bload)

                def f1mm(e, s=s, fb=fb):
                    for ch in range(4):
                        for r in range(NCB):
                            e.matmul(P[ch][:, :],
                                     wts2[:, r * D + fb * 128: r * D + (fb + 1) * 128],
                                     hT[:, r * T + ch * 512: r * T + (ch + 1) * 512],
                                     start=(r == 0), stop=(r == NCB - 1))
                pb.step(tensor=f1mm)

                def f1ev(e, fb=fb):
                    for ch in range(4):
                        e.activation(hidT[:, fb * T + ch * 512: fb * T + (ch + 1) * 512],
                                     P[ch][:, :], AF.Relu, bias=bcol[:, 0:1])
                pb.step(scalar=f1ev)

            def ldw2all(e, s=s):
                for fb in range(8):
                    rw = s * 1024 + fb * 128
                    if e_idx is None:
                        pb.dma(e, wts2[:, fb * D:(fb + 1) * D], W2d[rw:rw + 128, :], "g")
                    else:
                        pb.dma(e, wts2[:, fb * D:(fb + 1) * D], W2d[e_idx, rw:rw + 128, :], "g")
            pb.step(gpsimd=ldw2all)

            for qt in range(NQT):
                def f2mm(e, s=s, qt=qt):
                    for fb in range(8):
                        for oc in range(2):
                            e.matmul(P[4 + oc][:, :],
                                     hidT[:, fb * T + qt * 128: fb * T + (qt + 1) * 128],
                                     wts2[:, fb * D + oc * 512: fb * D + (oc + 1) * 512],
                                     start=(fb == 0), stop=(fb == 7))
                pb.step(tensor=f2mm)

                def f2ev(e, s=s, qt=qt):
                    for oc in range(2):
                        dst = accF[:, qt * D + oc * 512: qt * D + (oc + 1) * 512]
                        psrc = P[4 + oc][:, :]
                        if init_mode == "add":
                            e.tensor_tensor(dst, psrc, dst, ALU.add)
                        else:
                            e.scalar_tensor_tensor(dst, psrc,
                                                   msks[:, qt * 8 + e_idx: qt * 8 + e_idx + 1],
                                                   dst, ALU.mult, ALU.add)
                pb.step(vector=f2ev)

    def ldb2(e):
        pb.dma(e, lnrow[0:1, 0:D], bff2r, "h")
    pb.step(sync=ldb2)
    for qt in range(NQT):
        def b2mm(e, qt=qt):
            e.matmul(P[0][:, :], ones_f[0:1, 0:128], lnrow[0:1, 0:512], start=True, stop=True)
            e.matmul(P[1][:, :], ones_f[0:1, 0:128], lnrow[0:1, 512:1024], start=True, stop=True)
        pb.step(tensor=b2mm)

        def b2ev(e, qt=qt):
            e.tensor_copy(accF[:, qt * D: qt * D + 512], P[0][:, :])
            e.drain()
            e.tensor_copy(accF[:, qt * D + 512: (qt + 1) * D], P[1][:, :])
            e.drain()
        pb.step(vector=b2ev)
    emit_ffn(Wff1, Wff2, bff1c, 4, None, accF, "add")

    def ldlp2(e):
        pb.dma(e, lnrow[0:1, 0:D], ln2g, "h")
        pb.dma(e, lnrow[0:1, D:2 * D], ln2b, "h")
    pb.step(sync=ldlp2)
    tfn2, vfn2 = gb_build("ln2")
    pb.step(tensor=tfn2)
    pb.step(vector=vfn2)

    for qt in range(NQT):
        def ldh1(e, qt=qt):
            pb.dma(e, xt_l[:, :], h1sp[qt * 128:(qt + 1) * 128, :], "h")
        pb.step(sync=ldh1)

        def ev2(e, qt=qt):
            e.scalar_tensor_tensor(scrA[:, 0:512], accF[:, qt * D: qt * D + 512],
                                   1.0, xt_l[:, 0:512], ALU.mult, ALU.add,
                                   accum_out=acol[:, 6:7])
            e.drain()
            e.scalar_tensor_tensor(scrA[:, 512:1024], accF[:, qt * D + 512:(qt + 1) * D],
                                   1.0, xt_l[:, 512:1024], ALU.mult, ALU.add,
                                   accum_out=acol[:, 7:8])
            e.drain()
            e.tensor_tensor(acol[:, 8:9], acol[:, 6:7], acol[:, 7:8], ALU.add)
        pb.step(vector=ev2)

        _ln_emit(pb, scrA, acol, xt_l, xbf, None, None)

        def gbap2(e):
            e.tensor_tensor(xt_l[:, 0:D], xt_l[:, 0:D], gbG[:, 0:D], ALU.mult)
            e.drain()
            e.tensor_tensor(xt_l[:, 0:D], xt_l[:, 0:D], gbB[:, 0:D], ALU.add)
            e.drain()
            e.tensor_copy(xbf[:, 0:D], xt_l[:, 0:D])
        pb.step(vector=gbap2)

        def spl2(e, qt=qt):
            pb.dma(e, h2sp[qt * 128:(qt + 1) * 128, :], xt_l[:, 0:D], "h")
        pb.step(sync=spl2)

        def trh2(e):
            for cb in range(NCB):
                e.transpose(PT[:, cb * 128:(cb + 1) * 128],
                            xbf[:, cb * 128:(cb + 1) * 128], ident_b[:, :])
        pb.step(tensor=trh2)

        def trhe2(e, qt=qt):
            for cb in range(NCB):
                e.tensor_copy(hT[:, cb * T + qt * 128: cb * T + (qt + 1) * 128],
                              PT[:, cb * 128:(cb + 1) * 128])
                e.drain()
        pb.step(vector=trhe2)

    if LIMITB < 7:
        return _finishB(pb)

    mskb = pb.sbm("mskb", 128, 8, BF16, side="right")
    for qt in range(NQT):
        def gmm(e, qt=qt):
            for r in range(NCB):
                e.matmul(P[0][:, 0:8],
                         hT[:, r * T + qt * 128: r * T + (qt + 1) * 128],
                         wgsb[:, r * 8:(r + 1) * 8],
                         start=(r == 0), stop=(r == NCB - 1))
        pb.step(tensor=gmm)

        def gred(e, qt=qt):
            e.reduce_max(acol[:, 0:1], P[0][:, 0:8], AX.X)
            e.drain()
            e.tensor_scalar(acol[:, 1:2], acol[:, 0:1], -1.0, None, ALU.mult)
            e.drain()
            e.tensor_scalar(msks[:, qt * 8:(qt + 1) * 8], P[0][:, 0:8],
                            acol[:, 0:1], None, ALU.is_ge)
            e.drain()
            e.tensor_copy(mskb[:, 0:8], msks[:, qt * 8:(qt + 1) * 8])
            e.drain()
        pb.step(vector=gred)

        def gexp(e, qt=qt):
            e.activation(scrA[:, 0:8], P[0][:, 0:8], AF.Exp, bias=acol[:, 1:2],
                         accum_out=acol[:, 2:3])
        pb.step(scalar=gexp)

        def ginv(e, qt=qt):
            e.reciprocal(gate_c[:, qt:qt + 1], acol[:, 2:3])
        pb.step(vector=ginv)

        def gtp(e, qt=qt):
            e.transpose(PT[0:8, 0:128], mskb[:, 0:8], ident_b[:, :])
        pb.step(tensor=gtp)

        def gte(e, qt=qt):
            e.tensor_copy(mskT[0:8, qt * 128:(qt + 1) * 128], PT[0:8, 0:128])
        pb.step(vector=gte)

    for qt in range(NQT):
        def bemm(e, qt=qt):
            for oc in range(2):
                e.matmul(P[oc][:, :], mskT[0:8, qt * 128:(qt + 1) * 128],
                         be2s[0:8, oc * 512:(oc + 1) * 512], start=True, stop=True)
        pb.step(tensor=bemm)

        def beev(e, qt=qt):
            e.tensor_copy(accF[:, qt * D: qt * D + 512], P[0][:, :])
            e.drain()
            e.tensor_copy(accF[:, qt * D + 512:(qt + 1) * D], P[1][:, :])
            e.drain()
        pb.step(vector=beev)

    for ei in range(NE):
        emit_ffn(We1, We2, be1c, 4, ei, accF, "mask")

    def ldlp3(e):
        pb.dma(e, lnrow[0:1, 0:D], ln3g, "h")
        pb.dma(e, lnrow[0:1, D:2 * D], ln3b, "h")
    pb.step(sync=ldlp3)
    tfn3, vfn3 = gb_build("ln3")
    pb.step(tensor=tfn3)
    pb.step(vector=vfn3)

    for qt in range(NQT):
        def ldh2(e, qt=qt):
            pb.dma(e, xt_l[:, :], h2sp[qt * 128:(qt + 1) * 128, :], "h")
        pb.step(sync=ldh2)

        def ev3(e, qt=qt):
            e.tensor_scalar(accF[:, qt * D:(qt + 1) * D], accF[:, qt * D:(qt + 1) * D],
                            gate_c[:, qt:qt + 1], None, ALU.mult)
            e.drain()
            e.scalar_tensor_tensor(scrA[:, 0:D], accF[:, qt * D:(qt + 1) * D],
                                   1.0, xt_l[:, 0:D], ALU.mult, ALU.add,
                                   accum_out=acol[:, 8:9])
            e.drain()
        pb.step(vector=ev3)

        _ln_emit(pb, scrA, acol, xt_l, xbf, None, None)

        def gbap3(e):
            e.tensor_tensor(xt_l[:, 0:D], xt_l[:, 0:D], gbG[:, 0:D], ALU.mult)
            e.drain()
            e.tensor_tensor(xt_l[:, 0:D], xt_l[:, 0:D], gbB[:, 0:D], ALU.add)
            e.drain()
            e.tensor_copy(xbf[:, 0:D], xt_l[:, 0:D])
        pb.step(vector=gbap3)

        def trh3(e):
            for cb in range(NCB):
                e.transpose(PT[:, cb * 128:(cb + 1) * 128],
                            xbf[:, cb * 128:(cb + 1) * 128], ident_b[:, :])
        pb.step(tensor=trh3)

        def trhe3(e, qt=qt):
            for cb in range(NCB):
                e.tensor_copy(hT[:, cb * T + qt * 128: cb * T + (qt + 1) * 128],
                              PT[:, cb * 128:(cb + 1) * 128])
                e.drain()
        pb.step(vector=trhe3)

    def ldwout(e):
        for cb in range(NCB):
            pb.dma(e, wts2[:, cb * D:(cb + 1) * D], Wout[cb * 128:(cb + 1) * 128, :], "g")
    pb.step(gpsimd=ldwout)

    for qt in range(NQT):
        def ldx3(e, qt=qt):
            pb.dma(e, xt_l[:, :], x3sp[qt * 128:(qt + 1) * 128, :], "h")
        pb.step(sync=ldx3)

        def omm(e, qt=qt):
            for oc in range(2):
                for r in range(NCB):
                    e.matmul(P[oc][:, :],
                             hT[:, r * T + qt * 128: r * T + (qt + 1) * 128],
                             wts2[:, r * D + oc * 512: r * D + (oc + 1) * 512],
                             start=(r == 0), stop=(r == NCB - 1))
        pb.step(tensor=omm)

        def evo(e):
            e.scalar_tensor_tensor(scrA[:, 0:512], P[0][:, :], 1.0, xt_l[:, 0:512],
                                   ALU.mult, ALU.add, accum_out=acol[:, 6:7])
            e.drain()
            e.scalar_tensor_tensor(scrA[:, 512:1024], P[1][:, :], 1.0, xt_l[:, 512:1024],
                                   ALU.mult, ALU.add, accum_out=acol[:, 7:8])
            e.drain()
            e.tensor_tensor(acol[:, 8:9], acol[:, 6:7], acol[:, 7:8], ALU.add)
        pb.step(vector=evo)

        _ln_emit(pb, scrA, acol, xt_l, xbf, None, None)

        def sto(e, qt=qt):
            pb.dma(e, xo[qt * 128:(qt + 1) * 128, :], xt_l[:, 0:D], "h")
        pb.step(sync=sto)

    return _finishB(pb)


def _finishB(pb):
    hf = pb.c["h"]; gf = pb.c["g"]

    def fence(e):
        if hf:
            e.wait_ge(pb._sems["h"], hf)
        if gf:
            e.wait_ge(pb._sems["g"], gf)
    pb.block.sync(fence)
    return pb.done()


def _ln_emit(pb, xr_sb, acol, out_f32, out_bf, gcol, bcol):
    AFx = AF

    def sq(e):
        e.scalar_tensor_tensor(out_bf[:, 0:D], xr_sb[:, 0:D], 1.0, xr_sb[:, 0:D],
                               ALU.mult, ALU.mult, accum_out=acol[:, 9:10])
    pb.step(vector=sq)

    def stats(e):
        e.tensor_scalar(acol[:, 10:11], acol[:, 8:9], 1.0 / D, None, ALU.mult)
        e.drain()
        e.tensor_tensor(acol[:, 11:12], acol[:, 10:11], acol[:, 10:11], ALU.mult)
        e.drain()
        e.scalar_tensor_tensor(acol[:, 12:13], acol[:, 9:10], 1.0 / D, acol[:, 11:12],
                               ALU.mult, ALU.subtract)
        e.drain()
        e.tensor_scalar(acol[:, 12:13], acol[:, 12:13], 1e-5, None, ALU.add)
    pb.step(vector=stats)

    def rstd(e):
        e.activation(acol[:, 13:14], acol[:, 12:13], AF.Ln)
        e.drain()
        e.activation(acol[:, 14:15], acol[:, 13:14], AF.Exp, scale=-0.5)
    pb.step(scalar=rstd)

    def nmr(e):
        e.scalar_tensor_tensor(acol[:, 15:16], acol[:, 10:11], -1.0, acol[:, 14:15],
                               ALU.mult, ALU.mult)
    pb.step(vector=nmr)

    def norm(e):
        e.tensor_scalar(out_f32[:, 0:D], xr_sb[:, 0:D], acol[:, 14:15], acol[:, 15:16],
                        ALU.mult, ALU.add)
        e.tensor_scalar(out_bf[:, 0:D], xr_sb[:, 0:D], acol[:, 14:15], acol[:, 15:16],
                        ALU.mult, ALU.add)
    pb.step(vector=norm)


def _s4_consts():
    L = 8; M = 5
    om = np.exp(np.complex64(-2j * np.pi / L) * np.arange(M, dtype=np.complex64))
    om = om.astype(np.complex64)
    z = (2 * (1 - om) / (1 + om)).astype(np.complex64)
    w2 = (2 / (1 + om)).astype(np.complex64)
    Z = [(float(np.float32(z[m].real)), float(np.float32(z[m].imag))) for m in range(M)]
    W2 = [(float(np.float32(w2[m].real)), float(np.float32(w2[m].imag))) for m in range(M)]
    IRW = []
    for m in range(M):
        row = []
        for l in range(L):
            if m == 0:
                wr_, wi_ = 1.0 / L, 0.0
            elif m == 4:
                wr_, wi_ = ((-1.0) ** l) / L, 0.0
            else:
                wr_ = 2.0 * np.cos(2 * np.pi * m * l / L) / L
                wi_ = -2.0 * np.sin(2 * np.pi * m * l / L) / L
            row.append((float(np.float32(wr_)), float(np.float32(wi_))))
        IRW.append(row)
    return Z, W2, IRW


def prep_A(inputs, b):
    x = np.asarray(inputs["L0_attn_x"])[b]
    rm = np.asarray(inputs["L0_ffn_rail_maker"])
    up = np.asarray(inputs["L0_ffn_rail_up_proj"])
    bdm = np.zeros((32, 128, 128), np.float32)
    for hh in range(4):
        for g8 in range(8):
            for rr in range(8):
                r = g8 * 8 + rr
                bdm[hh * 8 + g8, 16 * rr:16 * rr + 16, 16 * rr:16 * rr + 16] = rm[hh, r]
    uts = np.zeros((128, 128, 128), np.float32)
    for s in range(8):
        hh = s // 2
        rbase = (s % 2) * 32
        for m in range(16):
            gl = m // 4
            gs = rbase + 8 * gl
            for pi in range(2):
                r = rbase + 2 * m + pi
                uts[s * 16 + m, 16 * (r - gs):16 * (r - gs) + 16,
                    64 * pi:64 * pi + 64] = up[hh, r].T
    pm = np.concatenate([np.ones(64, np.float32), -np.ones(64, np.float32)])
    bdo = np.zeros((128, 128), np.float32)
    bdo[0:64, 0:64] = 1.0
    bdo[64:128, 64:128] = 1.0
    return {
        "x0": np.ascontiguousarray(x, np.float32),
        "x0Td": np.ascontiguousarray(x.T, np.float32),
        "WqT": np.ascontiguousarray(np.asarray(inputs["L0_attn_Wq"]).T, np.float32),
        "WkT": np.ascontiguousarray(np.asarray(inputs["L0_attn_Wk"]).T, np.float32),
        "WvT": np.ascontiguousarray(np.asarray(inputs["L0_attn_Wv"]).T, np.float32),
        "WoT": np.ascontiguousarray(np.asarray(inputs["L0_attn_Wo"]).T, np.float32),
        "bo": np.asarray(inputs["L0_attn_bo"], np.float32).reshape(1, D),
        "lq1": np.asarray(inputs["L0_attn_lambda_q1"], np.float32).reshape(1, 64),
        "lk1": np.asarray(inputs["L0_attn_lambda_k1"], np.float32).reshape(1, 64),
        "lq2": np.asarray(inputs["L0_attn_lambda_q2"], np.float32).reshape(1, 64),
        "lk2": np.asarray(inputs["L0_attn_lambda_k2"], np.float32).reshape(1, 64),
        "dgr": np.asarray(inputs["L0_attn_delta_gain"], np.float32).reshape(1, 8),
        "chd": np.asarray(inputs["L0_attn_cos_head_delta"], np.float32).reshape(1, 8),
        "raw": np.asarray(inputs["L0_attn_cos_logit_scale_raw"], np.float32).reshape(1, 1),
        "subw": np.asarray(inputs["L0_attn_subln_weight"], np.float32).reshape(128, 1),
        "pm64": pm.reshape(128, 1),
        "bdo": bdo,
        "bdm": bdm,
        "uts": uts,
        "wcT": np.ascontiguousarray(np.asarray(inputs["L0_ffn_w_cproj"]).T, np.float32),
    }



def prep_B(inputs, x2_full, b):
    g = lambda k: np.asarray(inputs[k], np.float32)

    def arr_pbn(a):
        return np.ascontiguousarray(a.reshape(8, 128, 16).transpose(1, 0, 2).reshape(128, 128))

    x2 = np.ascontiguousarray(x2_full[b], np.float32)
    x2rT = np.stack([np.ascontiguousarray(x2_full[(b - j) % 8].T) for j in range(8)], 0)
    wcvT = np.zeros((8, 8), np.float32)
    for j in range(8):
        if b - j >= 0:
            wcvT[j, j] = 1.0
    ldt = g("L1_attn_log_dt").reshape(1024)
    ldt_bc = np.repeat(ldt.reshape(8, 128).T.reshape(128, 8, 1), 16, 2).reshape(128, 128)
    B_ = g("L1_attn_B"); P_ = g("L1_attn_P")[0]; C_ = g("L1_attn_C")
    pe_pos = np.arange(T - 1, -1, -1, dtype=np.float32)
    inv_freq = (1.0 / (10000.0 ** (np.arange(0, D, 2, dtype=np.float32) / np.float32(D)))).astype(np.float32)
    sinusoid = pe_pos[:, None] * inv_freq[None, :]
    pe = np.concatenate([np.sin(sinusoid), np.cos(sinusoid)], 1).astype(np.float32)
    return {
        "x2": x2,
        "x2rT": np.ascontiguousarray(x2rT, np.float32),
        "wcvT": wcvT,
        "ldt_bc": np.ascontiguousarray(ldt_bc, np.float32),
        "iwr_a": arr_pbn(g("L1_attn_inv_w_real")),
        "wim_a": arr_pbn(g("L1_attn_w_imaginary")),
        "Br_a": arr_pbn(B_[:, :, 0]), "Bi_a": arr_pbn(B_[:, :, 1]),
        "Pr_a": arr_pbn(P_[:, :, 0]), "Pi_a": arr_pbn(P_[:, :, 1]),
        "Cr_a": arr_pbn(C_[:, :, 0]), "Ci_a": arr_pbn(C_[:, :, 1]),
        "Dd_c": np.ascontiguousarray(g("L1_attn_D").reshape(8, 128).T),
        "peT": np.ascontiguousarray(pe.T),
        "Wqkv": g("L1_ffn_Wqkv"),
        "Wo1": g("L1_ffn_Wo"),
        "Wr": g("L1_ffn_Wr"),
        "ubT": np.ascontiguousarray(g("L1_ffn_u_bias").T),
        "vbT": np.ascontiguousarray(g("L1_ffn_v_bias").T),
        "ln1g": g("L1_ffn_ln1_g").reshape(1, D), "ln1b": g("L1_ffn_ln1_b").reshape(1, D),
        "ln2g": g("L1_ffn_ln2_g").reshape(1, D), "ln2b": g("L1_ffn_ln2_b").reshape(1, D),
        "ln3g": g("L1_ffn_ln3_g").reshape(1, D), "ln3b": g("L1_ffn_ln3_b").reshape(1, D),
        "Wff1": g("L1_ffn_Wff1"), "bff1c": g("L1_ffn_bff1").reshape(FF, 1),
        "Wff2": g("L1_ffn_Wff2"), "bff2r": g("L1_ffn_bff2").reshape(1, D),
        "Wg": g("L1_ffn_Wg"),
        "We1": g("L1_ffn_We1"), "be1c": g("L1_ffn_be1").reshape(NE * FF, 1),
        "We2": g("L1_ffn_We2"), "be2m": g("L1_ffn_be2"),
        "Wout": g("L1_ffn_Wout"),
    }


_CACHE = {}


def run_A(inputs):
    if "A" not in _CACHE:
        _CACHE["A"] = build_launchA()
    ncA = _CACHE["A"]
    maps = [prep_A(inputs, b) for b in range(NCORE)]
    res = run_bass_kernel_spmd(ncA, maps, core_ids=list(range(NCORE)))
    return np.stack([res.results[b]["xout"] for b in range(NCORE)], 0)


def run_B(inputs, x2_full):
    if "B" not in _CACHE:
        _CACHE["B"] = build_launchB()
    ncB = _CACHE["B"]
    maps = [prep_B(inputs, x2_full, b) for b in range(NCORE)]
    res = run_bass_kernel_spmd(ncB, maps, core_ids=list(range(NCORE)))
    return np.stack([res.results[b]["xo"] for b in range(NCORE)], 0)


def kernel(**inputs):
    x2 = run_A(inputs)
    x4 = run_B(inputs, x2)
    return x4


# version 05
# speedup vs baseline: 1.1673x; 1.1673x over previous
import numpy as np
import ml_dtypes
from contextlib import ExitStack

import concourse.bass as bass
import concourse.mybir as mybir
from concourse.bass_utils import run_bass_kernel_spmd

F32 = mybir.dt.float32
BF16 = mybir.dt.bfloat16
AF = mybir.ActivationFunctionType
ALU = mybir.AluOpType
AX = mybir.AxisListType

NCORE = 8
T = 2048
D = 1024
NQT = T // 128
NCB = D // 128
BF = ml_dtypes.bfloat16

ENGS = ("mm", "vv", "ss", "pp", "sy", "h", "g")
MS = {"tensor": "mm", "vector": "vv", "scalar": "ss", "gpsimd": "pp", "sync": "sy"}


_NOREC = {"dma_start", "wait_ge", "nop", "drain", "print"}


class _Rec:

    def __init__(self, e):
        self._e = e
        self.last = None

    def __getattr__(self, n):
        a = getattr(self._e, n)
        if callable(a) and not n.startswith("_"):
            def w(*args, __a=a, __n=n, **kw):
                r = __a(*args, **kw)
                if __n not in _NOREC and r is not None and hasattr(r, "then_inc"):
                    self.last = r
                return r
            return w
        return a


class PB:
    def __init__(self):
        self.nc = bass.Bass("TRN2", target_bir_lowering=False)
        self.ctx = ExitStack()
        self._sems = {}
        self.c = {}
        for s in ENGS:
            self.sem(s)

    def sem(self, name):
        if name not in self._sems:
            self._sems[name] = self.ctx.enter_context(self.nc.semaphore(name))
            self.c[name] = 0
        return self._sems[name]

    def sb(self, name, p, f, dt, side="left"):
        return self.ctx.enter_context(self.nc.sbuf_tensor(name, [p, f], dt, side=side))

    def sbm(self, name, p, f, dt, side="left"):
        cm = self.nc.sbuf_tensor(name, [p, f], dt, side=side)
        t = cm.__enter__()
        if not hasattr(self, "_stack"):
            self._stack = []
        self._stack.append((name, cm))
        return t

    def free(self, name):
        n, cm = self._stack.pop()
        assert n == name, f"free order: want {name} got {n}"
        cm.__exit__(None, None, None)

    def ps(self, name, f, dt=F32):
        return self.ctx.enter_context(self.nc.psum_tensor(name, [128, f], dt))

    def begin(self):
        self.block = self.ctx.enter_context(self.nc.Block())

    def step(self, **fns):
        snap = dict(self.c)
        S = self._sems
        if not hasattr(self, "_lastw"):
            self._lastw = {}
        for name, fn in fns.items():
            ms = MS[name]
            lw = self._lastw.setdefault(name, {s: 0 for s in ENGS})

            def body(e, fn=fn, ms=ms, lw=lw):
                for s in ENGS:
                    if snap[s] > lw[s]:
                        e.wait_ge(S[s], snap[s])
                        lw[s] = snap[s]
                rec = _Rec(e)
                fn(rec)
                if rec.last is not None:
                    rec.last.then_inc(S[ms], 1)
                else:
                    e.nop().then_inc(S[ms], 1)

            getattr(self.block, name)(body)
            self.c[ms] += 1

    def dma(self, e, out, in_, q):
        sem = "h" if q == "h" else "g"
        e.dma_start(out=out, in_=in_).then_inc(self._sems[sem], 16)
        self.c[sem] += 16

    def done(self):
        for n, cm in reversed(getattr(self, "_stack", [])):
            cm.__exit__(None, None, None)
        self._stack = []
        self.ctx.close()
        return self.nc


LIMIT = 99
H_LIMIT = 99

def build_launchA():
    pb = PB()
    nc = pb.nc

    def din(name, shape):
        return nc.dram_tensor(name, list(shape), F32, kind="ExternalInput").ap()

    x0 = din("x0", (T, D))
    x0T_d = din("x0Td", (D, T))
    WqT = din("WqT", (D, D)); WkT = din("WkT", (D, D))
    WvT = din("WvT", (D, D)); WoT = din("WoT", (D, D))
    bo = din("bo", (1, D))
    lq1 = din("lq1", (1, 64)); lk1 = din("lk1", (1, 64))
    lq2 = din("lq2", (1, 64)); lk2 = din("lk2", (1, 64))
    dgr = din("dgr", (1, 8)); chd = din("chd", (1, 8)); raw = din("raw", (1, 1))
    subw = din("subw", (128, 1)); pm64 = din("pm64", (128, 1))
    bdo = din("bdo", (128, 128))
    bdm = din("bdm", (32, 128, 128))
    uts = din("uts", (128, 128, 128))
    wcT = din("wcT", (16 * D, D))
    xout = nc.dram_tensor("xout", [T, D], F32, kind="ExternalOutput").ap()
    x1sp = nc.dram_tensor("x1sp", [T, D], F32).ap()

    ident_b = pb.sb("ident_b", 128, 128, BF16)
    ones_b = pb.sb("ones_b", 128, 128, BF16)
    ones_f = pb.sb("ones_f", 128, 128, F32)
    bdo_s = pb.sb("bdo_s", 128, 128, F32)
    bo_s = pb.sb("bo_s", 1, D, BF16)
    lrow = pb.sb("lrow", 1, 512, F32)
    trow = pb.sb("trow", 1, 64, F32)
    crow = pb.sb("crow", 1, 16, F32)
    cs_cols = pb.sb("cs_cols", 128, 8, F32)
    dg_cols = pb.sb("dg_cols", 128, 8, F32)
    lamn_col = pb.sb("lamn_col", 128, 1, F32)
    subw_c = pb.sb("subw_c", 128, 1, F32)
    pm_c = pb.sb("pm_c", 128, 1, F32)
    scrA = pb.sb("scrA", 128, T, F32)
    scrB = pb.sb("scrB", 128, T, F32)
    scrC = pb.sb("scrC", 128, T, F32)
    e1c = pb.sb("e1c", 128, 512, BF16)
    e2c = pb.sb("e2c", 128, 512, BF16)
    gstr = pb.sb("gstr", 128, 512, BF16)
    xt_l = pb.sb("xt_l", 128, D, F32)
    k2s = pb.sb("k2s", 64, T, BF16)
    q2s = pb.sb("q2s", 64, T, BF16)
    xbf = pb.sb("xbf", 128, D, BF16)
    acol = pb.sb("acol", 128, 16, F32)

    kTn = pb.sbm("kTn", 128, NCB * T, BF16, side="right")
    v_b = pb.sbm("v_b", 128, NQT * D, BF16, side="right")
    qTn = pb.sbm("qTn", 128, NCB * T, BF16, side="right")
    wts = pb.sbm("wts", 128, NCB * D, BF16, side="right")
    x0T = pb.sbm("x0T", 128, NCB * T, BF16, side="right")

    P = [pb.ps(f"P{i}", 512) for i in range(7)]
    PT = pb.ps("PT", 1024, BF16)

    pb.begin()

    def l_sync(e):
        pb.dma(e, bdo_s[:, :], bdo, "h")
        pb.dma(e, subw_c[:, :], subw, "h")
        pb.dma(e, pm_c[:, :], pm64, "h")
        pb.dma(e, lrow[0:1, 0:64], lq1, "h")
        pb.dma(e, lrow[0:1, 64:128], lk1, "h")
        pb.dma(e, lrow[0:1, 128:192], lq2, "h")
        pb.dma(e, lrow[0:1, 192:256], lk2, "h")
        pb.dma(e, lrow[0:1, 256:264], dgr, "h")
        pb.dma(e, lrow[0:1, 264:272], chd, "h")
        pb.dma(e, lrow[0:1, 272:273], raw, "h")

    def l_g(e):
        for cb in range(NCB):
            pb.dma(e, x0T[:, cb * T:(cb + 1) * T], x0T_d[cb * 128:(cb + 1) * 128, :], "g")
        pb.dma(e, bo_s[0:1, :], bo, "g")
        e.memset(ones_b[:, :], 1.0)
        e.memset(ones_f[:, :], 1.0)
        e.drain()
        e.affine_select(ident_b[:, :], ones_b[:, :], [[-1, 128]], ALU.is_equal, 0.0,
                        channel_multiplier=1)

    pb.step(sync=l_sync, gpsimd=l_g)

    def p1v(e):
        e.tensor_tensor(trow[0:1, 0:64], lrow[0:1, 0:64], lrow[0:1, 64:128], ALU.mult)
        e.drain()
        e.reduce_sum(crow[0:1, 0:1], trow[0:1, 0:64], AX.X)
        e.drain()
        e.tensor_tensor(trow[0:1, 0:64], lrow[0:1, 128:192], lrow[0:1, 192:256], ALU.mult)
        e.drain()
        e.reduce_sum(crow[0:1, 1:2], trow[0:1, 0:64], AX.X)
        e.drain()
        e.reduce_sum(crow[0:1, 2:3], lrow[0:1, 264:272], AX.X)

    pb.step(vector=p1v)

    def p1s(e):
        e.activation(crow[0:1, 3:4], crow[0:1, 0:1], AF.Exp)
        e.drain()
        e.activation(crow[0:1, 4:5], crow[0:1, 1:2], AF.Exp)
        e.drain()
        e.activation(crow[0:1, 5:6], lrow[0:1, 272:273], AF.Sigmoid)

    pb.step(scalar=p1s)

    def p1v2(e):
        e.tensor_tensor(crow[0:1, 6:7], crow[0:1, 4:5], crow[0:1, 3:4], ALU.subtract)
        e.drain()
        e.tensor_scalar(crow[0:1, 6:7], crow[0:1, 6:7], -0.5, None, ALU.add)
        e.drain()
        e.tensor_scalar(crow[0:1, 7:8], crow[0:1, 5:6], 15.0, None, ALU.mult)
        e.drain()
        e.tensor_scalar(crow[0:1, 2:3], crow[0:1, 2:3], 0.125, None, ALU.mult)
        e.drain()
        e.tensor_scalar(trow[0:1, 0:8], lrow[0:1, 264:272], crow[0:1, 2:3], None,
                        ALU.subtract)

    pb.step(vector=p1v2)

    def p1s2(e):
        e.activation(trow[0:1, 8:16], trow[0:1, 0:8], AF.Tanh)

    pb.step(scalar=p1s2)

    def p1v3(e):
        e.tensor_scalar(trow[0:1, 16:24], trow[0:1, 8:16], 0.5, 1.0, ALU.mult, ALU.add)
        e.drain()
        e.tensor_scalar(trow[0:1, 16:24], trow[0:1, 16:24], crow[0:1, 7:8], None,
                        ALU.mult)
        e.drain()
        e.tensor_scalar(subw_c[:, 0:1], subw_c[:, 0:1], 0.5, None, ALU.mult)

    pb.step(vector=p1v3)

    def p1t(e):
        e.matmul(P[0][:, 0:8], ones_f[0:1, 0:128], trow[0:1, 16:24], start=True, stop=True)
        e.matmul(P[1][:, 0:8], ones_f[0:1, 0:128], lrow[0:1, 256:264], start=True, stop=True)
        e.matmul(P[2][:, 0:1], ones_f[0:1, 0:128], crow[0:1, 6:7], start=True, stop=True)

    pb.step(tensor=p1t)

    def p1v4(e):
        e.tensor_copy(cs_cols[:, :], P[0][:, 0:8])
        e.drain()
        e.tensor_copy(dg_cols[:, :], P[1][:, 0:8])
        e.drain()
        e.tensor_copy(lamn_col[:, :], P[2][:, 0:1])

    pb.step(vector=p1v4)

    if LIMIT < 2:
        return _finishA(pb)
    def load_w(wdr):
        def f(e):
            for cb in range(NCB):
                pb.dma(e, wts[:, cb * D:(cb + 1) * D],
                       wdr[cb * 128:(cb + 1) * 128, :], "g")
        return f

    for side, wdr, dstn in (("q", WqT, qTn), ("k", WkT, kTn)):
        pb.step(gpsimd=load_w(wdr))
        for h in range(NCB):
            def mmf(e, h=h):
                for ch in range(4):
                    for r in range(NCB):
                        e.matmul(P[ch][:, :],
                                 wts[:, r * D + h * 128: r * D + (h + 1) * 128],
                                 x0T[:, r * T + ch * 512: r * T + (ch + 1) * 512],
                                 start=(r == 0), stop=(r == NCB - 1))
            pb.step(tensor=mmf)

            def evacf(e):
                for ch in range(4):
                    e.tensor_copy(scrA[:, ch * 512:(ch + 1) * 512], P[ch][:, :])
            pb.step(vector=evacf)

            def sqf(e):
                e.activation(scrB[:, :], scrA[:, :], AF.Square)
            pb.step(scalar=sqf)

            def nrm_mm(e):
                for ch in range(4):
                    e.matmul(P[ch][:, :], bdo_s[:, :], scrB[:, ch * 512:(ch + 1) * 512],
                             start=True, stop=True)
            pb.step(tensor=nrm_mm)

            def nrm_max(e):
                for ch in range(4):
                    e.tensor_scalar(scrB[:, ch * 512:(ch + 1) * 512], P[ch][:, :],
                                    1e-12, None, ALU.max)
            pb.step(vector=nrm_max)

            def nrm_inv(e):
                e.activation(scrC[:, :], scrB[:, :], AF.Ln)
                e.drain()
                e.activation(scrC[:, :], scrC[:, :], AF.Exp, scale=-0.5)
            pb.step(scalar=nrm_inv)

            if side == "q":
                def mulq(e, h=h):
                    e.scalar_tensor_tensor(qTn[:, h * T:(h + 1) * T], scrA[:, :],
                                           cs_cols[:, h:h + 1], scrC[:, :],
                                           ALU.mult, ALU.mult)
                pb.step(vector=mulq)
            else:
                def mulk(e, h=h):
                    e.scalar_tensor_tensor(kTn[:, h * T:(h + 1) * T], scrA[:, :],
                                           pm_c[:, 0:1], scrC[:, :],
                                           ALU.mult, ALU.mult)
                pb.step(vector=mulk)

    pb.step(gpsimd=load_w(WvT))
    for qp in range(NQT // 2):
        def vmm(e, qp=qp):
            for half in range(2):
                qt = qp * 2 + half
                for oc in range(2):
                    for r in range(NCB):
                        e.matmul(P[half * 2 + oc][:, :],
                                 x0T[:, r * T + qt * 128: r * T + (qt + 1) * 128],
                                 wts[:, r * D + oc * 512: r * D + (oc + 1) * 512],
                                 start=(r == 0), stop=(r == NCB - 1))
        pb.step(tensor=vmm)

        def vev(e, qp=qp):
            for half in range(2):
                qt = qp * 2 + half
                for oc in range(2):
                    e.tensor_copy(v_b[:, qt * D + oc * 512: qt * D + (oc + 1) * 512],
                                  P[half * 2 + oc][:, :])
        pb.step(vector=vev)

    pb.free("x0T")
    if LIMIT < 3:
        return _finishA(pb)

    for h in range(min(NCB, H_LIMIT)):
        def cph(e, h=h):
            e.tensor_copy(k2s[:, :], kTn[64:128, h * T:(h + 1) * T])
            e.drain()
            e.tensor_copy(q2s[:, :], qTn[64:128, h * T:(h + 1) * T])
        pb.step(vector=cph)
        for qc in range(4):
            for kt in range(NQT):
                k1 = kTn[0:64, h * T + kt * 128: h * T + (kt + 1) * 128]
                k2 = k2s[:, kt * 128:(kt + 1) * 128]
                q1 = qTn[0:64, h * T + qc * 512: h * T + (qc + 1) * 512]
                q2 = q2s[:, qc * 512:(qc + 1) * 512]

                def lgmm(e, k1=k1, k2=k2, q1=q1, q2=q2):
                    e.matmul(P[0][:, :], k1, q1, start=True, stop=False)
                    e.matmul(P[2][:, :], k1, q1, start=True, stop=False)
                    e.matmul(P[1][:, :], k2, q2, start=True, stop=False)
                    e.matmul(P[2][:, :], k2, q2, start=False, stop=True)
                pb.step(tensor=lgmm)

                def gtan(e, h=h):
                    e.activation(gstr[:, :], P[2][:, :], AF.Tanh,
                                 scale=dg_cols[:, h:h + 1])
                pb.step(scalar=gtan)

                def gadd(e):
                    e.matmul(P[0][:, :], ident_b[:, :], gstr[:, :], start=False, stop=True)
                    e.matmul(P[1][:, :], ident_b[:, :], gstr[:, :], start=False, stop=True)
                pb.step(tensor=gadd)

                def eexp(e):
                    e.activation(e1c[:, :], P[0][:, :], AF.Exp)
                    e.drain()
                    e.activation(e2c[:, :], P[1][:, :], AF.Exp, scale=-1.0)
                pb.step(scalar=eexp)

                vt = v_b[:, kt * D + h * 128: kt * D + (h + 1) * 128]

                def avmm(e, vt=vt, kt=kt):
                    st = (kt == 0); sp = (kt == NQT - 1)
                    e.matmul(P[3][:, :], vt, e1c[:, :], start=st, stop=sp)
                    e.matmul(P[4][:, :], vt, e2c[:, :], start=st, stop=sp)
                    e.matmul(P[5][:, :], ones_b[:, :], e1c[:, :], start=st, stop=sp)
                    e.matmul(P[6][:, :], ones_b[:, :], e2c[:, :], start=st, stop=sp)
                pb.step(tensor=avmm)

            def fin_v(e):
                e.reciprocal(scrB[:, 0:512], P[5][:, :])
                e.drain()
                e.reciprocal(scrB[:, 512:1024], P[6][:, :])
                e.drain()
                e.tensor_tensor(scrA[:, 0:512], P[3][:, :], scrB[:, 0:512], ALU.mult)
                e.drain()
                e.tensor_tensor(scrA[:, 512:1024], P[4][:, :], scrB[:, 512:1024], ALU.mult)
                e.drain()
                e.scalar_tensor_tensor(scrB[:, 1024:1536], scrA[:, 512:1024],
                                       lamn_col[:, 0:1], scrA[:, 0:512],
                                       ALU.mult, ALU.add)
            pb.step(vector=fin_v)

            def fin_sq(e):
                e.activation(scrC[:, 0:512], scrB[:, 1024:1536], AF.Square)
            pb.step(scalar=fin_sq)

            def fin_mm(e):
                e.matmul(P[0][:, :], ones_f[:, :], scrC[:, 0:512], start=True, stop=True)
            pb.step(tensor=fin_mm)

            def fin_ms(e):
                e.tensor_scalar(scrC[:, 512:1024], P[0][:, :], 1.0 / 128, 1e-5,
                                ALU.mult, ALU.add)
            pb.step(vector=fin_ms)

            def fin_ln(e):
                e.activation(scrC[:, 1024:1536], scrC[:, 512:1024], AF.Ln)
                e.drain()
                e.activation(scrC[:, 512:1024], scrC[:, 1024:1536], AF.Exp, scale=-0.5)
            pb.step(scalar=fin_ln)

            def fin_out(e, h=h, qc=qc):
                e.tensor_tensor(scrA[:, 0:512], scrB[:, 1024:1536], scrC[:, 512:1024],
                                ALU.mult)
                e.drain()
                e.tensor_scalar(qTn[:, h * T + qc * 512: h * T + (qc + 1) * 512],
                                scrA[:, 0:512], subw_c[:, 0:1], None, ALU.mult)
            pb.step(vector=fin_out)

    ysubT = qTn
    if LIMIT < 4:
        return _finishA(pb)

    pb.step(gpsimd=load_w(WoT))
    x1T = kTn
    for qt in range(NQT):
        def ldx(e, qt=qt):
            pb.dma(e, xt_l[:, :], x0[qt * 128:(qt + 1) * 128, :], "h")
        pb.step(sync=ldx)

        def womm2(e, qt=qt):
            for oc in range(2):
                for r in range(NCB):
                    e.matmul(P[oc][:, :],
                             ysubT[:, r * T + qt * 128: r * T + (qt + 1) * 128],
                             wts[:, r * D + oc * 512: r * D + (oc + 1) * 512],
                             start=(r == 0), stop=False)
                e.matmul(P[oc][:, :], ones_b[0:1, 0:128],
                         bo_s[0:1, oc * 512:(oc + 1) * 512], start=False, stop=True)
        pb.step(tensor=womm2)

        def ev(e):
            e.scalar_tensor_tensor(scrA[:, 0:512], P[0][:, :], 1.0, xt_l[:, 0:512],
                                   ALU.mult, ALU.add, accum_out=acol[:, 0:1])
            e.drain()
            e.scalar_tensor_tensor(scrA[:, 512:1024], P[1][:, :], 1.0, xt_l[:, 512:1024],
                                   ALU.mult, ALU.add, accum_out=acol[:, 1:2])
            e.drain()
            e.tensor_tensor(acol[:, 2:3], acol[:, 0:1], acol[:, 1:2], ALU.add)
        pb.step(vector=ev)

        def sq(e):
            e.activation(scrB[:, 0:1024], scrA[:, 0:1024], AF.Square,
                         accum_out=acol[:, 3:4])
        pb.step(scalar=sq)

        def stats(e):
            e.tensor_scalar(acol[:, 4:5], acol[:, 2:3], 1.0 / D, None, ALU.mult)
            e.drain()
            e.tensor_tensor(acol[:, 5:6], acol[:, 4:5], acol[:, 4:5], ALU.mult)
            e.drain()
            e.scalar_tensor_tensor(acol[:, 6:7], acol[:, 3:4], 1.0 / D, acol[:, 5:6],
                                   ALU.mult, ALU.subtract)
            e.drain()
            e.tensor_scalar(acol[:, 6:7], acol[:, 6:7], 1e-5, None, ALU.add)
        pb.step(vector=stats)

        def rstd(e):
            e.activation(acol[:, 7:8], acol[:, 6:7], AF.Ln)
            e.drain()
            e.activation(acol[:, 8:9], acol[:, 7:8], AF.Exp, scale=-0.5)
        pb.step(scalar=rstd)

        def nmr(e):
            e.scalar_tensor_tensor(acol[:, 9:10], acol[:, 4:5], -1.0, acol[:, 8:9],
                                   ALU.mult, ALU.mult)
        pb.step(vector=nmr)

        def norm(e):
            e.activation(xt_l[:, 0:1024], scrA[:, 0:1024], AF.Identity,
                         bias=acol[:, 9:10], scale=acol[:, 8:9])
            e.drain()
            e.activation(xbf[:, 0:1024], scrA[:, 0:1024], AF.Identity,
                         bias=acol[:, 9:10], scale=acol[:, 8:9])
        pb.step(scalar=norm)

        def spill(e, qt=qt):
            pb.dma(e, x1sp[qt * 128:(qt + 1) * 128, :], xt_l[:, 0:1024], "h")
        pb.step(sync=spill)

        def tr(e):
            for cb in range(NCB):
                e.transpose(PT[:, cb * 128:(cb + 1) * 128],
                            xbf[:, cb * 128:(cb + 1) * 128], ident_b[:, :])
        pb.step(tensor=tr)

        def trev(e, qt=qt):
            for cb in range(NCB):
                e.tensor_copy(x1T[:, cb * T + qt * 128: cb * T + (qt + 1) * 128],
                              PT[:, cb * 128:(cb + 1) * 128])
                e.drain()
        pb.step(vector=trev)

    if LIMIT < 5:
        return _finishA(pb)
    pb.free("wts")
    pb.free("qTn")
    bdm_s = pb.sbm("bdm_s", 128, 32 * 128, BF16, side="right")
    uts_s = pb.sbm("uts_s", 128, 16 * 128, BF16, side="right")
    yts = pb.sbm("yts", 128, 512, BF16, side="right")
    wsup = pb.sbm("wsup", 128, 16 * D, BF16, side="right")
    of_s = pb.sbm("of_s", 128, 4 * D, F32, side="right")

    def ldbdm(e):
        for i in range(32):
            pb.dma(e, bdm_s[:, i * 128:(i + 1) * 128], bdm[i, :, :], "g")
    pb.step(gpsimd=ldbdm)

    hts = v_b
    for hf in range(4):
        tof = hf * 512
        for s in range(8):
            def ldu(e, s=s):
                for m in range(16):
                    pb.dma(e, uts_s[:, m * 128:(m + 1) * 128], uts[s * 16 + m, :, :], "g")
            pb.step(gpsimd=ldu)

            hh = s // 2
            for gl in range(4):
                g8 = (s % 2) * 4 + gl
                idx = hh * 8 + g8

                def ymm(e, idx=idx, g8=g8, tof=tof):
                    e.matmul(P[0][:, :], bdm_s[:, idx * 128:(idx + 1) * 128],
                             x1T[:, g8 * T + tof: g8 * T + tof + 512],
                             start=True, stop=True)
                pb.step(tensor=ymm)

                def yev(e):
                    e.tensor_copy(yts[:, 0:512], P[0][:, :])
                pb.step(vector=yev)

                def zmm(e, gl=gl):
                    for mg in range(4):
                        m = gl * 4 + mg
                        e.matmul(P[2 + mg][:, :], uts_s[:, m * 128:(m + 1) * 128],
                                 yts[:, 0:512], start=True, stop=True)
                pb.step(tensor=zmm)

                def gel(e, gl=gl):
                    for mg in range(4):
                        m = gl * 4 + mg
                        e.activation(hts[:, m * 512: (m + 1) * 512],
                                     P[2 + mg][:, :], AF.Gelu)
                pb.step(scalar=gel)

            def ldw0(e, s=s):
                for fb in range(16):
                    pb.dma(e, wsup[:, fb * D:(fb + 1) * D],
                           wcT[(s * 16 + fb) * 128:(s * 16 + fb + 1) * 128, :], "g")
            pb.step(gpsimd=ldw0)

            def mk_cpmm(q8):
                bp = 4 + (q8 % 2) * 2 if False else (4 if q8 % 2 == 0 else 6)
                def cpmm(e, q8=q8, bp=bp):
                    for fb in range(16):
                        for oc in range(2):
                            e.matmul(P[bp + oc][:, :] if bp + oc < 7 else P[0][:, :],
                                     hts[:, fb * 512 + q8 * 128:fb * 512 + (q8 + 1) * 128],
                                     wsup[:, fb * D + oc * 512: fb * D + (oc + 1) * 512],
                                     start=(fb == 0), stop=(fb == 15))
                return cpmm

            def mk_cpadd(q8, s):
                bp = 4 if q8 % 2 == 0 else 6
                def cpadd(e, q8=q8, s=s, bp=bp):
                    for oc in range(2):
                        psrc = P[bp + oc][:, :] if bp + oc < 7 else P[0][:, :]
                        dst = of_s[:, q8 * D + oc * 512: q8 * D + (oc + 1) * 512]
                        if s == 0:
                            e.tensor_copy(dst, psrc)
                        else:
                            e.tensor_tensor(dst, psrc, dst, ALU.add)
                return cpadd

            for q8 in range(4):
                pb.step(tensor=mk_cpmm(q8))
                if q8 >= 1:
                    pb.step(vector=mk_cpadd(q8 - 1, s))
            pb.step(vector=mk_cpadd(3, s))

        for q8 in range(4):
            qt = hf * 4 + q8

            def ldx1(e, qt=qt):
                pb.dma(e, xt_l[:, :], x1sp[qt * 128:(qt + 1) * 128, :], "h")
            pb.step(sync=ldx1)

            def ev2(e, q8=q8):
                e.scalar_tensor_tensor(scrA[:, 0:512], of_s[:, q8 * D:q8 * D + 512],
                                       1.0, xt_l[:, 0:512], ALU.mult, ALU.add,
                                       accum_out=acol[:, 0:1])
                e.drain()
                e.scalar_tensor_tensor(scrA[:, 512:1024], of_s[:, q8 * D + 512:(q8 + 1) * D],
                                       1.0, xt_l[:, 512:1024], ALU.mult, ALU.add,
                                       accum_out=acol[:, 1:2])
                e.drain()
                e.tensor_tensor(acol[:, 2:3], acol[:, 0:1], acol[:, 1:2], ALU.add)
            pb.step(vector=ev2)

            def sq2(e):
                e.activation(scrB[:, 0:1024], scrA[:, 0:1024], AF.Square,
                             accum_out=acol[:, 3:4])
            pb.step(scalar=sq2)

            def stats2(e):
                e.tensor_scalar(acol[:, 4:5], acol[:, 2:3], 1.0 / D, None, ALU.mult)
                e.drain()
                e.tensor_tensor(acol[:, 5:6], acol[:, 4:5], acol[:, 4:5], ALU.mult)
                e.drain()
                e.scalar_tensor_tensor(acol[:, 6:7], acol[:, 3:4], 1.0 / D, acol[:, 5:6],
                                       ALU.mult, ALU.subtract)
                e.drain()
                e.tensor_scalar(acol[:, 6:7], acol[:, 6:7], 1e-5, None, ALU.add)
            pb.step(vector=stats2)

            def rstd2(e):
                e.activation(acol[:, 7:8], acol[:, 6:7], AF.Ln)
                e.drain()
                e.activation(acol[:, 8:9], acol[:, 7:8], AF.Exp, scale=-0.5)
            pb.step(scalar=rstd2)

            def nmr2(e):
                e.scalar_tensor_tensor(acol[:, 9:10], acol[:, 4:5], -1.0, acol[:, 8:9],
                                       ALU.mult, ALU.mult)
            pb.step(vector=nmr2)

            def norm2(e):
                e.activation(xt_l[:, 0:1024], scrA[:, 0:1024], AF.Identity,
                             bias=acol[:, 9:10], scale=acol[:, 8:9])
            pb.step(scalar=norm2)

            def st2(e, qt=qt):
                pb.dma(e, xout[qt * 128:(qt + 1) * 128, :], xt_l[:, 0:1024], "h")
            pb.step(sync=st2)

    return _finishA(pb)


def _finishA(pb):
    hf_target = pb.c["h"]
    gf_target = pb.c["g"]

    def fence(e):
        if hf_target:
            e.wait_ge(pb._sems["h"], hf_target)
        if gf_target:
            e.wait_ge(pb._sems["g"], gf_target)
    pb.block.sync(fence)
    return pb.done()



FF = 4096
NE = 8
NH = 16

LIMITB = 99


def build_launchB():
    pb = PB()
    nc = pb.nc

    def din(name, shape):
        return nc.dram_tensor(name, list(shape), F32, kind="ExternalInput").ap()

    x2 = din("x2", (T, D))
    x2rT = din("x2rT", (NCORE, D, T))
    wcvT = din("wcvT", (8, 8))
    ldt_bc = din("ldt_bc", (128, 128))
    iwr_a = din("iwr_a", (128, 128)); wim_a = din("wim_a", (128, 128))
    Br_a = din("Br_a", (128, 128)); Bi_a = din("Bi_a", (128, 128))
    Pr_a = din("Pr_a", (128, 128)); Pi_a = din("Pi_a", (128, 128))
    Cr_a = din("Cr_a", (128, 128)); Ci_a = din("Ci_a", (128, 128))
    Dd_c = din("Dd_c", (128, 8))
    peT = din("peT", (D, T))
    Wqkv = din("Wqkv", (D, 3 * D))
    Wo1 = din("Wo1", (D, D))
    Wr = din("Wr", (D, D))
    ubT = din("ubT", (64, 16)); vbT = din("vbT", (64, 16))
    ln1g = din("ln1g", (1, D)); ln1b = din("ln1b", (1, D))
    ln2g = din("ln2g", (1, D)); ln2b = din("ln2b", (1, D))
    ln3g = din("ln3g", (1, D)); ln3b = din("ln3b", (1, D))
    Wff1 = din("Wff1", (D, FF)); bff1c = din("bff1c", (FF, 1))
    Wff2 = din("Wff2", (FF, D)); bff2r = din("bff2r", (1, D))
    Wg = din("Wg", (D, NE))
    We1 = din("We1", (NE, D, FF)); be1c = din("be1c", (NE * FF, 1))
    We2 = din("We2", (NE, FF, D)); be2m = din("be2m", (NE, D))
    Wout = din("Wout", (D, D))
    xo = nc.dram_tensor("xo", [T, D], F32, kind="ExternalOutput").ap()
    x3sp = nc.dram_tensor("x3sp", [T, D], F32).ap()
    h1sp = nc.dram_tensor("h1sp", [T, D], F32).ap()
    h2sp = nc.dram_tensor("h2sp", [T, D], F32).ap()
    rTsp = nc.dram_tensor("rTsp", [D, T], F32).ap()
    kTsp = nc.dram_tensor("kTsp", [D, T], BF16).ap()
    sc0 = nc.dram_tensor("sc0", [T * 2049 + 128], BF16).ap()
    sc1 = nc.dram_tensor("sc1", [T * 2049 + 128], BF16).ap()

    ident_b = pb.sb("ident_b", 128, 128, BF16)
    ident_f = pb.sb("ident_f", 128, 128, F32)
    ones_b = pb.sb("ones_b", 128, 128, BF16)
    ones_f = pb.sb("ones_f", 128, 128, F32)
    scrA = pb.sb("scrA", 128, T, F32)
    xt_l = pb.sb("xt_l", 128, D, F32)
    xbf = pb.sb("xbf", 128, D, BF16)
    acol = pb.sb("acol", 128, 16, F32)
    rowb = pb.sb("rowb", 1, 256, F32)
    ubc = pb.sb("ubc", 64, 16, F32)
    vbc = pb.sb("vbc", 64, 16, F32)
    mskT = pb.sb("mskT", 8, T, BF16)
    msks = pb.sb("msks", 128, 8 * NQT, F32)
    gate_c = pb.sb("gate_c", 128, NQT, F32)
    wgsb = pb.sb("wgsb", 128, 8 * 8, BF16)
    be2s = pb.sb("be2s", 8, D, BF16)
    wcvs = pb.sb("wcvs", 8, 8, F32)
    ksel = pb.sb("ksel", 128, 64, F32)
    kblk = pb.sb("kblk", 128, 80, F32)
    ebuf = pb.sb("ebuf", 128, T, BF16)
    bdr = pb.sb("bdr", 128, T, BF16)
    bdsh = pb.sb("bdsh", 128, T, BF16)
    etil = pb.sb("etil", 128, 1024, BF16)
    dmsk = pb.sb("dmsk", 128, 128, F32)
    bcol = pb.sb("bcol", 128, 8, F32)
    pebuf = pb.sb("pebuf", 128, 1024, BF16)

    x3T = pb.sbm("x3T", 128, NCB * T, BF16, side="right")
    accc = pb.sbm("accc", 128, NCB * T, F32, side="right")
    xjt = pb.sbm("xjt", 128, 2 * T, F32, side="right")
    s4a = pb.sbm("s4a", 128, 128, F32, side="right")
    s4b = pb.sbm("s4b", 128, 128, F32, side="right")
    s4c = pb.sbm("s4c", 128, 128, F32, side="right")
    s4d = pb.sbm("s4d", 128, 128, F32, side="right")
    s4e = pb.sbm("s4e", 128, 128, F32, side="right")
    s4f = pb.sbm("s4f", 128, 320, F32, side="right")

    P = [pb.ps(f"P{i}", 512) for i in range(7)]
    PT = pb.ps("PT", 1024, BF16)

    S = pb._sems
    pb.begin()

    def b0g(e):
        e.memset(ones_b[:, :], 1.0)
        e.memset(ones_f[:, :], 1.0)
        e.drain()
        e.affine_select(ident_b[:, :], ones_b[:, :], [[-1, 128]], ALU.is_equal, 0.0,
                        channel_multiplier=1)
        e.drain()
        e.affine_select(ident_f[:, :], ones_f[:, :], [[-1, 128]], ALU.is_equal, 0.0,
                        channel_multiplier=1)
        e.drain()
        e.memset(dmsk[:, :], 0.0)
        e.drain()
        e.affine_select(dmsk[:, :], dmsk[:, :], [[-1, 128]], ALU.is_ge, -1e30,
                        channel_multiplier=1)

    def b0s(e):
        pb.dma(e, wcvs[:, :], wcvT, "h")
        pb.dma(e, ubc[:, :], ubT, "h")
        pb.dma(e, vbc[:, :], vbT, "h")
        pb.dma(e, s4a[:, :], ldt_bc, "h")
        pb.dma(e, kblk[:, 0:8], Dd_c, "h")

    def b0g2(e):
        pb.dma(e, be2s[:, :], be2m, "g")
        for cb in range(NCB):
            pb.dma(e, wgsb[:, cb * 8:(cb + 1) * 8], Wg[cb * 128:(cb + 1) * 128, :], "g")

    pb.step(gpsimd=b0g, sync=b0s)
    pb.step(gpsimd=b0g2)

    def b0v(e):
        e.tensor_scalar(ubc[:, :], ubc[:, :], 0.125, None, ALU.mult)
        e.drain()
        e.tensor_scalar(vbc[:, :], vbc[:, :], 0.125, None, ALU.mult)
    pb.step(vector=b0v)

    Z, W2, IRW = _s4_consts()

    def b1s(e):
        e.activation(s4b[:, :], s4a[:, :], AF.Exp)
    pb.step(scalar=b1s)

    def b1sy(e):
        pb.dma(e, s4a[:, :], iwr_a, "h")
    pb.step(sync=b1sy)

    def b1s2(e):
        e.activation(s4c[:, :], s4a[:, :], AF.Exp)
    pb.step(scalar=b1s2)

    def b1sy2(e):
        pb.dma(e, s4a[:, :], wim_a, "h")
    pb.step(sync=b1sy2)

    def b1v(e):
        e.tensor_tensor(s4c[:, :], s4c[:, :], s4b[:, :], ALU.mult)
        e.drain()
        e.tensor_scalar(s4c[:, :], s4c[:, :], -1.0, None, ALU.mult)
        e.drain()
        e.tensor_tensor(s4d[:, :], s4a[:, :], s4b[:, :], ALU.mult)
    pb.step(vector=b1v)

    rcols = {}
    colidx = [0]

    def alloc_col():
        c = colidx[0]; colidx[0] += 1
        return c

    for ab, (Xr_d, Xi_d, Yr_d, Yi_d, conjY) in enumerate((
            (Br_a, Bi_a, Cr_a, Ci_a, False),
            (Br_a, Bi_a, Pr_a, Pi_a, True),
            (Pr_a, Pi_a, Cr_a, Ci_a, False),
            (Pr_a, Pi_a, Pr_a, Pi_a, True))):
        def ldv(e, Xr_d=Xr_d, Xi_d=Xi_d, Yr_d=Yr_d, Yi_d=Yi_d):
            pb.dma(e, s4a[:, :], Xr_d, "h")
            pb.dma(e, xjt[:, 1664:1792], Xi_d, "h")
            pb.dma(e, s4e[:, :], Yr_d, "h")
            pb.dma(e, xjt[:, 0:128], Yi_d, "h")
        pb.step(sync=ldv)

        sgn = -1.0 if conjY else 1.0

        def vab(e, sgn=sgn):
            e.tensor_tensor(xjt[:, 128:256], s4a[:, :], s4e[:, :], ALU.mult)
            e.drain()
            e.tensor_tensor(xjt[:, 256:384], xjt[:, 1664:1792], xjt[:, 0:128], ALU.mult)
            e.drain()
            e.tensor_scalar(xjt[:, 256:384], xjt[:, 256:384], sgn, None, ALU.mult)
            e.drain()
            e.tensor_tensor(xjt[:, 384:512], xjt[:, 128:256], xjt[:, 256:384],
                            ALU.subtract)
            e.drain()
            e.tensor_tensor(xjt[:, 512:640], s4a[:, :], xjt[:, 0:128], ALU.mult)
            e.drain()
            e.tensor_scalar(xjt[:, 512:640], xjt[:, 512:640], sgn, None, ALU.mult)
            e.drain()
            e.tensor_tensor(xjt[:, 640:768], xjt[:, 1664:1792], s4e[:, :], ALU.mult)
            e.drain()
            e.tensor_tensor(xjt[:, 512:640], xjt[:, 512:640], xjt[:, 640:768],
                            ALU.add)
        pb.step(vector=vab)

        for m in range(5):
            zr, zi = Z[m]
            cr_ = alloc_col(); ci_ = alloc_col()
            rcols[(ab, m)] = (cr_, ci_)

            def cch(e, zr=zr, zi=zi, cr_=cr_, ci_=ci_):
                e.tensor_scalar(xjt[:, 768:896], s4c[:, :], -1.0, zr, ALU.mult, ALU.add)
                e.drain()
                e.tensor_scalar(xjt[:, 896:1024], s4d[:, :], -1.0, zi, ALU.mult, ALU.add)
                e.drain()
                e.tensor_tensor(xjt[:, 1024:1152], xjt[:, 768:896], xjt[:, 768:896], ALU.mult)
                e.drain()
                e.tensor_tensor(xjt[:, 1152:1280], xjt[:, 896:1024], xjt[:, 896:1024], ALU.mult)
                e.drain()
                e.tensor_tensor(xjt[:, 1024:1152], xjt[:, 1024:1152], xjt[:, 1152:1280], ALU.add)
                e.drain()
                e.reciprocal(xjt[:, 1152:1280], xjt[:, 1024:1152])
                e.drain()
                e.tensor_tensor(xjt[:, 1280:1408], xjt[:, 384:512], xjt[:, 768:896], ALU.mult)
                e.drain()
                e.tensor_tensor(xjt[:, 1408:1536], xjt[:, 512:640], xjt[:, 896:1024], ALU.mult)
                e.drain()
                e.tensor_tensor(xjt[:, 1280:1408], xjt[:, 1280:1408], xjt[:, 1408:1536], ALU.add)
                e.drain()
                e.tensor_tensor(xjt[:, 1408:1536], xjt[:, 512:640], xjt[:, 768:896], ALU.mult)
                e.drain()
                e.tensor_tensor(xjt[:, 1536:1664], xjt[:, 384:512], xjt[:, 896:1024], ALU.mult)
                e.drain()
                e.tensor_tensor(xjt[:, 1408:1536], xjt[:, 1408:1536], xjt[:, 1536:1664], ALU.subtract)
                e.drain()
                e.tensor_tensor(xjt[:, 1280:1408], xjt[:, 1280:1408], xjt[:, 1152:1280], ALU.mult)
                e.drain()
                e.tensor_tensor(xjt[:, 1408:1536], xjt[:, 1408:1536], xjt[:, 1152:1280], ALU.mult)
                e.drain()
                e.reduce_sum(s4f[:, cr_ * 8:(cr_ + 1) * 8],
                             xjt[:, 1280:1408].rearrange("p (b n) -> p b n", n=16), AX.X)
                e.drain()
                e.reduce_sum(s4f[:, ci_ * 8:(ci_ + 1) * 8],
                             xjt[:, 1408:1536].rearrange("p (b n) -> p b n", n=16), AX.X)
            pb.step(vector=cch)

    def rdt(e):
        dtb = s4b[:, 0:128:16]
        for idx in range(40):
            e.tensor_tensor(s4f[:, idx * 8:(idx + 1) * 8],
                            s4f[:, idx * 8:(idx + 1) * 8], dtb, ALU.mult)
            e.drain()
    pb.step(vector=rdt)

    def kfm(e):
        e.memset(kblk[:, 8:72], 0.0)
        e.drain()
        for m in range(5):
            c00r, c00i = rcols[(0, m)]; c01r, c01i = rcols[(1, m)]
            c10r, c10i = rcols[(2, m)]; c11r, c11i = rcols[(3, m)]
            def col(c): return s4f[:, c * 8:(c + 1) * 8]
            e.tensor_scalar(s4a[:, 0:8], col(c11r), 1.0, None, ALU.add)
            e.drain()
            e.tensor_tensor(s4a[:, 8:16], col(c01r), col(c10r), ALU.mult)
            e.drain()
            e.tensor_tensor(s4a[:, 16:24], col(c01i), col(c10i), ALU.mult)
            e.drain()
            e.tensor_tensor(s4a[:, 8:16], s4a[:, 8:16], s4a[:, 16:24], ALU.subtract)
            e.drain()
            e.tensor_tensor(s4a[:, 24:32], col(c01r), col(c10i), ALU.mult)
            e.drain()
            e.tensor_tensor(s4a[:, 32:40], col(c01i), col(c10r), ALU.mult)
            e.drain()
            e.tensor_tensor(s4a[:, 24:32], s4a[:, 24:32], s4a[:, 32:40], ALU.add)
            e.drain()
            e.tensor_tensor(s4a[:, 40:48], s4a[:, 0:8], s4a[:, 0:8], ALU.mult)
            e.drain()
            e.tensor_tensor(s4a[:, 48:56], col(c11i), col(c11i), ALU.mult)
            e.drain()
            e.tensor_tensor(s4a[:, 40:48], s4a[:, 40:48], s4a[:, 48:56], ALU.add)
            e.drain()
            e.reciprocal(s4a[:, 48:56], s4a[:, 40:48])
            e.drain()
            e.tensor_tensor(s4a[:, 56:64], s4a[:, 8:16], s4a[:, 0:8], ALU.mult)
            e.drain()
            e.tensor_tensor(s4a[:, 64:72], s4a[:, 24:32], col(c11i), ALU.mult)
            e.drain()
            e.tensor_tensor(s4a[:, 56:64], s4a[:, 56:64], s4a[:, 64:72], ALU.add)
            e.drain()
            e.tensor_tensor(s4a[:, 56:64], s4a[:, 56:64], s4a[:, 48:56], ALU.mult)
            e.drain()
            e.tensor_tensor(s4a[:, 64:72], s4a[:, 24:32], s4a[:, 0:8], ALU.mult)
            e.drain()
            e.tensor_tensor(s4a[:, 72:80], s4a[:, 8:16], col(c11i), ALU.mult)
            e.drain()
            e.tensor_tensor(s4a[:, 64:72], s4a[:, 64:72], s4a[:, 72:80], ALU.subtract)
            e.drain()
            e.tensor_tensor(s4a[:, 64:72], s4a[:, 64:72], s4a[:, 48:56], ALU.mult)
            e.drain()
            e.tensor_tensor(s4a[:, 80:88], col(c00r), s4a[:, 56:64], ALU.subtract)
            e.drain()
            e.tensor_tensor(s4a[:, 88:96], col(c00i), s4a[:, 64:72], ALU.subtract)
            e.drain()
            w2r, w2i = W2[m]
            e.tensor_scalar(s4a[:, 96:104], s4a[:, 80:88], w2r, None, ALU.mult)
            e.drain()
            e.tensor_scalar(s4a[:, 104:112], s4a[:, 88:96], w2i, None, ALU.mult)
            e.drain()
            e.tensor_tensor(s4a[:, 96:104], s4a[:, 96:104], s4a[:, 104:112], ALU.subtract)
            e.drain()
            e.tensor_scalar(s4a[:, 104:112], s4a[:, 80:88], w2i, None, ALU.mult)
            e.drain()
            e.tensor_scalar(s4a[:, 112:120], s4a[:, 88:96], w2r, None, ALU.mult)
            e.drain()
            e.tensor_tensor(s4a[:, 104:112], s4a[:, 104:112], s4a[:, 112:120], ALU.add)
            e.drain()
            for l in range(8):
                wr_, wi_ = IRW[m][l]
                if wr_ != 0.0:
                    e.scalar_tensor_tensor(kblk[:, 8 + l * 8:16 + l * 8],
                                           s4a[:, 96:104], wr_, kblk[:, 8 + l * 8:16 + l * 8],
                                           ALU.mult, ALU.add)
                    e.drain()
                if wi_ != 0.0:
                    e.scalar_tensor_tensor(kblk[:, 8 + l * 8:16 + l * 8],
                                           s4a[:, 104:112], wi_, kblk[:, 8 + l * 8:16 + l * 8],
                                           ALU.mult, ALU.add)
                    e.drain()
    pb.step(vector=kfm)

    for cb in range(NCB):
        def ktr(e, cb=cb):
            for l in range(8):
                e.tensor_copy(s4a[:, 120 + l:121 + l], kblk[:, 8 + l * 8 + cb:9 + l * 8 + cb])
                e.drain()
            e.nop()
        pb.step(vector=ktr)

        def ktp(e):
            e.transpose(P[0][0:8, 0:128], s4a[:, 120:128], ident_f[:, :])
        pb.step(tensor=ktp)

        def kev(e):
            e.tensor_copy(s4b[0:8, 0:128], P[0][0:8, 0:128])
        pb.step(vector=kev)

        def ksl(e):
            e.matmul(P[1][:, 0:8], s4b[0:8, 0:128], wcvs[0:8, 0:8], start=True, stop=True)
        pb.step(tensor=ksl)

        def kse(e, cb=cb):
            e.tensor_copy(ksel[:, cb * 8:(cb + 1) * 8], P[1][:, 0:8])
        pb.step(vector=kse)

    for nm in ("s4f", "s4e", "s4d", "s4c", "s4b", "s4a"):
        pb.free(nm)

    if LIMITB < 2:
        return _finishB(pb)

    for cb in range(NCB):
        def cinit(e, cb=cb):
            pb.dma(e, xjt[:, 0:T], x2rT[0, cb * 128:(cb + 1) * 128, :], "h")
        pb.step(sync=cinit)

        def cacc0(e, cb=cb):
            e.tensor_scalar(accc[:, cb * T:(cb + 1) * T], xjt[:, 0:T],
                            kblk[:, cb:cb + 1], None, ALU.mult)
            e.drain()
            e.scalar_tensor_tensor(accc[:, cb * T:(cb + 1) * T], xjt[:, 0:T],
                                   ksel[:, cb * 8:cb * 8 + 1],
                                   accc[:, cb * T:(cb + 1) * T], ALU.mult, ALU.add)
        pb.step(vector=cacc0)

        for j in range(1, 8):
            par = j % 2

            def cld(e, j=j, cb=cb, par=par):
                pb.dma(e, xjt[:, par * T:(par + 1) * T],
                       x2rT[j, cb * 128:(cb + 1) * 128, :], "h")
            pb.step(sync=cld)

            def cac(e, j=j, cb=cb, par=par):
                e.scalar_tensor_tensor(accc[:, cb * T:(cb + 1) * T],
                                       xjt[:, par * T:(par + 1) * T],
                                       ksel[:, cb * 8 + j:cb * 8 + j + 1],
                                       accc[:, cb * T:(cb + 1) * T], ALU.mult, ALU.add)
            pb.step(vector=cac)

    for cb in range(NCB):
        def sil(e, cb=cb):
            e.activation(accc[:, cb * T:(cb + 1) * T], accc[:, cb * T:(cb + 1) * T],
                         AF.Silu)
        pb.step(scalar=sil)

    for qt in range(NQT):
        def ldx2(e, qt=qt):
            pb.dma(e, xt_l[:, :], x2[qt * 128:(qt + 1) * 128, :], "h")
        pb.step(sync=ldx2)

        def tr3a(e, qt=qt):
            for cb in range(4):
                e.transpose(P[4 + cb // 2][:, (cb % 2) * 128:(cb % 2) * 128 + 128],
                            accc[:, cb * T + qt * 128: cb * T + (qt + 1) * 128],
                            ident_f[:, :])
        pb.step(tensor=tr3a)

        def ev3a(e, qt=qt):
            for cb in range(4):
                e.scalar_tensor_tensor(scrA[:, cb * 128:(cb + 1) * 128],
                                       P[4 + cb // 2][:, (cb % 2) * 128:(cb % 2) * 128 + 128],
                                       1.0, xt_l[:, cb * 128:(cb + 1) * 128],
                                       ALU.mult, ALU.add, accum_out=acol[:, cb:cb + 1])
                e.drain()
        pb.step(vector=ev3a)

        def tr3b(e, qt=qt):
            for cb in range(4, 8):
                e.transpose(P[4 + (cb - 4) // 2][:, ((cb - 4) % 2) * 128:((cb - 4) % 2) * 128 + 128],
                            accc[:, cb * T + qt * 128: cb * T + (qt + 1) * 128],
                            ident_f[:, :])
        pb.step(tensor=tr3b)

        def ev3b(e, qt=qt):
            for cb in range(4, 8):
                e.scalar_tensor_tensor(scrA[:, cb * 128:(cb + 1) * 128],
                                       P[4 + (cb - 4) // 2][:, ((cb - 4) % 2) * 128:((cb - 4) % 2) * 128 + 128],
                                       1.0, xt_l[:, cb * 128:(cb + 1) * 128],
                                       ALU.mult, ALU.add, accum_out=acol[:, cb:cb + 1])
                e.drain()
            e.reduce_sum(acol[:, 8:9], acol[:, 0:8], AX.X)
        pb.step(vector=ev3b)

        _ln_emit(pb, scrA, acol, xt_l, xbf, None, None)

        def sp3(e, qt=qt):
            pb.dma(e, x3sp[qt * 128:(qt + 1) * 128, :], xt_l[:, 0:D], "h")
        pb.step(sync=sp3)

        def t3T(e):
            for cb in range(NCB):
                e.transpose(PT[:, cb * 128:(cb + 1) * 128],
                            xbf[:, cb * 128:(cb + 1) * 128], ident_b[:, :])
        pb.step(tensor=t3T)

        def t3Te(e, qt=qt):
            for cb in range(NCB):
                e.tensor_copy(x3T[:, cb * T + qt * 128: cb * T + (qt + 1) * 128],
                              PT[:, cb * 128:(cb + 1) * 128])
        pb.step(vector=t3Te)

    pb.free("xjt")
    pb.free("accc")

    if LIMITB < 3:
        return _finishB(pb)

    qT = pb.sbm("qT", 128, NCB * T, BF16, side="right")
    v_b = pb.sbm("v_b", 128, NQT * D, BF16, side="right")
    wts = pb.sbm("wts", 128, NCB * D, BF16, side="right")

    def ldw_sec(off):
        def f(e):
            for cb in range(NCB):
                pb.dma(e, wts[:, cb * D:(cb + 1) * D],
                       Wqkv[cb * 128:(cb + 1) * 128, off:off + D], "g")
        return f

    pb.step(gpsimd=ldw_sec(0))
    for dblk in range(NCB):
        def qmm(e, dblk=dblk):
            for ch in range(4):
                for r in range(NCB):
                    e.matmul(P[ch][:, :],
                             wts[:, r * D + dblk * 128: r * D + (dblk + 1) * 128],
                             x3T[:, r * T + ch * 512: r * T + (ch + 1) * 512],
                             start=(r == 0), stop=(r == NCB - 1))
        pb.step(tensor=qmm)

        def qev(e, dblk=dblk):
            for ch in range(4):
                e.tensor_scalar(qT[:, dblk * T + ch * 512: dblk * T + (ch + 1) * 512],
                                P[ch][:, :], 0.125, None, ALU.mult)
        pb.step(vector=qev)

    pb.step(gpsimd=ldw_sec(D))
    for dblk in range(NCB):
        def kmm(e, dblk=dblk):
            for ch in range(4):
                for r in range(NCB):
                    e.matmul(P[ch][:, :],
                             wts[:, r * D + dblk * 128: r * D + (dblk + 1) * 128],
                             x3T[:, r * T + ch * 512: r * T + (ch + 1) * 512],
                             start=(r == 0), stop=(r == NCB - 1))
        pb.step(tensor=kmm)

        def kev(e, dblk=dblk):
            for ch in range(4):
                e.tensor_copy(ebuf[:, ch * 512:(ch + 1) * 512], P[ch][:, :])
                e.drain()
        pb.step(vector=kev)

        def ksp(e, dblk=dblk):
            pb.dma(e, kTsp[dblk * 128:(dblk + 1) * 128, :], ebuf[:, 0:T], "h")
        pb.step(sync=ksp)

    pb.step(gpsimd=ldw_sec(2 * D))
    for qp in range(NQT // 2):
        def vmm(e, qp=qp):
            for half in range(2):
                qt = qp * 2 + half
                for oc in range(2):
                    for r in range(NCB):
                        e.matmul(P[half * 2 + oc][:, :],
                                 x3T[:, r * T + qt * 128: r * T + (qt + 1) * 128],
                                 wts[:, r * D + oc * 512: r * D + (oc + 1) * 512],
                                 start=(r == 0), stop=(r == NCB - 1))
        pb.step(tensor=vmm)

        def vev(e, qp=qp):
            for half in range(2):
                qt = qp * 2 + half
                for oc in range(2):
                    e.tensor_copy(v_b[:, qt * D + oc * 512: qt * D + (oc + 1) * 512],
                                  P[half * 2 + oc][:, :])
        pb.step(vector=vev)

    def ldwr(e):
        for cb in range(NCB):
            pb.dma(e, wts[:, cb * D:(cb + 1) * D], Wr[cb * 128:(cb + 1) * 128, :], "g")
    pb.step(gpsimd=ldwr)

    for ch in range(4):
        def peload2(e, ch=ch):
            for cb in range(4):
                pb.dma(e, ebuf[:, cb * 512:(cb + 1) * 512],
                       peT[cb * 128:(cb + 1) * 128, ch * 512:(ch + 1) * 512], "g")
            for cb in range(4, 8):
                pb.dma(e, bdr[:, (cb - 4) * 512:(cb - 3) * 512],
                       peT[cb * 128:(cb + 1) * 128, ch * 512:(ch + 1) * 512], "g")
        pb.step(gpsimd=peload2)

        def rmm(e, ch=ch):
            for dblk in range(4):
                for r in range(NCB):
                    mv = ebuf[:, r * 512:(r + 1) * 512] if r < 4 else bdr[:, (r - 4) * 512:(r - 3) * 512]
                    e.matmul(P[dblk][:, :],
                             wts[:, r * D + dblk * 128: r * D + (dblk + 1) * 128],
                             mv, start=(r == 0), stop=(r == NCB - 1))
        pb.step(tensor=rmm)

        def rev(e, ch=ch):
            for dblk in range(4):
                e.tensor_copy(scrA[:, dblk * 512:(dblk + 1) * 512], P[dblk][:, :])
                e.drain()
        pb.step(vector=rev)

        def rst(e, ch=ch):
            for dblk in range(4):
                pb.dma(e, rTsp[dblk * 128:(dblk + 1) * 128, ch * 512:(ch + 1) * 512],
                       scrA[:, dblk * 512:(dblk + 1) * 512], "h")
        pb.step(sync=rst)

        def rmm2(e, ch=ch):
            for dblk in range(4, 8):
                for r in range(NCB):
                    mv = ebuf[:, r * 512:(r + 1) * 512] if r < 4 else bdr[:, (r - 4) * 512:(r - 3) * 512]
                    e.matmul(P[dblk - 4][:, :],
                             wts[:, r * D + dblk * 128: r * D + (dblk + 1) * 128],
                             mv, start=(r == 0), stop=(r == NCB - 1))
        pb.step(tensor=rmm2)

        def rev2(e, ch=ch):
            for dblk in range(4, 8):
                e.tensor_copy(scrA[:, (dblk - 4) * 512:(dblk - 3) * 512], P[dblk - 4][:, :])
                e.drain()
        pb.step(vector=rev2)

        def rst2(e, ch=ch):
            for dblk in range(4, 8):
                pb.dma(e, rTsp[dblk * 128:(dblk + 1) * 128, ch * 512:(ch + 1) * 512],
                       scrA[:, (dblk - 4) * 512:(dblk - 3) * 512], "h")
        pb.step(sync=rst2)

    pb.free("wts")

    if LIMITB < 4:
        return _finishB(pb)

    def zf(e):
        e.memset(ebuf[:, :], 0.0)
        e.drain()
        for i in range(17):
            n = min(T * 2049 + 128 - i * 256 * 1024, 256 * 1024)
            if n <= 0:
                break
            pb.dma(e, sc0[i * 256 * 1024: i * 256 * 1024 + n].rearrange("(p f) -> p f", p=128),
                   ebuf[:, 0:n // 128], "g")
            pb.dma(e, sc1[i * 256 * 1024: i * 256 * 1024 + n].rearrange("(p f) -> p f", p=128),
                   ebuf[:, 0:n // 128], "g")
    pb.step(gpsimd=zf)

    att = pb.sbm("att", 64, 3 * T, BF16, side="right")
    rbuf = pb.sbm("rbuf", 64, 2 * T, BF16, side="right")

    scs = (sc0, sc1)
    for h in range(NH):
        dblk = h // 2
        rb = (h % 2) * 64

        def cph(e, h=h):
            pb.dma(e, att[:, 0:T], kTsp[h * 64:(h + 1) * 64, :], "h")
        pb.step(sync=cph)

        def cpq(e, h=h, dblk=dblk, rb=rb):
            e.tensor_scalar(att[:, T:2 * T], qT[rb:rb + 64, dblk * T:(dblk + 1) * T],
                            ubc[:, h:h + 1], None, ALU.add)
            e.tensor_scalar(att[:, 2 * T:3 * T], qT[rb:rb + 64, dblk * T:(dblk + 1) * T],
                            vbc[:, h:h + 1], None, ALU.add)
        pb.step(vector=cpq)

        sc = scs[h % 2]
        for a in range(NQT):
            cw = 128 * (a + 1)
            m_lo = T - cw
            nch = (cw + 511) // 512

            def rld(e, h=h, a=a, m_lo=m_lo, cw=cw):
                par = (a % 2) * T
                pb.dma(e, rbuf[:, par:par + cw],
                       rTsp[h * 64:(h + 1) * 64, m_lo:T], "g")
            pb.step(gpsimd=rld)

            def bmm(e, h=h, a=a, cw=cw, nch=nch):
                par = (a % 2) * T
                for bc in range(nch):
                    w = min(512, cw - bc * 512)
                    e.matmul(P[bc][:, 0:w],
                             att[:, 2 * T + a * 128: 2 * T + (a + 1) * 128],
                             rbuf[:, par + bc * 512: par + bc * 512 + w],
                             start=True, stop=True)
            pb.step(tensor=bmm)

            def bev(e, cw=cw, nch=nch):
                for bc in range(nch):
                    w = min(512, cw - bc * 512)
                    e.tensor_copy(bdr[:, bc * 512: bc * 512 + w], P[bc][:, 0:w])
            pb.step(vector=bev)

            def shw(e, h=h, a=a, m_lo=m_lo, cw=cw, sc=sc):
                pad = sc[0: T * 2049].rearrange("(i m) -> i m", m=2049)
                pb.dma(e, pad[a * 128:(a + 1) * 128, m_lo:T], bdr[:, 0:cw], "h")
            pb.step(sync=shw)

            def shr(e, h=h, a=a, cw=cw, sc=sc):
                shv = sc[2047: 2047 + T * T].rearrange("(i j) -> i j", j=T)
                pb.dma(e, bdsh[:, 0:cw], shv[a * 128:(a + 1) * 128, 0:cw], "h")
            pb.step(sync=shr)

            def amm(e, h=h, a=a, cw=cw, nch=nch):
                for bc in range(nch):
                    w = min(512, cw - bc * 512)
                    e.matmul(P[bc][:, 0:w],
                             att[:, T + a * 128: T + (a + 1) * 128],
                             att[:, bc * 512: bc * 512 + w],
                             start=True, stop=True)
            pb.step(tensor=amm)

            def sadd(e, cw=cw, nch=nch):
                for bc in range(nch):
                    w = min(512, cw - bc * 512)
                    e.tensor_tensor(scrA[:, bc * 512: bc * 512 + w], P[bc][:, 0:w],
                                    bdsh[:, bc * 512: bc * 512 + w], ALU.add)
            pb.step(vector=sadd)

            def dmask(e, a=a, cw=cw):
                e.tensor_tensor(scrA[:, cw - 128: cw], scrA[:, cw - 128: cw],
                                dmsk[:, :], ALU.add)
            pb.step(vector=dmask)

            def eexp(e, cw=cw):
                e.activation(ebuf[:, 0:cw], scrA[:, 0:cw], AF.Exp,
                             accum_out=acol[:, 0:1])
            pb.step(scalar=eexp)

            def esc(e, cw=cw):
                e.reciprocal(acol[:, 1:2], acol[:, 0:1])
                e.drain()
                e.tensor_scalar(ebuf[:, 0:cw], ebuf[:, 0:cw], acol[:, 1:2], None,
                                ALU.mult)
            pb.step(vector=esc)

            for kt0 in range(0, a + 1, 8):
                nb = min(8, a + 1 - kt0)

                def etp(e, kt0=kt0, nb=nb):
                    for i in range(nb):
                        e.transpose(PT[:, i * 128:(i + 1) * 128],
                                    ebuf[:, (kt0 + i) * 128:(kt0 + i + 1) * 128],
                                    ident_b[:, :])
                pb.step(tensor=etp)

                def ete(e, nb=nb):
                    e.tensor_copy(etil[:, 0:nb * 128], PT[:, 0:nb * 128])
                pb.step(vector=ete)

                def avm(e, kt0=kt0, nb=nb, h=h, a=a):
                    for i in range(nb):
                        kt = kt0 + i
                        e.matmul(P[6][0:64, 0:128],
                                 v_b[:, kt * D + h * 64: kt * D + (h + 1) * 64],
                                 etil[:, i * 128:(i + 1) * 128],
                                 start=(kt == 0), stop=(kt == a))
                pb.step(tensor=avm)

            def ctxe(e, h=h, a=a, dblk=dblk, rb=rb):
                e.tensor_copy(x3T[rb:rb + 64, dblk * T + a * 128: dblk * T + (a + 1) * 128],
                              P[6][0:64, 0:128])
            pb.step(vector=ctxe)

    ctxT = x3T

    if LIMITB < 5:
        return _finishB(pb)

    pb.free("rbuf")
    pb.free("att")
    pb.free("v_b")
    pb.free("qT")
    wts2 = pb.sbm("wts2", 128, NCB * D, BF16, side="right")
    hT = pb.sbm("hT", 128, NCB * T, BF16, side="right")
    accF = pb.sbm("accF", 128, NQT * D, F32, side="right")
    lnrow = pb.sbm("lnrow", 1, 2 * D, F32, side="right")
    gbG = pb.sbm("gbG", 128, D, F32, side="right")
    gbB = pb.sbm("gbB", 128, D, F32, side="right")

    def ldlp(e, gr, br):
        pb.dma(e, lnrow[0:1, 0:D], gr, "g")
        pb.dma(e, lnrow[0:1, D:2 * D], br, "g")

    def gb_build(tag):
        def t(e):
            for oc in range(2):
                e.matmul(P[0][:, :], ones_f[0:1, 0:128], lnrow[0:1, oc * 512:(oc + 1) * 512],
                         start=True, stop=True) if oc == 0 else None
            e.matmul(P[0][:, :], ones_f[0:1, 0:128], lnrow[0:1, 0:512], start=True, stop=True)
            e.matmul(P[1][:, :], ones_f[0:1, 0:128], lnrow[0:1, 512:1024], start=True, stop=True)
            e.matmul(P[2][:, :], ones_f[0:1, 0:128], lnrow[0:1, 1024:1536], start=True, stop=True)
            e.matmul(P[3][:, :], ones_f[0:1, 0:128], lnrow[0:1, 1536:2048], start=True, stop=True)
        def v(e):
            e.tensor_copy(gbG[:, 0:512], P[0][:, :])
            e.drain()
            e.tensor_copy(gbG[:, 512:1024], P[1][:, :])
            e.drain()
            e.tensor_copy(gbB[:, 0:512], P[2][:, :])
            e.drain()
            e.tensor_copy(gbB[:, 512:1024], P[3][:, :])
        return t, v

    def ldwo(e):
        for cb in range(NCB):
            pb.dma(e, wts2[:, cb * D:(cb + 1) * D], Wo1[cb * 128:(cb + 1) * 128, :], "g")
        ldlp(e, ln1g, ln1b)
    pb.step(gpsimd=ldwo)
    tfn, vfn = gb_build("ln1")
    pb.step(tensor=tfn)
    pb.step(vector=vfn)

    def wo_ln(spill, resid_dram, gb=True):
        for qt in range(NQT):
            def ldres(e, qt=qt):
                pb.dma(e, xt_l[:, :], resid_dram[qt * 128:(qt + 1) * 128, :], "h")
            pb.step(sync=ldres)

            def womm(e, qt=qt):
                for oc in range(2):
                    for r in range(NCB):
                        e.matmul(P[oc][:, :],
                                 ctxT[:, r * T + qt * 128: r * T + (qt + 1) * 128],
                                 wts2[:, r * D + oc * 512: r * D + (oc + 1) * 512],
                                 start=(r == 0), stop=(r == NCB - 1))
            pb.step(tensor=womm)

            def ev(e):
                e.scalar_tensor_tensor(scrA[:, 0:512], P[0][:, :], 1.0, xt_l[:, 0:512],
                                       ALU.mult, ALU.add, accum_out=acol[:, 6:7])
                e.drain()
                e.scalar_tensor_tensor(scrA[:, 512:1024], P[1][:, :], 1.0, xt_l[:, 512:1024],
                                       ALU.mult, ALU.add, accum_out=acol[:, 7:8])
                e.drain()
                e.tensor_tensor(acol[:, 8:9], acol[:, 6:7], acol[:, 7:8], ALU.add)
            pb.step(vector=ev)

            _ln_emit(pb, scrA, acol, xt_l, xbf, None, None)

            if gb:
                def gbap(e):
                    e.tensor_tensor(xt_l[:, 0:D], xt_l[:, 0:D], gbG[:, 0:D], ALU.mult)
                    e.drain()
                    e.tensor_tensor(xt_l[:, 0:D], xt_l[:, 0:D], gbB[:, 0:D], ALU.add)
                    e.drain()
                    e.tensor_copy(xbf[:, 0:D], xt_l[:, 0:D])
                pb.step(vector=gbap)

            def spl(e, qt=qt):
                pb.dma(e, spill[qt * 128:(qt + 1) * 128, :], xt_l[:, 0:D], "h")
            pb.step(sync=spl)

            def trh(e):
                for cb in range(NCB):
                    e.transpose(PT[:, cb * 128:(cb + 1) * 128],
                                xbf[:, cb * 128:(cb + 1) * 128], ident_b[:, :])
            pb.step(tensor=trh)

            def trhe(e, qt=qt):
                for cb in range(NCB):
                    e.tensor_copy(hT[:, cb * T + qt * 128: cb * T + (qt + 1) * 128],
                                  PT[:, cb * 128:(cb + 1) * 128])
            pb.step(vector=trhe)

    wo_ln(h1sp, x3sp, gb=True)

    if LIMITB < 6:
        return _finishB(pb)

    hidT = ctxT
    w2b = pb.sbm("w2b", 128, 2 * D, BF16, side="right")

    def emit_ffn(W1, W2d, b1_slice, nsup, e_idx, accF, init_mode):
        for s in range(nsup):
            def ldw1(e, s=s):
                for cb in range(NCB):
                    if e_idx is None:
                        pb.dma(e, wts2[:, cb * D:(cb + 1) * D],
                               W1[cb * 128:(cb + 1) * 128, s * 1024:(s + 1) * 1024], "g")
                    else:
                        pb.dma(e, wts2[:, cb * D:(cb + 1) * D],
                               W1[e_idx, cb * 128:(cb + 1) * 128, s * 1024:(s + 1) * 1024], "g")
            pb.step(gpsimd=ldw1)

            def bload(e, s=s):
                off0 = (0 if e_idx is None else e_idx * FF) + s * 1024
                for fb in range(8):
                    pb.dma(e, bcol[:, fb:fb + 1], b1_slice[off0 + fb * 128: off0 + (fb + 1) * 128, :], "h")
            pb.step(sync=bload)

            PAIRS = ((0, 1), (2, 3), (4, 5))

            def mk_f1mm(fb, hh, pi):
                def f1mm(e, fb=fb, hh=hh, pi=pi):
                    for ci in range(2):
                        ch = hh * 2 + ci
                        for r in range(NCB):
                            e.matmul(P[PAIRS[pi][ci]][:, :],
                                     wts2[:, r * D + fb * 128: r * D + (fb + 1) * 128],
                                     hT[:, r * T + ch * 512: r * T + (ch + 1) * 512],
                                     start=(r == 0), stop=(r == NCB - 1))
                return f1mm

            def mk_f1ev(fb, hh, pi):
                def f1ev(e, fb=fb, hh=hh, pi=pi):
                    for ci in range(2):
                        ch = hh * 2 + ci
                        e.activation(hidT[:, fb * T + ch * 512: fb * T + (ch + 1) * 512],
                                     P[PAIRS[pi][ci]][:, :], AF.Relu, bias=bcol[:, fb:fb + 1])
                return f1ev

            prev = None
            u = 0
            for fb in range(8):
                for hh in range(2):
                    pb.step(tensor=mk_f1mm(fb, hh, u % 3))
                    if prev is not None:
                        pb.step(scalar=mk_f1ev(*prev))
                    prev = (fb, hh, u % 3)
                    u += 1
            pb.step(scalar=mk_f1ev(*prev))

            def ldw2all(e, s=s):
                for fb in range(8):
                    rw = s * 1024 + fb * 128
                    if e_idx is None:
                        pb.dma(e, wts2[:, fb * D:(fb + 1) * D], W2d[rw:rw + 128, :], "g")
                    else:
                        pb.dma(e, wts2[:, fb * D:(fb + 1) * D], W2d[e_idx, rw:rw + 128, :], "g")
            pb.step(gpsimd=ldw2all)

            def mk_f2mm(qt, pi):
                def f2mm(e, qt=qt, pi=pi):
                    for oc in range(2):
                        for fb in range(8):
                            e.matmul(P[PAIRS[pi][oc]][:, :],
                                     hidT[:, fb * T + qt * 128: fb * T + (qt + 1) * 128],
                                     wts2[:, fb * D + oc * 512: fb * D + (oc + 1) * 512],
                                     start=(fb == 0), stop=(fb == 7))
                return f2mm

            def mk_f2ev(qt, pi):
                def f2ev(e, qt=qt, pi=pi):
                    for oc in range(2):
                        dst = accF[:, qt * D + oc * 512: qt * D + (oc + 1) * 512]
                        psrc = P[PAIRS[pi][oc]][:, :]
                        if init_mode == "add":
                            e.tensor_tensor(dst, psrc, dst, ALU.add)
                        else:
                            e.scalar_tensor_tensor(dst, psrc,
                                                   msks[:, qt * 8 + e_idx: qt * 8 + e_idx + 1],
                                                   dst, ALU.mult, ALU.add)
                return f2ev

            prev = None
            for qt in range(NQT):
                pb.step(tensor=mk_f2mm(qt, qt % 3))
                if prev is not None:
                    pb.step(vector=mk_f2ev(*prev))
                prev = (qt, qt % 3)
            pb.step(vector=mk_f2ev(*prev))

    def ldb2(e):
        pb.dma(e, lnrow[0:1, 0:D], bff2r, "h")
    pb.step(sync=ldb2)
    for qt in range(NQT):
        def b2mm(e, qt=qt):
            e.matmul(P[0][:, :], ones_f[0:1, 0:128], lnrow[0:1, 0:512], start=True, stop=True)
            e.matmul(P[1][:, :], ones_f[0:1, 0:128], lnrow[0:1, 512:1024], start=True, stop=True)
        pb.step(tensor=b2mm)

        def b2ev(e, qt=qt):
            e.tensor_copy(accF[:, qt * D: qt * D + 512], P[0][:, :])
            e.drain()
            e.tensor_copy(accF[:, qt * D + 512: (qt + 1) * D], P[1][:, :])
            e.drain()
        pb.step(vector=b2ev)
    emit_ffn(Wff1, Wff2, bff1c, 4, None, accF, "add")

    def ldlp2(e):
        pb.dma(e, lnrow[0:1, 0:D], ln2g, "h")
        pb.dma(e, lnrow[0:1, D:2 * D], ln2b, "h")
    pb.step(sync=ldlp2)
    tfn2, vfn2 = gb_build("ln2")
    pb.step(tensor=tfn2)
    pb.step(vector=vfn2)

    for qt in range(NQT):
        def ldh1(e, qt=qt):
            pb.dma(e, xt_l[:, :], h1sp[qt * 128:(qt + 1) * 128, :], "h")
        pb.step(sync=ldh1)

        def ev2(e, qt=qt):
            e.scalar_tensor_tensor(scrA[:, 0:512], accF[:, qt * D: qt * D + 512],
                                   1.0, xt_l[:, 0:512], ALU.mult, ALU.add,
                                   accum_out=acol[:, 6:7])
            e.drain()
            e.scalar_tensor_tensor(scrA[:, 512:1024], accF[:, qt * D + 512:(qt + 1) * D],
                                   1.0, xt_l[:, 512:1024], ALU.mult, ALU.add,
                                   accum_out=acol[:, 7:8])
            e.drain()
            e.tensor_tensor(acol[:, 8:9], acol[:, 6:7], acol[:, 7:8], ALU.add)
        pb.step(vector=ev2)

        _ln_emit(pb, scrA, acol, xt_l, xbf, None, None)

        def gbap2(e):
            e.tensor_tensor(xt_l[:, 0:D], xt_l[:, 0:D], gbG[:, 0:D], ALU.mult)
            e.drain()
            e.tensor_tensor(xt_l[:, 0:D], xt_l[:, 0:D], gbB[:, 0:D], ALU.add)
            e.drain()
            e.tensor_copy(xbf[:, 0:D], xt_l[:, 0:D])
        pb.step(vector=gbap2)

        def spl2(e, qt=qt):
            pb.dma(e, h2sp[qt * 128:(qt + 1) * 128, :], xt_l[:, 0:D], "h")
        pb.step(sync=spl2)

        def trh2(e):
            for cb in range(NCB):
                e.transpose(PT[:, cb * 128:(cb + 1) * 128],
                            xbf[:, cb * 128:(cb + 1) * 128], ident_b[:, :])
        pb.step(tensor=trh2)

        def trhe2(e, qt=qt):
            for cb in range(NCB):
                e.tensor_copy(hT[:, cb * T + qt * 128: cb * T + (qt + 1) * 128],
                              PT[:, cb * 128:(cb + 1) * 128])
                e.drain()
        pb.step(vector=trhe2)

    if LIMITB < 7:
        return _finishB(pb)

    mskb = pb.sbm("mskb", 128, 8, BF16, side="right")
    for qt in range(NQT):
        def gmm(e, qt=qt):
            for r in range(NCB):
                e.matmul(P[0][:, 0:8],
                         hT[:, r * T + qt * 128: r * T + (qt + 1) * 128],
                         wgsb[:, r * 8:(r + 1) * 8],
                         start=(r == 0), stop=(r == NCB - 1))
        pb.step(tensor=gmm)

        def gred(e, qt=qt):
            e.reduce_max(acol[:, 0:1], P[0][:, 0:8], AX.X)
            e.drain()
            e.tensor_scalar(acol[:, 1:2], acol[:, 0:1], -1.0, None, ALU.mult)
            e.drain()
            e.tensor_scalar(msks[:, qt * 8:(qt + 1) * 8], P[0][:, 0:8],
                            acol[:, 0:1], None, ALU.is_ge)
            e.drain()
            e.tensor_copy(mskb[:, 0:8], msks[:, qt * 8:(qt + 1) * 8])
            e.drain()
        pb.step(vector=gred)

        def gexp(e, qt=qt):
            e.activation(scrA[:, 0:8], P[0][:, 0:8], AF.Exp, bias=acol[:, 1:2],
                         accum_out=acol[:, 2:3])
        pb.step(scalar=gexp)

        def ginv(e, qt=qt):
            e.reciprocal(gate_c[:, qt:qt + 1], acol[:, 2:3])
        pb.step(vector=ginv)

        def gtp(e, qt=qt):
            e.transpose(PT[0:8, 0:128], mskb[:, 0:8], ident_b[:, :])
        pb.step(tensor=gtp)

        def gte(e, qt=qt):
            e.tensor_copy(mskT[0:8, qt * 128:(qt + 1) * 128], PT[0:8, 0:128])
        pb.step(vector=gte)

    for qt in range(NQT):
        def bemm(e, qt=qt):
            for oc in range(2):
                e.matmul(P[oc][:, :], mskT[0:8, qt * 128:(qt + 1) * 128],
                         be2s[0:8, oc * 512:(oc + 1) * 512], start=True, stop=True)
        pb.step(tensor=bemm)

        def beev(e, qt=qt):
            e.tensor_copy(accF[:, qt * D: qt * D + 512], P[0][:, :])
            e.drain()
            e.tensor_copy(accF[:, qt * D + 512:(qt + 1) * D], P[1][:, :])
            e.drain()
        pb.step(vector=beev)

    for ei in range(NE):
        emit_ffn(We1, We2, be1c, 4, ei, accF, "mask")

    def ldlp3(e):
        pb.dma(e, lnrow[0:1, 0:D], ln3g, "h")
        pb.dma(e, lnrow[0:1, D:2 * D], ln3b, "h")
    pb.step(sync=ldlp3)
    tfn3, vfn3 = gb_build("ln3")
    pb.step(tensor=tfn3)
    pb.step(vector=vfn3)

    for qt in range(NQT):
        def ldh2(e, qt=qt):
            pb.dma(e, xt_l[:, :], h2sp[qt * 128:(qt + 1) * 128, :], "h")
        pb.step(sync=ldh2)

        def ev3(e, qt=qt):
            e.tensor_scalar(accF[:, qt * D:(qt + 1) * D], accF[:, qt * D:(qt + 1) * D],
                            gate_c[:, qt:qt + 1], None, ALU.mult)
            e.drain()
            e.scalar_tensor_tensor(scrA[:, 0:D], accF[:, qt * D:(qt + 1) * D],
                                   1.0, xt_l[:, 0:D], ALU.mult, ALU.add,
                                   accum_out=acol[:, 8:9])
            e.drain()
        pb.step(vector=ev3)

        _ln_emit(pb, scrA, acol, xt_l, xbf, None, None)

        def gbap3(e):
            e.tensor_tensor(xt_l[:, 0:D], xt_l[:, 0:D], gbG[:, 0:D], ALU.mult)
            e.drain()
            e.tensor_tensor(xt_l[:, 0:D], xt_l[:, 0:D], gbB[:, 0:D], ALU.add)
            e.drain()
            e.tensor_copy(xbf[:, 0:D], xt_l[:, 0:D])
        pb.step(vector=gbap3)

        def trh3(e):
            for cb in range(NCB):
                e.transpose(PT[:, cb * 128:(cb + 1) * 128],
                            xbf[:, cb * 128:(cb + 1) * 128], ident_b[:, :])
        pb.step(tensor=trh3)

        def trhe3(e, qt=qt):
            for cb in range(NCB):
                e.tensor_copy(hT[:, cb * T + qt * 128: cb * T + (qt + 1) * 128],
                              PT[:, cb * 128:(cb + 1) * 128])
                e.drain()
        pb.step(vector=trhe3)

    def ldwout(e):
        for cb in range(NCB):
            pb.dma(e, wts2[:, cb * D:(cb + 1) * D], Wout[cb * 128:(cb + 1) * 128, :], "g")
    pb.step(gpsimd=ldwout)

    for qt in range(NQT):
        def ldx3(e, qt=qt):
            pb.dma(e, xt_l[:, :], x3sp[qt * 128:(qt + 1) * 128, :], "h")
        pb.step(sync=ldx3)

        def omm(e, qt=qt):
            for oc in range(2):
                for r in range(NCB):
                    e.matmul(P[oc][:, :],
                             hT[:, r * T + qt * 128: r * T + (qt + 1) * 128],
                             wts2[:, r * D + oc * 512: r * D + (oc + 1) * 512],
                             start=(r == 0), stop=(r == NCB - 1))
        pb.step(tensor=omm)

        def evo(e):
            e.scalar_tensor_tensor(scrA[:, 0:512], P[0][:, :], 1.0, xt_l[:, 0:512],
                                   ALU.mult, ALU.add, accum_out=acol[:, 6:7])
            e.drain()
            e.scalar_tensor_tensor(scrA[:, 512:1024], P[1][:, :], 1.0, xt_l[:, 512:1024],
                                   ALU.mult, ALU.add, accum_out=acol[:, 7:8])
            e.drain()
            e.tensor_tensor(acol[:, 8:9], acol[:, 6:7], acol[:, 7:8], ALU.add)
        pb.step(vector=evo)

        _ln_emit(pb, scrA, acol, xt_l, xbf, None, None)

        def sto(e, qt=qt):
            pb.dma(e, xo[qt * 128:(qt + 1) * 128, :], xt_l[:, 0:D], "h")
        pb.step(sync=sto)

    return _finishB(pb)


def _finishB(pb):
    hf = pb.c["h"]; gf = pb.c["g"]

    def fence(e):
        if hf:
            e.wait_ge(pb._sems["h"], hf)
        if gf:
            e.wait_ge(pb._sems["g"], gf)
    pb.block.sync(fence)
    return pb.done()


def _ln_emit(pb, xr_sb, acol, out_f32, out_bf, gcol, bcol):
    AFx = AF

    def sq(e):
        e.scalar_tensor_tensor(out_bf[:, 0:D], xr_sb[:, 0:D], 1.0, xr_sb[:, 0:D],
                               ALU.mult, ALU.mult, accum_out=acol[:, 9:10])
    pb.step(vector=sq)

    def stats(e):
        e.tensor_scalar(acol[:, 10:11], acol[:, 8:9], 1.0 / D, None, ALU.mult)
        e.drain()
        e.tensor_tensor(acol[:, 11:12], acol[:, 10:11], acol[:, 10:11], ALU.mult)
        e.drain()
        e.scalar_tensor_tensor(acol[:, 12:13], acol[:, 9:10], 1.0 / D, acol[:, 11:12],
                               ALU.mult, ALU.subtract)
        e.drain()
        e.tensor_scalar(acol[:, 12:13], acol[:, 12:13], 1e-5, None, ALU.add)
    pb.step(vector=stats)

    def rstd(e):
        e.activation(acol[:, 13:14], acol[:, 12:13], AF.Ln)
        e.drain()
        e.activation(acol[:, 14:15], acol[:, 13:14], AF.Exp, scale=-0.5)
    pb.step(scalar=rstd)

    def nmr(e):
        e.scalar_tensor_tensor(acol[:, 15:16], acol[:, 10:11], -1.0, acol[:, 14:15],
                               ALU.mult, ALU.mult)
    pb.step(vector=nmr)

    def norm(e):
        e.tensor_scalar(out_f32[:, 0:D], xr_sb[:, 0:D], acol[:, 14:15], acol[:, 15:16],
                        ALU.mult, ALU.add)
        e.tensor_scalar(out_bf[:, 0:D], xr_sb[:, 0:D], acol[:, 14:15], acol[:, 15:16],
                        ALU.mult, ALU.add)
    pb.step(vector=norm)


def _s4_consts():
    L = 8; M = 5
    om = np.exp(np.complex64(-2j * np.pi / L) * np.arange(M, dtype=np.complex64))
    om = om.astype(np.complex64)
    z = (2 * (1 - om) / (1 + om)).astype(np.complex64)
    w2 = (2 / (1 + om)).astype(np.complex64)
    Z = [(float(np.float32(z[m].real)), float(np.float32(z[m].imag))) for m in range(M)]
    W2 = [(float(np.float32(w2[m].real)), float(np.float32(w2[m].imag))) for m in range(M)]
    IRW = []
    for m in range(M):
        row = []
        for l in range(L):
            if m == 0:
                wr_, wi_ = 1.0 / L, 0.0
            elif m == 4:
                wr_, wi_ = ((-1.0) ** l) / L, 0.0
            else:
                wr_ = 2.0 * np.cos(2 * np.pi * m * l / L) / L
                wi_ = -2.0 * np.sin(2 * np.pi * m * l / L) / L
            row.append((float(np.float32(wr_)), float(np.float32(wi_))))
        IRW.append(row)
    return Z, W2, IRW


def prep_A(inputs, b):
    x = np.asarray(inputs["L0_attn_x"])[b]
    rm = np.asarray(inputs["L0_ffn_rail_maker"])
    up = np.asarray(inputs["L0_ffn_rail_up_proj"])
    bdm = np.zeros((32, 128, 128), np.float32)
    for hh in range(4):
        for g8 in range(8):
            for rr in range(8):
                r = g8 * 8 + rr
                bdm[hh * 8 + g8, 16 * rr:16 * rr + 16, 16 * rr:16 * rr + 16] = rm[hh, r]
    uts = np.zeros((128, 128, 128), np.float32)
    for s in range(8):
        hh = s // 2
        rbase = (s % 2) * 32
        for m in range(16):
            gl = m // 4
            gs = rbase + 8 * gl
            for pi in range(2):
                r = rbase + 2 * m + pi
                uts[s * 16 + m, 16 * (r - gs):16 * (r - gs) + 16,
                    64 * pi:64 * pi + 64] = up[hh, r].T
    pm = np.concatenate([np.ones(64, np.float32), -np.ones(64, np.float32)])
    bdo = np.zeros((128, 128), np.float32)
    bdo[0:64, 0:64] = 1.0
    bdo[64:128, 64:128] = 1.0
    return {
        "x0": np.ascontiguousarray(x, np.float32),
        "x0Td": np.ascontiguousarray(x.T, np.float32),
        "WqT": np.ascontiguousarray(np.asarray(inputs["L0_attn_Wq"]).T, np.float32),
        "WkT": np.ascontiguousarray(np.asarray(inputs["L0_attn_Wk"]).T, np.float32),
        "WvT": np.ascontiguousarray(np.asarray(inputs["L0_attn_Wv"]).T, np.float32),
        "WoT": np.ascontiguousarray(np.asarray(inputs["L0_attn_Wo"]).T, np.float32),
        "bo": np.asarray(inputs["L0_attn_bo"], np.float32).reshape(1, D),
        "lq1": np.asarray(inputs["L0_attn_lambda_q1"], np.float32).reshape(1, 64),
        "lk1": np.asarray(inputs["L0_attn_lambda_k1"], np.float32).reshape(1, 64),
        "lq2": np.asarray(inputs["L0_attn_lambda_q2"], np.float32).reshape(1, 64),
        "lk2": np.asarray(inputs["L0_attn_lambda_k2"], np.float32).reshape(1, 64),
        "dgr": np.asarray(inputs["L0_attn_delta_gain"], np.float32).reshape(1, 8),
        "chd": np.asarray(inputs["L0_attn_cos_head_delta"], np.float32).reshape(1, 8),
        "raw": np.asarray(inputs["L0_attn_cos_logit_scale_raw"], np.float32).reshape(1, 1),
        "subw": np.asarray(inputs["L0_attn_subln_weight"], np.float32).reshape(128, 1),
        "pm64": pm.reshape(128, 1),
        "bdo": bdo,
        "bdm": bdm,
        "uts": uts,
        "wcT": np.ascontiguousarray(np.asarray(inputs["L0_ffn_w_cproj"]).T, np.float32),
    }



def prep_B(inputs, x2_full, b):
    g = lambda k: np.asarray(inputs[k], np.float32)

    def arr_pbn(a):
        return np.ascontiguousarray(a.reshape(8, 128, 16).transpose(1, 0, 2).reshape(128, 128))

    x2 = np.ascontiguousarray(x2_full[b], np.float32)
    x2rT = np.stack([np.ascontiguousarray(x2_full[(b - j) % 8].T) for j in range(8)], 0)
    wcvT = np.zeros((8, 8), np.float32)
    for j in range(8):
        if b - j >= 0:
            wcvT[j, j] = 1.0
    ldt = g("L1_attn_log_dt").reshape(1024)
    ldt_bc = np.repeat(ldt.reshape(8, 128).T.reshape(128, 8, 1), 16, 2).reshape(128, 128)
    B_ = g("L1_attn_B"); P_ = g("L1_attn_P")[0]; C_ = g("L1_attn_C")
    pe_pos = np.arange(T - 1, -1, -1, dtype=np.float32)
    inv_freq = (1.0 / (10000.0 ** (np.arange(0, D, 2, dtype=np.float32) / np.float32(D)))).astype(np.float32)
    sinusoid = pe_pos[:, None] * inv_freq[None, :]
    pe = np.concatenate([np.sin(sinusoid), np.cos(sinusoid)], 1).astype(np.float32)
    return {
        "x2": x2,
        "x2rT": np.ascontiguousarray(x2rT, np.float32),
        "wcvT": wcvT,
        "ldt_bc": np.ascontiguousarray(ldt_bc, np.float32),
        "iwr_a": arr_pbn(g("L1_attn_inv_w_real")),
        "wim_a": arr_pbn(g("L1_attn_w_imaginary")),
        "Br_a": arr_pbn(B_[:, :, 0]), "Bi_a": arr_pbn(B_[:, :, 1]),
        "Pr_a": arr_pbn(P_[:, :, 0]), "Pi_a": arr_pbn(P_[:, :, 1]),
        "Cr_a": arr_pbn(C_[:, :, 0]), "Ci_a": arr_pbn(C_[:, :, 1]),
        "Dd_c": np.ascontiguousarray(g("L1_attn_D").reshape(8, 128).T),
        "peT": np.ascontiguousarray(pe.T),
        "Wqkv": g("L1_ffn_Wqkv"),
        "Wo1": g("L1_ffn_Wo"),
        "Wr": g("L1_ffn_Wr"),
        "ubT": np.ascontiguousarray(g("L1_ffn_u_bias").T),
        "vbT": np.ascontiguousarray(g("L1_ffn_v_bias").T),
        "ln1g": g("L1_ffn_ln1_g").reshape(1, D), "ln1b": g("L1_ffn_ln1_b").reshape(1, D),
        "ln2g": g("L1_ffn_ln2_g").reshape(1, D), "ln2b": g("L1_ffn_ln2_b").reshape(1, D),
        "ln3g": g("L1_ffn_ln3_g").reshape(1, D), "ln3b": g("L1_ffn_ln3_b").reshape(1, D),
        "Wff1": g("L1_ffn_Wff1"), "bff1c": g("L1_ffn_bff1").reshape(FF, 1),
        "Wff2": g("L1_ffn_Wff2"), "bff2r": g("L1_ffn_bff2").reshape(1, D),
        "Wg": g("L1_ffn_Wg"),
        "We1": g("L1_ffn_We1"), "be1c": g("L1_ffn_be1").reshape(NE * FF, 1),
        "We2": g("L1_ffn_We2"), "be2m": g("L1_ffn_be2"),
        "Wout": g("L1_ffn_Wout"),
    }


_CACHE = {}


def run_A(inputs):
    if "A" not in _CACHE:
        _CACHE["A"] = build_launchA()
    ncA = _CACHE["A"]
    maps = [prep_A(inputs, b) for b in range(NCORE)]
    res = run_bass_kernel_spmd(ncA, maps, core_ids=list(range(NCORE)))
    return np.stack([res.results[b]["xout"] for b in range(NCORE)], 0)


def run_B(inputs, x2_full):
    if "B" not in _CACHE:
        _CACHE["B"] = build_launchB()
    ncB = _CACHE["B"]
    maps = [prep_B(inputs, x2_full, b) for b in range(NCORE)]
    res = run_bass_kernel_spmd(ncB, maps, core_ids=list(range(NCORE)))
    return np.stack([res.results[b]["xo"] for b in range(NCORE)], 0)


def kernel(**inputs):
    x2 = run_A(inputs)
    x4 = run_B(inputs, x2)
    return x4


# version 06
# speedup vs baseline: 1.1758x; 1.0072x over previous
import numpy as np
import ml_dtypes
from contextlib import ExitStack

import concourse.bass as bass
import concourse.mybir as mybir
from concourse.bass_utils import run_bass_kernel_spmd

F32 = mybir.dt.float32
BF16 = mybir.dt.bfloat16
AF = mybir.ActivationFunctionType
ALU = mybir.AluOpType
AX = mybir.AxisListType

NCORE = 8
T = 2048
D = 1024
NQT = T // 128
NCB = D // 128
BF = ml_dtypes.bfloat16

ENGS = ("mm", "vv", "ss", "pp", "sy", "h", "g")
MS = {"tensor": "mm", "vector": "vv", "scalar": "ss", "gpsimd": "pp", "sync": "sy"}


_NOREC = {"dma_start", "wait_ge", "nop", "drain", "print"}


class _Rec:

    def __init__(self, e):
        self._e = e
        self.last = None

    def __getattr__(self, n):
        a = getattr(self._e, n)
        if callable(a) and not n.startswith("_"):
            def w(*args, __a=a, __n=n, **kw):
                r = __a(*args, **kw)
                if __n not in _NOREC and r is not None and hasattr(r, "then_inc"):
                    self.last = r
                return r
            return w
        return a


class PB:
    def __init__(self):
        self.nc = bass.Bass("TRN2", target_bir_lowering=False)
        self.ctx = ExitStack()
        self._sems = {}
        self.c = {}
        for s in ENGS:
            self.sem(s)

    def sem(self, name):
        if name not in self._sems:
            self._sems[name] = self.ctx.enter_context(self.nc.semaphore(name))
            self.c[name] = 0
        return self._sems[name]

    def sb(self, name, p, f, dt, side="left"):
        return self.ctx.enter_context(self.nc.sbuf_tensor(name, [p, f], dt, side=side))

    def sbm(self, name, p, f, dt, side="left"):
        cm = self.nc.sbuf_tensor(name, [p, f], dt, side=side)
        t = cm.__enter__()
        if not hasattr(self, "_stack"):
            self._stack = []
        self._stack.append((name, cm))
        return t

    def free(self, name):
        n, cm = self._stack.pop()
        assert n == name, f"free order: want {name} got {n}"
        cm.__exit__(None, None, None)

    def ps(self, name, f, dt=F32):
        return self.ctx.enter_context(self.nc.psum_tensor(name, [128, f], dt))

    def begin(self):
        self.block = self.ctx.enter_context(self.nc.Block())

    def step(self, **fns):
        snap = dict(self.c)
        S = self._sems
        if not hasattr(self, "_lastw"):
            self._lastw = {}
        for name, fn in fns.items():
            ms = MS[name]
            lw = self._lastw.setdefault(name, {s: 0 for s in ENGS})

            def body(e, fn=fn, ms=ms, lw=lw):
                for s in ENGS:
                    if snap[s] > lw[s]:
                        e.wait_ge(S[s], snap[s])
                        lw[s] = snap[s]
                rec = _Rec(e)
                fn(rec)
                if rec.last is not None:
                    rec.last.then_inc(S[ms], 1)
                else:
                    e.nop().then_inc(S[ms], 1)

            getattr(self.block, name)(body)
            self.c[ms] += 1

    def dma(self, e, out, in_, q):
        sem = "h" if q == "h" else "g"
        e.dma_start(out=out, in_=in_).then_inc(self._sems[sem], 16)
        self.c[sem] += 16

    def done(self):
        for n, cm in reversed(getattr(self, "_stack", [])):
            cm.__exit__(None, None, None)
        self._stack = []
        self.ctx.close()
        return self.nc


LIMIT = 99
H_LIMIT = 99

def build_launchA():
    pb = PB()
    nc = pb.nc

    def din(name, shape):
        return nc.dram_tensor(name, list(shape), F32, kind="ExternalInput").ap()

    x0 = din("x0", (T, D))
    x0T_d = din("x0Td", (D, T))
    WqT = din("WqT", (D, D)); WkT = din("WkT", (D, D))
    WvT = din("WvT", (D, D)); WoT = din("WoT", (D, D))
    bo = din("bo", (1, D))
    lq1 = din("lq1", (1, 64)); lk1 = din("lk1", (1, 64))
    lq2 = din("lq2", (1, 64)); lk2 = din("lk2", (1, 64))
    dgr = din("dgr", (1, 8)); chd = din("chd", (1, 8)); raw = din("raw", (1, 1))
    subw = din("subw", (128, 1)); pm64 = din("pm64", (128, 1))
    bdo = din("bdo", (128, 128))
    bdm = din("bdm", (32, 128, 128))
    uts = din("uts", (128, 128, 128))
    wcT = din("wcT", (16 * D, D))
    xout = nc.dram_tensor("xout", [T, D], F32, kind="ExternalOutput").ap()
    x1sp = nc.dram_tensor("x1sp", [T, D], F32).ap()

    ident_b = pb.sb("ident_b", 128, 128, BF16)
    ones_b = pb.sb("ones_b", 128, 128, BF16)
    ones_f = pb.sb("ones_f", 128, 128, F32)
    bdo_s = pb.sb("bdo_s", 128, 128, F32)
    bo_s = pb.sb("bo_s", 1, D, BF16)
    lrow = pb.sb("lrow", 1, 512, F32)
    trow = pb.sb("trow", 1, 64, F32)
    crow = pb.sb("crow", 1, 16, F32)
    cs_cols = pb.sb("cs_cols", 128, 8, F32)
    dg_cols = pb.sb("dg_cols", 128, 8, F32)
    lamn_col = pb.sb("lamn_col", 128, 1, F32)
    subw_c = pb.sb("subw_c", 128, 1, F32)
    pm_c = pb.sb("pm_c", 128, 1, F32)
    scrA = pb.sb("scrA", 128, T, F32)
    scrB = pb.sb("scrB", 128, T, F32)
    scrC = pb.sb("scrC", 128, T, F32)
    e1c = pb.sb("e1c", 128, 512, BF16)
    e2c = pb.sb("e2c", 128, 512, BF16)
    gstr = pb.sb("gstr", 128, 512, BF16)
    xt_l = pb.sb("xt_l", 128, D, F32)
    k2s = pb.sb("k2s", 64, T, BF16)
    q2s = pb.sb("q2s", 64, T, BF16)
    xbf = pb.sb("xbf", 128, D, BF16)
    acol = pb.sb("acol", 128, 16, F32)

    kTn = pb.sbm("kTn", 128, NCB * T, BF16, side="right")
    v_b = pb.sbm("v_b", 128, NQT * D, BF16, side="right")
    qTn = pb.sbm("qTn", 128, NCB * T, BF16, side="right")
    wts = pb.sbm("wts", 128, NCB * D, BF16, side="right")
    x0T = pb.sbm("x0T", 128, NCB * T, BF16, side="right")

    P = [pb.ps(f"P{i}", 512) for i in range(7)]
    PT = pb.ps("PT", 1024, BF16)

    pb.begin()

    def l_sync(e):
        pb.dma(e, bdo_s[:, :], bdo, "h")
        pb.dma(e, subw_c[:, :], subw, "h")
        pb.dma(e, pm_c[:, :], pm64, "h")
        pb.dma(e, lrow[0:1, 0:64], lq1, "h")
        pb.dma(e, lrow[0:1, 64:128], lk1, "h")
        pb.dma(e, lrow[0:1, 128:192], lq2, "h")
        pb.dma(e, lrow[0:1, 192:256], lk2, "h")
        pb.dma(e, lrow[0:1, 256:264], dgr, "h")
        pb.dma(e, lrow[0:1, 264:272], chd, "h")
        pb.dma(e, lrow[0:1, 272:273], raw, "h")

    def l_g(e):
        for cb in range(NCB):
            pb.dma(e, x0T[:, cb * T:(cb + 1) * T], x0T_d[cb * 128:(cb + 1) * 128, :], "g")
        pb.dma(e, bo_s[0:1, :], bo, "g")
        e.memset(ones_b[:, :], 1.0)
        e.memset(ones_f[:, :], 1.0)
        e.drain()
        e.affine_select(ident_b[:, :], ones_b[:, :], [[-1, 128]], ALU.is_equal, 0.0,
                        channel_multiplier=1)

    pb.step(sync=l_sync, gpsimd=l_g)

    def p1v(e):
        e.tensor_tensor(trow[0:1, 0:64], lrow[0:1, 0:64], lrow[0:1, 64:128], ALU.mult)
        e.drain()
        e.reduce_sum(crow[0:1, 0:1], trow[0:1, 0:64], AX.X)
        e.drain()
        e.tensor_tensor(trow[0:1, 0:64], lrow[0:1, 128:192], lrow[0:1, 192:256], ALU.mult)
        e.drain()
        e.reduce_sum(crow[0:1, 1:2], trow[0:1, 0:64], AX.X)
        e.drain()
        e.reduce_sum(crow[0:1, 2:3], lrow[0:1, 264:272], AX.X)

    pb.step(vector=p1v)

    def p1s(e):
        e.activation(crow[0:1, 3:4], crow[0:1, 0:1], AF.Exp)
        e.drain()
        e.activation(crow[0:1, 4:5], crow[0:1, 1:2], AF.Exp)
        e.drain()
        e.activation(crow[0:1, 5:6], lrow[0:1, 272:273], AF.Sigmoid)

    pb.step(scalar=p1s)

    def p1v2(e):
        e.tensor_tensor(crow[0:1, 6:7], crow[0:1, 4:5], crow[0:1, 3:4], ALU.subtract)
        e.drain()
        e.tensor_scalar(crow[0:1, 6:7], crow[0:1, 6:7], -0.5, None, ALU.add)
        e.drain()
        e.tensor_scalar(crow[0:1, 7:8], crow[0:1, 5:6], 15.0, None, ALU.mult)
        e.drain()
        e.tensor_scalar(crow[0:1, 2:3], crow[0:1, 2:3], 0.125, None, ALU.mult)
        e.drain()
        e.tensor_scalar(trow[0:1, 0:8], lrow[0:1, 264:272], crow[0:1, 2:3], None,
                        ALU.subtract)

    pb.step(vector=p1v2)

    def p1s2(e):
        e.activation(trow[0:1, 8:16], trow[0:1, 0:8], AF.Tanh)

    pb.step(scalar=p1s2)

    def p1v3(e):
        e.tensor_scalar(trow[0:1, 16:24], trow[0:1, 8:16], 0.5, 1.0, ALU.mult, ALU.add)
        e.drain()
        e.tensor_scalar(trow[0:1, 16:24], trow[0:1, 16:24], crow[0:1, 7:8], None,
                        ALU.mult)
        e.drain()
        e.tensor_scalar(subw_c[:, 0:1], subw_c[:, 0:1], 0.5, None, ALU.mult)

    pb.step(vector=p1v3)

    def p1t(e):
        e.matmul(P[0][:, 0:8], ones_f[0:1, 0:128], trow[0:1, 16:24], start=True, stop=True)
        e.matmul(P[1][:, 0:8], ones_f[0:1, 0:128], lrow[0:1, 256:264], start=True, stop=True)
        e.matmul(P[2][:, 0:1], ones_f[0:1, 0:128], crow[0:1, 6:7], start=True, stop=True)

    pb.step(tensor=p1t)

    def p1v4(e):
        e.tensor_copy(cs_cols[:, :], P[0][:, 0:8])
        e.drain()
        e.tensor_copy(dg_cols[:, :], P[1][:, 0:8])
        e.drain()
        e.tensor_copy(lamn_col[:, :], P[2][:, 0:1])

    pb.step(vector=p1v4)

    if LIMIT < 2:
        return _finishA(pb)
    def load_w(wdr):
        def f(e):
            for cb in range(NCB):
                pb.dma(e, wts[:, cb * D:(cb + 1) * D],
                       wdr[cb * 128:(cb + 1) * 128, :], "g")
        return f

    for side, wdr, dstn in (("q", WqT, qTn), ("k", WkT, kTn)):
        pb.step(gpsimd=load_w(wdr))
        for h in range(NCB):
            def mmf(e, h=h):
                for ch in range(4):
                    for r in range(NCB):
                        e.matmul(P[ch][:, :],
                                 wts[:, r * D + h * 128: r * D + (h + 1) * 128],
                                 x0T[:, r * T + ch * 512: r * T + (ch + 1) * 512],
                                 start=(r == 0), stop=(r == NCB - 1))
            pb.step(tensor=mmf)

            def evacf(e):
                for ch in range(4):
                    e.tensor_copy(scrA[:, ch * 512:(ch + 1) * 512], P[ch][:, :])
            pb.step(vector=evacf)

            def sqf(e):
                e.activation(scrB[:, :], scrA[:, :], AF.Square)
            pb.step(scalar=sqf)

            def nrm_mm(e):
                for ch in range(4):
                    e.matmul(P[ch][:, :], bdo_s[:, :], scrB[:, ch * 512:(ch + 1) * 512],
                             start=True, stop=True)
            pb.step(tensor=nrm_mm)

            def nrm_max(e):
                for ch in range(4):
                    e.tensor_scalar(scrB[:, ch * 512:(ch + 1) * 512], P[ch][:, :],
                                    1e-12, None, ALU.max)
            pb.step(vector=nrm_max)

            def nrm_inv(e):
                e.activation(scrC[:, :], scrB[:, :], AF.Ln)
                e.drain()
                e.activation(scrC[:, :], scrC[:, :], AF.Exp, scale=-0.5)
            pb.step(scalar=nrm_inv)

            if side == "q":
                def mulq(e, h=h):
                    e.scalar_tensor_tensor(qTn[:, h * T:(h + 1) * T], scrA[:, :],
                                           cs_cols[:, h:h + 1], scrC[:, :],
                                           ALU.mult, ALU.mult)
                pb.step(vector=mulq)
            else:
                def mulk(e, h=h):
                    e.scalar_tensor_tensor(kTn[:, h * T:(h + 1) * T], scrA[:, :],
                                           pm_c[:, 0:1], scrC[:, :],
                                           ALU.mult, ALU.mult)
                pb.step(vector=mulk)

    pb.step(gpsimd=load_w(WvT))
    for qp in range(NQT // 2):
        def vmm(e, qp=qp):
            for half in range(2):
                qt = qp * 2 + half
                for oc in range(2):
                    for r in range(NCB):
                        e.matmul(P[half * 2 + oc][:, :],
                                 x0T[:, r * T + qt * 128: r * T + (qt + 1) * 128],
                                 wts[:, r * D + oc * 512: r * D + (oc + 1) * 512],
                                 start=(r == 0), stop=(r == NCB - 1))
        pb.step(tensor=vmm)

        def vev(e, qp=qp):
            for half in range(2):
                qt = qp * 2 + half
                for oc in range(2):
                    e.tensor_copy(v_b[:, qt * D + oc * 512: qt * D + (oc + 1) * 512],
                                  P[half * 2 + oc][:, :])
        pb.step(vector=vev)

    pb.free("x0T")
    if LIMIT < 3:
        return _finishA(pb)

    for h in range(min(NCB, H_LIMIT)):
        def cph(e, h=h):
            e.tensor_copy(k2s[:, :], kTn[64:128, h * T:(h + 1) * T])
            e.drain()
            e.tensor_copy(q2s[:, :], qTn[64:128, h * T:(h + 1) * T])
        pb.step(vector=cph)
        for qc in range(4):
            for kt in range(NQT):
                k1 = kTn[0:64, h * T + kt * 128: h * T + (kt + 1) * 128]
                k2 = k2s[:, kt * 128:(kt + 1) * 128]
                q1 = qTn[0:64, h * T + qc * 512: h * T + (qc + 1) * 512]
                q2 = q2s[:, qc * 512:(qc + 1) * 512]

                def lgmm(e, k1=k1, k2=k2, q1=q1, q2=q2):
                    e.matmul(P[0][:, :], k1, q1, start=True, stop=False)
                    e.matmul(P[2][:, :], k1, q1, start=True, stop=False)
                    e.matmul(P[1][:, :], k2, q2, start=True, stop=False)
                    e.matmul(P[2][:, :], k2, q2, start=False, stop=True)
                pb.step(tensor=lgmm)

                def gtan(e, h=h):
                    e.activation(gstr[:, :], P[2][:, :], AF.Tanh,
                                 scale=dg_cols[:, h:h + 1])
                pb.step(scalar=gtan)

                def gadd(e):
                    e.matmul(P[0][:, :], ident_b[:, :], gstr[:, :], start=False, stop=True)
                    e.matmul(P[1][:, :], ident_b[:, :], gstr[:, :], start=False, stop=True)
                pb.step(tensor=gadd)

                def eexp(e):
                    e.activation(e1c[:, :], P[0][:, :], AF.Exp)
                    e.drain()
                    e.activation(e2c[:, :], P[1][:, :], AF.Exp, scale=-1.0)
                pb.step(scalar=eexp)

                vt = v_b[:, kt * D + h * 128: kt * D + (h + 1) * 128]

                def avmm(e, vt=vt, kt=kt):
                    st = (kt == 0); sp = (kt == NQT - 1)
                    e.matmul(P[3][:, :], vt, e1c[:, :], start=st, stop=sp)
                    e.matmul(P[4][:, :], vt, e2c[:, :], start=st, stop=sp)
                    e.matmul(P[5][:, :], ones_b[:, :], e1c[:, :], start=st, stop=sp)
                    e.matmul(P[6][:, :], ones_b[:, :], e2c[:, :], start=st, stop=sp)
                pb.step(tensor=avmm)

            def fin_v(e):
                e.reciprocal(scrB[:, 0:512], P[5][:, :])
                e.drain()
                e.reciprocal(scrB[:, 512:1024], P[6][:, :])
                e.drain()
                e.tensor_tensor(scrA[:, 0:512], P[3][:, :], scrB[:, 0:512], ALU.mult)
                e.drain()
                e.tensor_tensor(scrA[:, 512:1024], P[4][:, :], scrB[:, 512:1024], ALU.mult)
                e.drain()
                e.scalar_tensor_tensor(scrB[:, 1024:1536], scrA[:, 512:1024],
                                       lamn_col[:, 0:1], scrA[:, 0:512],
                                       ALU.mult, ALU.add)
            pb.step(vector=fin_v)

            def fin_sq(e):
                e.activation(scrC[:, 0:512], scrB[:, 1024:1536], AF.Square)
            pb.step(scalar=fin_sq)

            def fin_mm(e):
                e.matmul(P[0][:, :], ones_f[:, :], scrC[:, 0:512], start=True, stop=True)
            pb.step(tensor=fin_mm)

            def fin_ms(e):
                e.tensor_scalar(scrC[:, 512:1024], P[0][:, :], 1.0 / 128, 1e-5,
                                ALU.mult, ALU.add)
            pb.step(vector=fin_ms)

            def fin_ln(e):
                e.activation(scrC[:, 1024:1536], scrC[:, 512:1024], AF.Ln)
                e.drain()
                e.activation(scrC[:, 512:1024], scrC[:, 1024:1536], AF.Exp, scale=-0.5)
            pb.step(scalar=fin_ln)

            def fin_out(e, h=h, qc=qc):
                e.tensor_tensor(scrA[:, 0:512], scrB[:, 1024:1536], scrC[:, 512:1024],
                                ALU.mult)
                e.drain()
                e.tensor_scalar(qTn[:, h * T + qc * 512: h * T + (qc + 1) * 512],
                                scrA[:, 0:512], subw_c[:, 0:1], None, ALU.mult)
            pb.step(vector=fin_out)

    ysubT = qTn
    if LIMIT < 4:
        return _finishA(pb)

    pb.step(gpsimd=load_w(WoT))
    x1T = kTn
    for qt in range(NQT):
        def ldx(e, qt=qt):
            pb.dma(e, xt_l[:, :], x0[qt * 128:(qt + 1) * 128, :], "h")
        pb.step(sync=ldx)

        def womm2(e, qt=qt):
            for oc in range(2):
                for r in range(NCB):
                    e.matmul(P[oc][:, :],
                             ysubT[:, r * T + qt * 128: r * T + (qt + 1) * 128],
                             wts[:, r * D + oc * 512: r * D + (oc + 1) * 512],
                             start=(r == 0), stop=False)
                e.matmul(P[oc][:, :], ones_b[0:1, 0:128],
                         bo_s[0:1, oc * 512:(oc + 1) * 512], start=False, stop=True)
        pb.step(tensor=womm2)

        def ev(e):
            e.scalar_tensor_tensor(scrA[:, 0:512], P[0][:, :], 1.0, xt_l[:, 0:512],
                                   ALU.mult, ALU.add, accum_out=acol[:, 0:1])
            e.drain()
            e.scalar_tensor_tensor(scrA[:, 512:1024], P[1][:, :], 1.0, xt_l[:, 512:1024],
                                   ALU.mult, ALU.add, accum_out=acol[:, 1:2])
            e.drain()
            e.tensor_tensor(acol[:, 2:3], acol[:, 0:1], acol[:, 1:2], ALU.add)
        pb.step(vector=ev)

        def sq(e):
            e.activation(scrB[:, 0:1024], scrA[:, 0:1024], AF.Square,
                         accum_out=acol[:, 3:4])
        pb.step(scalar=sq)

        def stats(e):
            e.tensor_scalar(acol[:, 4:5], acol[:, 2:3], 1.0 / D, None, ALU.mult)
            e.drain()
            e.tensor_tensor(acol[:, 5:6], acol[:, 4:5], acol[:, 4:5], ALU.mult)
            e.drain()
            e.scalar_tensor_tensor(acol[:, 6:7], acol[:, 3:4], 1.0 / D, acol[:, 5:6],
                                   ALU.mult, ALU.subtract)
            e.drain()
            e.tensor_scalar(acol[:, 6:7], acol[:, 6:7], 1e-5, None, ALU.add)
        pb.step(vector=stats)

        def rstd(e):
            e.activation(acol[:, 7:8], acol[:, 6:7], AF.Ln)
            e.drain()
            e.activation(acol[:, 8:9], acol[:, 7:8], AF.Exp, scale=-0.5)
        pb.step(scalar=rstd)

        def nmr(e):
            e.scalar_tensor_tensor(acol[:, 9:10], acol[:, 4:5], -1.0, acol[:, 8:9],
                                   ALU.mult, ALU.mult)
        pb.step(vector=nmr)

        def norm(e):
            e.activation(xt_l[:, 0:1024], scrA[:, 0:1024], AF.Identity,
                         bias=acol[:, 9:10], scale=acol[:, 8:9])
            e.drain()
            e.activation(xbf[:, 0:1024], scrA[:, 0:1024], AF.Identity,
                         bias=acol[:, 9:10], scale=acol[:, 8:9])
        pb.step(scalar=norm)

        def spill(e, qt=qt):
            pb.dma(e, x1sp[qt * 128:(qt + 1) * 128, :], xt_l[:, 0:1024], "h")
        pb.step(sync=spill)

        def tr(e):
            for cb in range(NCB):
                e.transpose(PT[:, cb * 128:(cb + 1) * 128],
                            xbf[:, cb * 128:(cb + 1) * 128], ident_b[:, :])
        pb.step(tensor=tr)

        def trev(e, qt=qt):
            for cb in range(NCB):
                e.tensor_copy(x1T[:, cb * T + qt * 128: cb * T + (qt + 1) * 128],
                              PT[:, cb * 128:(cb + 1) * 128])
                e.drain()
        pb.step(vector=trev)

    if LIMIT < 5:
        return _finishA(pb)
    pb.free("wts")
    pb.free("qTn")
    bdm_s = pb.sbm("bdm_s", 128, 32 * 128, BF16, side="right")
    uts_s = pb.sbm("uts_s", 128, 16 * 128, BF16, side="right")
    yts = pb.sbm("yts", 128, 512, BF16, side="right")
    wsup = pb.sbm("wsup", 128, 16 * D, BF16, side="right")
    of_s = pb.sbm("of_s", 128, 4 * D, F32, side="right")

    def ldbdm(e):
        for i in range(32):
            pb.dma(e, bdm_s[:, i * 128:(i + 1) * 128], bdm[i, :, :], "g")
    pb.step(gpsimd=ldbdm)

    hts = v_b
    for hf in range(4):
        tof = hf * 512
        for s in range(8):
            def ldu(e, s=s):
                for m in range(16):
                    pb.dma(e, uts_s[:, m * 128:(m + 1) * 128], uts[s * 16 + m, :, :], "g")
            pb.step(gpsimd=ldu)

            hh = s // 2
            for gl in range(4):
                g8 = (s % 2) * 4 + gl
                idx = hh * 8 + g8

                def ymm(e, idx=idx, g8=g8, tof=tof):
                    e.matmul(P[0][:, :], bdm_s[:, idx * 128:(idx + 1) * 128],
                             x1T[:, g8 * T + tof: g8 * T + tof + 512],
                             start=True, stop=True)
                pb.step(tensor=ymm)

                def yev(e):
                    e.tensor_copy(yts[:, 0:512], P[0][:, :])
                pb.step(vector=yev)

                def zmm(e, gl=gl):
                    for mg in range(4):
                        m = gl * 4 + mg
                        e.matmul(P[2 + mg][:, :], uts_s[:, m * 128:(m + 1) * 128],
                                 yts[:, 0:512], start=True, stop=True)
                pb.step(tensor=zmm)

                def gel(e, gl=gl):
                    for mg in range(4):
                        m = gl * 4 + mg
                        e.activation(hts[:, m * 512: (m + 1) * 512],
                                     P[2 + mg][:, :], AF.Gelu)
                pb.step(scalar=gel)

            def ldw0(e, s=s):
                for fb in range(16):
                    pb.dma(e, wsup[:, fb * D:(fb + 1) * D],
                           wcT[(s * 16 + fb) * 128:(s * 16 + fb + 1) * 128, :], "g")
            pb.step(gpsimd=ldw0)

            def mk_cpmm(q8):
                bp = 4 + (q8 % 2) * 2 if False else (4 if q8 % 2 == 0 else 6)
                def cpmm(e, q8=q8, bp=bp):
                    for fb in range(16):
                        for oc in range(2):
                            e.matmul(P[bp + oc][:, :] if bp + oc < 7 else P[0][:, :],
                                     hts[:, fb * 512 + q8 * 128:fb * 512 + (q8 + 1) * 128],
                                     wsup[:, fb * D + oc * 512: fb * D + (oc + 1) * 512],
                                     start=(fb == 0), stop=(fb == 15))
                return cpmm

            def mk_cpadd(q8, s):
                bp = 4 if q8 % 2 == 0 else 6
                def cpadd(e, q8=q8, s=s, bp=bp):
                    for oc in range(2):
                        psrc = P[bp + oc][:, :] if bp + oc < 7 else P[0][:, :]
                        dst = of_s[:, q8 * D + oc * 512: q8 * D + (oc + 1) * 512]
                        if s == 0:
                            e.tensor_copy(dst, psrc)
                        else:
                            e.tensor_tensor(dst, psrc, dst, ALU.add)
                return cpadd

            for q8 in range(4):
                pb.step(tensor=mk_cpmm(q8))
                if q8 >= 1:
                    pb.step(vector=mk_cpadd(q8 - 1, s))
            pb.step(vector=mk_cpadd(3, s))

        for q8 in range(4):
            qt = hf * 4 + q8

            def ldx1(e, qt=qt):
                pb.dma(e, xt_l[:, :], x1sp[qt * 128:(qt + 1) * 128, :], "h")
            pb.step(sync=ldx1)

            def ev2(e, q8=q8):
                e.scalar_tensor_tensor(scrA[:, 0:512], of_s[:, q8 * D:q8 * D + 512],
                                       1.0, xt_l[:, 0:512], ALU.mult, ALU.add,
                                       accum_out=acol[:, 0:1])
                e.drain()
                e.scalar_tensor_tensor(scrA[:, 512:1024], of_s[:, q8 * D + 512:(q8 + 1) * D],
                                       1.0, xt_l[:, 512:1024], ALU.mult, ALU.add,
                                       accum_out=acol[:, 1:2])
                e.drain()
                e.tensor_tensor(acol[:, 2:3], acol[:, 0:1], acol[:, 1:2], ALU.add)
            pb.step(vector=ev2)

            def sq2(e):
                e.activation(scrB[:, 0:1024], scrA[:, 0:1024], AF.Square,
                             accum_out=acol[:, 3:4])
            pb.step(scalar=sq2)

            def stats2(e):
                e.tensor_scalar(acol[:, 4:5], acol[:, 2:3], 1.0 / D, None, ALU.mult)
                e.drain()
                e.tensor_tensor(acol[:, 5:6], acol[:, 4:5], acol[:, 4:5], ALU.mult)
                e.drain()
                e.scalar_tensor_tensor(acol[:, 6:7], acol[:, 3:4], 1.0 / D, acol[:, 5:6],
                                       ALU.mult, ALU.subtract)
                e.drain()
                e.tensor_scalar(acol[:, 6:7], acol[:, 6:7], 1e-5, None, ALU.add)
            pb.step(vector=stats2)

            def rstd2(e):
                e.activation(acol[:, 7:8], acol[:, 6:7], AF.Ln)
                e.drain()
                e.activation(acol[:, 8:9], acol[:, 7:8], AF.Exp, scale=-0.5)
            pb.step(scalar=rstd2)

            def nmr2(e):
                e.scalar_tensor_tensor(acol[:, 9:10], acol[:, 4:5], -1.0, acol[:, 8:9],
                                       ALU.mult, ALU.mult)
            pb.step(vector=nmr2)

            def norm2(e):
                e.activation(xt_l[:, 0:1024], scrA[:, 0:1024], AF.Identity,
                             bias=acol[:, 9:10], scale=acol[:, 8:9])
            pb.step(scalar=norm2)

            def st2(e, qt=qt):
                pb.dma(e, xout[qt * 128:(qt + 1) * 128, :], xt_l[:, 0:1024], "h")
            pb.step(sync=st2)

    return _finishA(pb)


def _finishA(pb):
    hf_target = pb.c["h"]
    gf_target = pb.c["g"]

    def fence(e):
        if hf_target:
            e.wait_ge(pb._sems["h"], hf_target)
        if gf_target:
            e.wait_ge(pb._sems["g"], gf_target)
    pb.block.sync(fence)
    return pb.done()



FF = 4096
NE = 8
NH = 16

LIMITB = 99


def build_launchB():
    pb = PB()
    nc = pb.nc

    def din(name, shape):
        return nc.dram_tensor(name, list(shape), F32, kind="ExternalInput").ap()

    x2 = din("x2", (T, D))
    x2rT = din("x2rT", (NCORE, D, T))
    wcvT = din("wcvT", (8, 8))
    ldt_bc = din("ldt_bc", (128, 128))
    iwr_a = din("iwr_a", (128, 128)); wim_a = din("wim_a", (128, 128))
    Br_a = din("Br_a", (128, 128)); Bi_a = din("Bi_a", (128, 128))
    Pr_a = din("Pr_a", (128, 128)); Pi_a = din("Pi_a", (128, 128))
    Cr_a = din("Cr_a", (128, 128)); Ci_a = din("Ci_a", (128, 128))
    Dd_c = din("Dd_c", (128, 8))
    peT = din("peT", (D, T))
    Wqkv = din("Wqkv", (D, 3 * D))
    Wo1 = din("Wo1", (D, D))
    Wr = din("Wr", (D, D))
    ubT = din("ubT", (64, 16)); vbT = din("vbT", (64, 16))
    ln1g = din("ln1g", (1, D)); ln1b = din("ln1b", (1, D))
    ln2g = din("ln2g", (1, D)); ln2b = din("ln2b", (1, D))
    ln3g = din("ln3g", (1, D)); ln3b = din("ln3b", (1, D))
    Wff1 = din("Wff1", (D, FF)); bff1c = din("bff1c", (FF, 1))
    Wff2 = din("Wff2", (FF, D)); bff2r = din("bff2r", (1, D))
    Wg = din("Wg", (D, NE))
    We1 = din("We1", (NE, D, FF)); be1c = din("be1c", (NE * FF, 1))
    We2 = din("We2", (NE, FF, D)); be2m = din("be2m", (NE, D))
    Wout = din("Wout", (D, D))
    xo = nc.dram_tensor("xo", [T, D], F32, kind="ExternalOutput").ap()
    x3sp = nc.dram_tensor("x3sp", [T, D], F32).ap()
    h1sp = nc.dram_tensor("h1sp", [T, D], F32).ap()
    h2sp = nc.dram_tensor("h2sp", [T, D], F32).ap()
    rTsp = nc.dram_tensor("rTsp", [D, T], F32).ap()
    kTsp = nc.dram_tensor("kTsp", [D, T], BF16).ap()
    sc0 = nc.dram_tensor("sc0", [T * 2049 + 128], BF16).ap()
    sc1 = nc.dram_tensor("sc1", [T * 2049 + 128], BF16).ap()

    ident_b = pb.sb("ident_b", 128, 128, BF16)
    ident_f = pb.sb("ident_f", 128, 128, F32)
    ones_b = pb.sb("ones_b", 128, 128, BF16)
    ones_f = pb.sb("ones_f", 128, 128, F32)
    scrA = pb.sb("scrA", 128, T, F32)
    xt_l = pb.sb("xt_l", 128, D, F32)
    xbf = pb.sb("xbf", 128, D, BF16)
    acol = pb.sb("acol", 128, 16, F32)
    rowb = pb.sb("rowb", 1, 256, F32)
    ubc = pb.sb("ubc", 64, 16, F32)
    vbc = pb.sb("vbc", 64, 16, F32)
    mskT = pb.sb("mskT", 8, T, BF16)
    msks = pb.sb("msks", 128, 8 * NQT, F32)
    gate_c = pb.sb("gate_c", 128, NQT, F32)
    wgsb = pb.sb("wgsb", 128, 8 * 8, BF16)
    be2s = pb.sb("be2s", 8, D, BF16)
    wcvs = pb.sb("wcvs", 8, 8, F32)
    ksel = pb.sb("ksel", 128, 64, F32)
    kblk = pb.sb("kblk", 128, 80, F32)
    ebuf = pb.sb("ebuf", 128, T, BF16)
    bdr = pb.sb("bdr", 128, T, BF16)
    bdsh = pb.sb("bdsh", 128, 2 * T, BF16)
    etil = pb.sb("etil", 128, 1024, BF16)
    dmsk = pb.sb("dmsk", 128, 128, F32)
    bcol = pb.sb("bcol", 128, 8, F32)
    pebuf = pb.sb("pebuf", 128, 1024, BF16)

    x3T = pb.sbm("x3T", 128, NCB * T, BF16, side="right")
    accc = pb.sbm("accc", 128, NCB * T, F32, side="right")
    xjt = pb.sbm("xjt", 128, 2 * T, F32, side="right")
    s4a = pb.sbm("s4a", 128, 128, F32, side="right")
    s4b = pb.sbm("s4b", 128, 128, F32, side="right")
    s4c = pb.sbm("s4c", 128, 128, F32, side="right")
    s4d = pb.sbm("s4d", 128, 128, F32, side="right")
    s4e = pb.sbm("s4e", 128, 128, F32, side="right")
    s4f = pb.sbm("s4f", 128, 320, F32, side="right")

    P = [pb.ps(f"P{i}", 512) for i in range(7)]
    PT = pb.ps("PT", 1024, BF16)

    S = pb._sems
    pb.begin()

    def b0g(e):
        e.memset(ones_b[:, :], 1.0)
        e.memset(ones_f[:, :], 1.0)
        e.drain()
        e.affine_select(ident_b[:, :], ones_b[:, :], [[-1, 128]], ALU.is_equal, 0.0,
                        channel_multiplier=1)
        e.drain()
        e.affine_select(ident_f[:, :], ones_f[:, :], [[-1, 128]], ALU.is_equal, 0.0,
                        channel_multiplier=1)
        e.drain()
        e.memset(dmsk[:, :], 0.0)
        e.drain()
        e.affine_select(dmsk[:, :], dmsk[:, :], [[-1, 128]], ALU.is_ge, -1e30,
                        channel_multiplier=1)

    def b0s(e):
        pb.dma(e, wcvs[:, :], wcvT, "h")
        pb.dma(e, ubc[:, :], ubT, "h")
        pb.dma(e, vbc[:, :], vbT, "h")
        pb.dma(e, s4a[:, :], ldt_bc, "h")
        pb.dma(e, kblk[:, 0:8], Dd_c, "h")

    def b0g2(e):
        pb.dma(e, be2s[:, :], be2m, "g")
        for cb in range(NCB):
            pb.dma(e, wgsb[:, cb * 8:(cb + 1) * 8], Wg[cb * 128:(cb + 1) * 128, :], "g")

    pb.step(gpsimd=b0g, sync=b0s)
    pb.step(gpsimd=b0g2)

    def b0v(e):
        e.tensor_scalar(ubc[:, :], ubc[:, :], 0.125, None, ALU.mult)
        e.drain()
        e.tensor_scalar(vbc[:, :], vbc[:, :], 0.125, None, ALU.mult)
    pb.step(vector=b0v)

    Z, W2, IRW = _s4_consts()

    def b1s(e):
        e.activation(s4b[:, :], s4a[:, :], AF.Exp)
    pb.step(scalar=b1s)

    def b1sy(e):
        pb.dma(e, s4a[:, :], iwr_a, "h")
    pb.step(sync=b1sy)

    def b1s2(e):
        e.activation(s4c[:, :], s4a[:, :], AF.Exp)
    pb.step(scalar=b1s2)

    def b1sy2(e):
        pb.dma(e, s4a[:, :], wim_a, "h")
    pb.step(sync=b1sy2)

    def b1v(e):
        e.tensor_tensor(s4c[:, :], s4c[:, :], s4b[:, :], ALU.mult)
        e.drain()
        e.tensor_scalar(s4c[:, :], s4c[:, :], -1.0, None, ALU.mult)
        e.drain()
        e.tensor_tensor(s4d[:, :], s4a[:, :], s4b[:, :], ALU.mult)
    pb.step(vector=b1v)

    rcols = {}
    colidx = [0]

    def alloc_col():
        c = colidx[0]; colidx[0] += 1
        return c

    for ab, (Xr_d, Xi_d, Yr_d, Yi_d, conjY) in enumerate((
            (Br_a, Bi_a, Cr_a, Ci_a, False),
            (Br_a, Bi_a, Pr_a, Pi_a, True),
            (Pr_a, Pi_a, Cr_a, Ci_a, False),
            (Pr_a, Pi_a, Pr_a, Pi_a, True))):
        def ldv(e, Xr_d=Xr_d, Xi_d=Xi_d, Yr_d=Yr_d, Yi_d=Yi_d):
            pb.dma(e, s4a[:, :], Xr_d, "h")
            pb.dma(e, xjt[:, 1664:1792], Xi_d, "h")
            pb.dma(e, s4e[:, :], Yr_d, "h")
            pb.dma(e, xjt[:, 0:128], Yi_d, "h")
        pb.step(sync=ldv)

        sgn = -1.0 if conjY else 1.0

        def vab(e, sgn=sgn):
            e.tensor_tensor(xjt[:, 128:256], s4a[:, :], s4e[:, :], ALU.mult)
            e.drain()
            e.tensor_tensor(xjt[:, 256:384], xjt[:, 1664:1792], xjt[:, 0:128], ALU.mult)
            e.drain()
            e.tensor_scalar(xjt[:, 256:384], xjt[:, 256:384], sgn, None, ALU.mult)
            e.drain()
            e.tensor_tensor(xjt[:, 384:512], xjt[:, 128:256], xjt[:, 256:384],
                            ALU.subtract)
            e.drain()
            e.tensor_tensor(xjt[:, 512:640], s4a[:, :], xjt[:, 0:128], ALU.mult)
            e.drain()
            e.tensor_scalar(xjt[:, 512:640], xjt[:, 512:640], sgn, None, ALU.mult)
            e.drain()
            e.tensor_tensor(xjt[:, 640:768], xjt[:, 1664:1792], s4e[:, :], ALU.mult)
            e.drain()
            e.tensor_tensor(xjt[:, 512:640], xjt[:, 512:640], xjt[:, 640:768],
                            ALU.add)
        pb.step(vector=vab)

        for m in range(5):
            zr, zi = Z[m]
            cr_ = alloc_col(); ci_ = alloc_col()
            rcols[(ab, m)] = (cr_, ci_)

            def cch(e, zr=zr, zi=zi, cr_=cr_, ci_=ci_):
                e.tensor_scalar(xjt[:, 768:896], s4c[:, :], -1.0, zr, ALU.mult, ALU.add)
                e.drain()
                e.tensor_scalar(xjt[:, 896:1024], s4d[:, :], -1.0, zi, ALU.mult, ALU.add)
                e.drain()
                e.tensor_tensor(xjt[:, 1024:1152], xjt[:, 768:896], xjt[:, 768:896], ALU.mult)
                e.drain()
                e.tensor_tensor(xjt[:, 1152:1280], xjt[:, 896:1024], xjt[:, 896:1024], ALU.mult)
                e.drain()
                e.tensor_tensor(xjt[:, 1024:1152], xjt[:, 1024:1152], xjt[:, 1152:1280], ALU.add)
                e.drain()
                e.reciprocal(xjt[:, 1152:1280], xjt[:, 1024:1152])
                e.drain()
                e.tensor_tensor(xjt[:, 1280:1408], xjt[:, 384:512], xjt[:, 768:896], ALU.mult)
                e.drain()
                e.tensor_tensor(xjt[:, 1408:1536], xjt[:, 512:640], xjt[:, 896:1024], ALU.mult)
                e.drain()
                e.tensor_tensor(xjt[:, 1280:1408], xjt[:, 1280:1408], xjt[:, 1408:1536], ALU.add)
                e.drain()
                e.tensor_tensor(xjt[:, 1408:1536], xjt[:, 512:640], xjt[:, 768:896], ALU.mult)
                e.drain()
                e.tensor_tensor(xjt[:, 1536:1664], xjt[:, 384:512], xjt[:, 896:1024], ALU.mult)
                e.drain()
                e.tensor_tensor(xjt[:, 1408:1536], xjt[:, 1408:1536], xjt[:, 1536:1664], ALU.subtract)
                e.drain()
                e.tensor_tensor(xjt[:, 1280:1408], xjt[:, 1280:1408], xjt[:, 1152:1280], ALU.mult)
                e.drain()
                e.tensor_tensor(xjt[:, 1408:1536], xjt[:, 1408:1536], xjt[:, 1152:1280], ALU.mult)
                e.drain()
                e.reduce_sum(s4f[:, cr_ * 8:(cr_ + 1) * 8],
                             xjt[:, 1280:1408].rearrange("p (b n) -> p b n", n=16), AX.X)
                e.drain()
                e.reduce_sum(s4f[:, ci_ * 8:(ci_ + 1) * 8],
                             xjt[:, 1408:1536].rearrange("p (b n) -> p b n", n=16), AX.X)
            pb.step(vector=cch)

    def rdt(e):
        dtb = s4b[:, 0:128:16]
        for idx in range(40):
            e.tensor_tensor(s4f[:, idx * 8:(idx + 1) * 8],
                            s4f[:, idx * 8:(idx + 1) * 8], dtb, ALU.mult)
            e.drain()
    pb.step(vector=rdt)

    def kfm(e):
        e.memset(kblk[:, 8:72], 0.0)
        e.drain()
        for m in range(5):
            c00r, c00i = rcols[(0, m)]; c01r, c01i = rcols[(1, m)]
            c10r, c10i = rcols[(2, m)]; c11r, c11i = rcols[(3, m)]
            def col(c): return s4f[:, c * 8:(c + 1) * 8]
            e.tensor_scalar(s4a[:, 0:8], col(c11r), 1.0, None, ALU.add)
            e.drain()
            e.tensor_tensor(s4a[:, 8:16], col(c01r), col(c10r), ALU.mult)
            e.drain()
            e.tensor_tensor(s4a[:, 16:24], col(c01i), col(c10i), ALU.mult)
            e.drain()
            e.tensor_tensor(s4a[:, 8:16], s4a[:, 8:16], s4a[:, 16:24], ALU.subtract)
            e.drain()
            e.tensor_tensor(s4a[:, 24:32], col(c01r), col(c10i), ALU.mult)
            e.drain()
            e.tensor_tensor(s4a[:, 32:40], col(c01i), col(c10r), ALU.mult)
            e.drain()
            e.tensor_tensor(s4a[:, 24:32], s4a[:, 24:32], s4a[:, 32:40], ALU.add)
            e.drain()
            e.tensor_tensor(s4a[:, 40:48], s4a[:, 0:8], s4a[:, 0:8], ALU.mult)
            e.drain()
            e.tensor_tensor(s4a[:, 48:56], col(c11i), col(c11i), ALU.mult)
            e.drain()
            e.tensor_tensor(s4a[:, 40:48], s4a[:, 40:48], s4a[:, 48:56], ALU.add)
            e.drain()
            e.reciprocal(s4a[:, 48:56], s4a[:, 40:48])
            e.drain()
            e.tensor_tensor(s4a[:, 56:64], s4a[:, 8:16], s4a[:, 0:8], ALU.mult)
            e.drain()
            e.tensor_tensor(s4a[:, 64:72], s4a[:, 24:32], col(c11i), ALU.mult)
            e.drain()
            e.tensor_tensor(s4a[:, 56:64], s4a[:, 56:64], s4a[:, 64:72], ALU.add)
            e.drain()
            e.tensor_tensor(s4a[:, 56:64], s4a[:, 56:64], s4a[:, 48:56], ALU.mult)
            e.drain()
            e.tensor_tensor(s4a[:, 64:72], s4a[:, 24:32], s4a[:, 0:8], ALU.mult)
            e.drain()
            e.tensor_tensor(s4a[:, 72:80], s4a[:, 8:16], col(c11i), ALU.mult)
            e.drain()
            e.tensor_tensor(s4a[:, 64:72], s4a[:, 64:72], s4a[:, 72:80], ALU.subtract)
            e.drain()
            e.tensor_tensor(s4a[:, 64:72], s4a[:, 64:72], s4a[:, 48:56], ALU.mult)
            e.drain()
            e.tensor_tensor(s4a[:, 80:88], col(c00r), s4a[:, 56:64], ALU.subtract)
            e.drain()
            e.tensor_tensor(s4a[:, 88:96], col(c00i), s4a[:, 64:72], ALU.subtract)
            e.drain()
            w2r, w2i = W2[m]
            e.tensor_scalar(s4a[:, 96:104], s4a[:, 80:88], w2r, None, ALU.mult)
            e.drain()
            e.tensor_scalar(s4a[:, 104:112], s4a[:, 88:96], w2i, None, ALU.mult)
            e.drain()
            e.tensor_tensor(s4a[:, 96:104], s4a[:, 96:104], s4a[:, 104:112], ALU.subtract)
            e.drain()
            e.tensor_scalar(s4a[:, 104:112], s4a[:, 80:88], w2i, None, ALU.mult)
            e.drain()
            e.tensor_scalar(s4a[:, 112:120], s4a[:, 88:96], w2r, None, ALU.mult)
            e.drain()
            e.tensor_tensor(s4a[:, 104:112], s4a[:, 104:112], s4a[:, 112:120], ALU.add)
            e.drain()
            for l in range(8):
                wr_, wi_ = IRW[m][l]
                if wr_ != 0.0:
                    e.scalar_tensor_tensor(kblk[:, 8 + l * 8:16 + l * 8],
                                           s4a[:, 96:104], wr_, kblk[:, 8 + l * 8:16 + l * 8],
                                           ALU.mult, ALU.add)
                    e.drain()
                if wi_ != 0.0:
                    e.scalar_tensor_tensor(kblk[:, 8 + l * 8:16 + l * 8],
                                           s4a[:, 104:112], wi_, kblk[:, 8 + l * 8:16 + l * 8],
                                           ALU.mult, ALU.add)
                    e.drain()
    pb.step(vector=kfm)

    for cb in range(NCB):
        def ktr(e, cb=cb):
            for l in range(8):
                e.tensor_copy(s4a[:, 120 + l:121 + l], kblk[:, 8 + l * 8 + cb:9 + l * 8 + cb])
                e.drain()
            e.nop()
        pb.step(vector=ktr)

        def ktp(e):
            e.transpose(P[0][0:8, 0:128], s4a[:, 120:128], ident_f[:, :])
        pb.step(tensor=ktp)

        def kev(e):
            e.tensor_copy(s4b[0:8, 0:128], P[0][0:8, 0:128])
        pb.step(vector=kev)

        def ksl(e):
            e.matmul(P[1][:, 0:8], s4b[0:8, 0:128], wcvs[0:8, 0:8], start=True, stop=True)
        pb.step(tensor=ksl)

        def kse(e, cb=cb):
            e.tensor_copy(ksel[:, cb * 8:(cb + 1) * 8], P[1][:, 0:8])
        pb.step(vector=kse)

    for nm in ("s4f", "s4e", "s4d", "s4c", "s4b", "s4a"):
        pb.free(nm)

    if LIMITB < 2:
        return _finishB(pb)

    for cb in range(NCB):
        def cinit(e, cb=cb):
            pb.dma(e, xjt[:, 0:T], x2rT[0, cb * 128:(cb + 1) * 128, :], "h")
        pb.step(sync=cinit)

        def cacc0(e, cb=cb):
            e.tensor_scalar(accc[:, cb * T:(cb + 1) * T], xjt[:, 0:T],
                            kblk[:, cb:cb + 1], None, ALU.mult)
            e.drain()
            e.scalar_tensor_tensor(accc[:, cb * T:(cb + 1) * T], xjt[:, 0:T],
                                   ksel[:, cb * 8:cb * 8 + 1],
                                   accc[:, cb * T:(cb + 1) * T], ALU.mult, ALU.add)
        pb.step(vector=cacc0)

        for j in range(1, 8):
            par = j % 2

            def cld(e, j=j, cb=cb, par=par):
                pb.dma(e, xjt[:, par * T:(par + 1) * T],
                       x2rT[j, cb * 128:(cb + 1) * 128, :], "h")
            pb.step(sync=cld)

            def cac(e, j=j, cb=cb, par=par):
                e.scalar_tensor_tensor(accc[:, cb * T:(cb + 1) * T],
                                       xjt[:, par * T:(par + 1) * T],
                                       ksel[:, cb * 8 + j:cb * 8 + j + 1],
                                       accc[:, cb * T:(cb + 1) * T], ALU.mult, ALU.add)
            pb.step(vector=cac)

    for cb in range(NCB):
        def sil(e, cb=cb):
            e.activation(accc[:, cb * T:(cb + 1) * T], accc[:, cb * T:(cb + 1) * T],
                         AF.Silu)
        pb.step(scalar=sil)

    for qt in range(NQT):
        def ldx2(e, qt=qt):
            pb.dma(e, xt_l[:, :], x2[qt * 128:(qt + 1) * 128, :], "h")
        pb.step(sync=ldx2)

        def tr3a(e, qt=qt):
            for cb in range(4):
                e.transpose(P[4 + cb // 2][:, (cb % 2) * 128:(cb % 2) * 128 + 128],
                            accc[:, cb * T + qt * 128: cb * T + (qt + 1) * 128],
                            ident_f[:, :])
        pb.step(tensor=tr3a)

        def ev3a(e, qt=qt):
            for cb in range(4):
                e.scalar_tensor_tensor(scrA[:, cb * 128:(cb + 1) * 128],
                                       P[4 + cb // 2][:, (cb % 2) * 128:(cb % 2) * 128 + 128],
                                       1.0, xt_l[:, cb * 128:(cb + 1) * 128],
                                       ALU.mult, ALU.add, accum_out=acol[:, cb:cb + 1])
                e.drain()
        pb.step(vector=ev3a)

        def tr3b(e, qt=qt):
            for cb in range(4, 8):
                e.transpose(P[4 + (cb - 4) // 2][:, ((cb - 4) % 2) * 128:((cb - 4) % 2) * 128 + 128],
                            accc[:, cb * T + qt * 128: cb * T + (qt + 1) * 128],
                            ident_f[:, :])
        pb.step(tensor=tr3b)

        def ev3b(e, qt=qt):
            for cb in range(4, 8):
                e.scalar_tensor_tensor(scrA[:, cb * 128:(cb + 1) * 128],
                                       P[4 + (cb - 4) // 2][:, ((cb - 4) % 2) * 128:((cb - 4) % 2) * 128 + 128],
                                       1.0, xt_l[:, cb * 128:(cb + 1) * 128],
                                       ALU.mult, ALU.add, accum_out=acol[:, cb:cb + 1])
                e.drain()
            e.reduce_sum(acol[:, 8:9], acol[:, 0:8], AX.X)
        pb.step(vector=ev3b)

        _ln_emit(pb, scrA, acol, xt_l, xbf, None, None)

        def sp3(e, qt=qt):
            pb.dma(e, x3sp[qt * 128:(qt + 1) * 128, :], xt_l[:, 0:D], "h")
        pb.step(sync=sp3)

        def t3T(e):
            for cb in range(NCB):
                e.transpose(PT[:, cb * 128:(cb + 1) * 128],
                            xbf[:, cb * 128:(cb + 1) * 128], ident_b[:, :])
        pb.step(tensor=t3T)

        def t3Te(e, qt=qt):
            for cb in range(NCB):
                e.tensor_copy(x3T[:, cb * T + qt * 128: cb * T + (qt + 1) * 128],
                              PT[:, cb * 128:(cb + 1) * 128])
        pb.step(vector=t3Te)

    pb.free("xjt")
    pb.free("accc")

    if LIMITB < 3:
        return _finishB(pb)

    qT = pb.sbm("qT", 128, NCB * T, BF16, side="right")
    v_b = pb.sbm("v_b", 128, NQT * D, BF16, side="right")
    wts = pb.sbm("wts", 128, NCB * D, BF16, side="right")

    def ldw_sec(off):
        def f(e):
            for cb in range(NCB):
                pb.dma(e, wts[:, cb * D:(cb + 1) * D],
                       Wqkv[cb * 128:(cb + 1) * 128, off:off + D], "g")
        return f

    pb.step(gpsimd=ldw_sec(0))
    for dblk in range(NCB):
        def qmm(e, dblk=dblk):
            for ch in range(4):
                for r in range(NCB):
                    e.matmul(P[ch][:, :],
                             wts[:, r * D + dblk * 128: r * D + (dblk + 1) * 128],
                             x3T[:, r * T + ch * 512: r * T + (ch + 1) * 512],
                             start=(r == 0), stop=(r == NCB - 1))
        pb.step(tensor=qmm)

        def qev(e, dblk=dblk):
            for ch in range(4):
                e.tensor_scalar(qT[:, dblk * T + ch * 512: dblk * T + (ch + 1) * 512],
                                P[ch][:, :], 0.125, None, ALU.mult)
        pb.step(vector=qev)

    pb.step(gpsimd=ldw_sec(D))
    for dblk in range(NCB):
        def kmm(e, dblk=dblk):
            for ch in range(4):
                for r in range(NCB):
                    e.matmul(P[ch][:, :],
                             wts[:, r * D + dblk * 128: r * D + (dblk + 1) * 128],
                             x3T[:, r * T + ch * 512: r * T + (ch + 1) * 512],
                             start=(r == 0), stop=(r == NCB - 1))
        pb.step(tensor=kmm)

        def kev(e, dblk=dblk):
            for ch in range(4):
                e.tensor_copy(ebuf[:, ch * 512:(ch + 1) * 512], P[ch][:, :])
                e.drain()
        pb.step(vector=kev)

        def ksp(e, dblk=dblk):
            pb.dma(e, kTsp[dblk * 128:(dblk + 1) * 128, :], ebuf[:, 0:T], "h")
        pb.step(sync=ksp)

    pb.step(gpsimd=ldw_sec(2 * D))
    for qp in range(NQT // 2):
        def vmm(e, qp=qp):
            for half in range(2):
                qt = qp * 2 + half
                for oc in range(2):
                    for r in range(NCB):
                        e.matmul(P[half * 2 + oc][:, :],
                                 x3T[:, r * T + qt * 128: r * T + (qt + 1) * 128],
                                 wts[:, r * D + oc * 512: r * D + (oc + 1) * 512],
                                 start=(r == 0), stop=(r == NCB - 1))
        pb.step(tensor=vmm)

        def vev(e, qp=qp):
            for half in range(2):
                qt = qp * 2 + half
                for oc in range(2):
                    e.tensor_copy(v_b[:, qt * D + oc * 512: qt * D + (oc + 1) * 512],
                                  P[half * 2 + oc][:, :])
        pb.step(vector=vev)

    def ldwr(e):
        for cb in range(NCB):
            pb.dma(e, wts[:, cb * D:(cb + 1) * D], Wr[cb * 128:(cb + 1) * 128, :], "g")
    pb.step(gpsimd=ldwr)

    for ch in range(4):
        def peload2(e, ch=ch):
            for cb in range(4):
                pb.dma(e, ebuf[:, cb * 512:(cb + 1) * 512],
                       peT[cb * 128:(cb + 1) * 128, ch * 512:(ch + 1) * 512], "g")
            for cb in range(4, 8):
                pb.dma(e, bdr[:, (cb - 4) * 512:(cb - 3) * 512],
                       peT[cb * 128:(cb + 1) * 128, ch * 512:(ch + 1) * 512], "g")
        pb.step(gpsimd=peload2)

        def rmm(e, ch=ch):
            for dblk in range(4):
                for r in range(NCB):
                    mv = ebuf[:, r * 512:(r + 1) * 512] if r < 4 else bdr[:, (r - 4) * 512:(r - 3) * 512]
                    e.matmul(P[dblk][:, :],
                             wts[:, r * D + dblk * 128: r * D + (dblk + 1) * 128],
                             mv, start=(r == 0), stop=(r == NCB - 1))
        pb.step(tensor=rmm)

        def rev(e, ch=ch):
            for dblk in range(4):
                e.tensor_copy(scrA[:, dblk * 512:(dblk + 1) * 512], P[dblk][:, :])
                e.drain()
        pb.step(vector=rev)

        def rst(e, ch=ch):
            for dblk in range(4):
                pb.dma(e, rTsp[dblk * 128:(dblk + 1) * 128, ch * 512:(ch + 1) * 512],
                       scrA[:, dblk * 512:(dblk + 1) * 512], "h")
        pb.step(sync=rst)

        def rmm2(e, ch=ch):
            for dblk in range(4, 8):
                for r in range(NCB):
                    mv = ebuf[:, r * 512:(r + 1) * 512] if r < 4 else bdr[:, (r - 4) * 512:(r - 3) * 512]
                    e.matmul(P[dblk - 4][:, :],
                             wts[:, r * D + dblk * 128: r * D + (dblk + 1) * 128],
                             mv, start=(r == 0), stop=(r == NCB - 1))
        pb.step(tensor=rmm2)

        def rev2(e, ch=ch):
            for dblk in range(4, 8):
                e.tensor_copy(scrA[:, (dblk - 4) * 512:(dblk - 3) * 512], P[dblk - 4][:, :])
                e.drain()
        pb.step(vector=rev2)

        def rst2(e, ch=ch):
            for dblk in range(4, 8):
                pb.dma(e, rTsp[dblk * 128:(dblk + 1) * 128, ch * 512:(ch + 1) * 512],
                       scrA[:, (dblk - 4) * 512:(dblk - 3) * 512], "h")
        pb.step(sync=rst2)

    pb.free("wts")

    if LIMITB < 4:
        return _finishB(pb)

    def zf(e):
        e.memset(ebuf[:, :], 0.0)
        e.drain()
        for i in range(17):
            n = min(T * 2049 + 128 - i * 256 * 1024, 256 * 1024)
            if n <= 0:
                break
            pb.dma(e, sc0[i * 256 * 1024: i * 256 * 1024 + n].rearrange("(p f) -> p f", p=128),
                   ebuf[:, 0:n // 128], "g")
            pb.dma(e, sc1[i * 256 * 1024: i * 256 * 1024 + n].rearrange("(p f) -> p f", p=128),
                   ebuf[:, 0:n // 128], "g")
    pb.step(gpsimd=zf)

    att = pb.sbm("att", 64, 3 * T, BF16, side="right")
    rbuf = pb.sbm("rbuf", 64, 2 * T, BF16, side="right")

    scs = (sc0, sc1)
    for h in range(NH):
        dblk = h // 2
        rb = (h % 2) * 64

        def cph(e, h=h):
            pb.dma(e, att[:, 0:T], kTsp[h * 64:(h + 1) * 64, :], "h")
        pb.step(sync=cph)

        def cpq(e, h=h, dblk=dblk, rb=rb):
            e.tensor_scalar(att[:, T:2 * T], qT[rb:rb + 64, dblk * T:(dblk + 1) * T],
                            ubc[:, h:h + 1], None, ALU.add)
            e.tensor_scalar(att[:, 2 * T:3 * T], qT[rb:rb + 64, dblk * T:(dblk + 1) * T],
                            vbc[:, h:h + 1], None, ALU.add)
        pb.step(vector=cpq)

        sc = scs[h % 2]

        def bd_phase(a):
            cw = 128 * (a + 1)
            m_lo = T - cw
            nch = (cw + 511) // 512
            par = a % 2

            def rld(e, a=a, m_lo=m_lo, cw=cw, par=par):
                pb.dma(e, rbuf[:, par * T:par * T + cw],
                       rTsp[h * 64:(h + 1) * 64, m_lo:T], "g")
            pb.step(gpsimd=rld)

            for wv in range(0, nch, 2):
                nw = min(2, nch - wv)

                def bmm(e, a=a, cw=cw, par=par, wv=wv, nw=nw):
                    for i in range(nw):
                        bc = wv + i
                        w = min(512, cw - bc * 512)
                        e.matmul(P[4 + i][:, 0:w],
                                 att[:, 2 * T + a * 128: 2 * T + (a + 1) * 128],
                                 rbuf[:, par * T + bc * 512: par * T + bc * 512 + w],
                                 start=True, stop=True)
                pb.step(tensor=bmm)

                def bev(e, cw=cw, wv=wv, nw=nw):
                    for i in range(nw):
                        bc = wv + i
                        w = min(512, cw - bc * 512)
                        e.tensor_copy(bdr[:, bc * 512: bc * 512 + w], P[4 + i][:, 0:w])
                pb.step(vector=bev)

            def shw(e, a=a, m_lo=m_lo, cw=cw):
                pad = sc[0: T * 2049].rearrange("(i m) -> i m", m=2049)
                pb.dma(e, pad[a * 128:(a + 1) * 128, m_lo:T], bdr[:, 0:cw], "h")
            pb.step(sync=shw)

            def shr(e, a=a, cw=cw, par=par):
                shv = sc[2047: 2047 + T * T].rearrange("(i j) -> i j", j=T)
                pb.dma(e, bdsh[:, par * T: par * T + cw], shv[a * 128:(a + 1) * 128, 0:cw], "h")
            pb.step(sync=shr)

        def score_phase(a):
            cw = 128 * (a + 1)
            nch = (cw + 511) // 512
            par = a % 2
            dblk2 = h // 2
            rb2 = (h % 2) * 64

            def amm(e, a=a, cw=cw, nch=nch):
                for bc in range(nch):
                    w = min(512, cw - bc * 512)
                    e.matmul(P[bc][:, 0:w],
                             att[:, T + a * 128: T + (a + 1) * 128],
                             att[:, bc * 512: bc * 512 + w],
                             start=True, stop=True)
            pb.step(tensor=amm)

            def sadd(e, cw=cw, nch=nch, par=par):
                for bc in range(nch):
                    w = min(512, cw - bc * 512)
                    e.tensor_tensor(scrA[:, bc * 512: bc * 512 + w], P[bc][:, 0:w],
                                    bdsh[:, par * T + bc * 512: par * T + bc * 512 + w],
                                    ALU.add)
            pb.step(vector=sadd)

            def dmask(e, cw=cw):
                e.tensor_tensor(scrA[:, cw - 128: cw], scrA[:, cw - 128: cw],
                                dmsk[:, :], ALU.add)
            pb.step(vector=dmask)

            def eexp(e, cw=cw):
                e.activation(ebuf[:, 0:cw], scrA[:, 0:cw], AF.Exp,
                             accum_out=acol[:, 0:1])
            pb.step(scalar=eexp)

            def esc(e, cw=cw):
                e.reciprocal(acol[:, 1:2], acol[:, 0:1])
                e.drain()
                e.tensor_scalar(ebuf[:, 0:cw], ebuf[:, 0:cw], acol[:, 1:2], None,
                                ALU.mult)
            pb.step(vector=esc)

            for kt0 in range(0, a + 1, 8):
                nb = min(8, a + 1 - kt0)

                def etp(e, kt0=kt0, nb=nb):
                    for i in range(nb):
                        e.transpose(PT[:, i * 128:(i + 1) * 128],
                                    ebuf[:, (kt0 + i) * 128:(kt0 + i + 1) * 128],
                                    ident_b[:, :])
                pb.step(tensor=etp)

                def ete(e, nb=nb):
                    e.tensor_copy(etil[:, 0:nb * 128], PT[:, 0:nb * 128])
                pb.step(vector=ete)

                def avm(e, kt0=kt0, nb=nb, a=a):
                    for i in range(nb):
                        kt = kt0 + i
                        e.matmul(P[6][0:64, 0:128],
                                 v_b[:, kt * D + h * 64: kt * D + (h + 1) * 64],
                                 etil[:, i * 128:(i + 1) * 128],
                                 start=(kt == 0), stop=(kt == a))
                pb.step(tensor=avm)

            def ctxe(e, a=a, dblk2=dblk2, rb2=rb2):
                e.tensor_copy(x3T[rb2:rb2 + 64, dblk2 * T + a * 128: dblk2 * T + (a + 1) * 128],
                              P[6][0:64, 0:128])
            pb.step(vector=ctxe)

        bd_phase(0)
        for a in range(NQT):
            if a + 1 < NQT:
                bd_phase(a + 1)
            score_phase(a)

    ctxT = x3T

    if LIMITB < 5:
        return _finishB(pb)

    pb.free("rbuf")
    pb.free("att")
    pb.free("v_b")
    pb.free("qT")
    wts2 = pb.sbm("wts2", 128, NCB * D, BF16, side="right")
    hT = pb.sbm("hT", 128, NCB * T, BF16, side="right")
    accF = pb.sbm("accF", 128, NQT * D, F32, side="right")
    lnrow = pb.sbm("lnrow", 1, 2 * D, F32, side="right")
    gbG = pb.sbm("gbG", 128, D, F32, side="right")
    gbB = pb.sbm("gbB", 128, D, F32, side="right")

    def ldlp(e, gr, br):
        pb.dma(e, lnrow[0:1, 0:D], gr, "g")
        pb.dma(e, lnrow[0:1, D:2 * D], br, "g")

    def gb_build(tag):
        def t(e):
            for oc in range(2):
                e.matmul(P[0][:, :], ones_f[0:1, 0:128], lnrow[0:1, oc * 512:(oc + 1) * 512],
                         start=True, stop=True) if oc == 0 else None
            e.matmul(P[0][:, :], ones_f[0:1, 0:128], lnrow[0:1, 0:512], start=True, stop=True)
            e.matmul(P[1][:, :], ones_f[0:1, 0:128], lnrow[0:1, 512:1024], start=True, stop=True)
            e.matmul(P[2][:, :], ones_f[0:1, 0:128], lnrow[0:1, 1024:1536], start=True, stop=True)
            e.matmul(P[3][:, :], ones_f[0:1, 0:128], lnrow[0:1, 1536:2048], start=True, stop=True)
        def v(e):
            e.tensor_copy(gbG[:, 0:512], P[0][:, :])
            e.drain()
            e.tensor_copy(gbG[:, 512:1024], P[1][:, :])
            e.drain()
            e.tensor_copy(gbB[:, 0:512], P[2][:, :])
            e.drain()
            e.tensor_copy(gbB[:, 512:1024], P[3][:, :])
        return t, v

    def ldwo(e):
        for cb in range(NCB):
            pb.dma(e, wts2[:, cb * D:(cb + 1) * D], Wo1[cb * 128:(cb + 1) * 128, :], "g")
        ldlp(e, ln1g, ln1b)
    pb.step(gpsimd=ldwo)
    tfn, vfn = gb_build("ln1")
    pb.step(tensor=tfn)
    pb.step(vector=vfn)

    def wo_ln(spill, resid_dram, gb=True):
        for qt in range(NQT):
            def ldres(e, qt=qt):
                pb.dma(e, xt_l[:, :], resid_dram[qt * 128:(qt + 1) * 128, :], "h")
            pb.step(sync=ldres)

            def womm(e, qt=qt):
                for oc in range(2):
                    for r in range(NCB):
                        e.matmul(P[oc][:, :],
                                 ctxT[:, r * T + qt * 128: r * T + (qt + 1) * 128],
                                 wts2[:, r * D + oc * 512: r * D + (oc + 1) * 512],
                                 start=(r == 0), stop=(r == NCB - 1))
            pb.step(tensor=womm)

            def ev(e):
                e.scalar_tensor_tensor(scrA[:, 0:512], P[0][:, :], 1.0, xt_l[:, 0:512],
                                       ALU.mult, ALU.add, accum_out=acol[:, 6:7])
                e.drain()
                e.scalar_tensor_tensor(scrA[:, 512:1024], P[1][:, :], 1.0, xt_l[:, 512:1024],
                                       ALU.mult, ALU.add, accum_out=acol[:, 7:8])
                e.drain()
                e.tensor_tensor(acol[:, 8:9], acol[:, 6:7], acol[:, 7:8], ALU.add)
            pb.step(vector=ev)

            _ln_emit(pb, scrA, acol, xt_l, xbf, None, None)

            if gb:
                def gbap(e):
                    e.tensor_tensor(xt_l[:, 0:D], xt_l[:, 0:D], gbG[:, 0:D], ALU.mult)
                    e.drain()
                    e.tensor_tensor(xt_l[:, 0:D], xt_l[:, 0:D], gbB[:, 0:D], ALU.add)
                    e.drain()
                    e.tensor_copy(xbf[:, 0:D], xt_l[:, 0:D])
                pb.step(vector=gbap)

            def spl(e, qt=qt):
                pb.dma(e, spill[qt * 128:(qt + 1) * 128, :], xt_l[:, 0:D], "h")
            pb.step(sync=spl)

            def trh(e):
                for cb in range(NCB):
                    e.transpose(PT[:, cb * 128:(cb + 1) * 128],
                                xbf[:, cb * 128:(cb + 1) * 128], ident_b[:, :])
            pb.step(tensor=trh)

            def trhe(e, qt=qt):
                for cb in range(NCB):
                    e.tensor_copy(hT[:, cb * T + qt * 128: cb * T + (qt + 1) * 128],
                                  PT[:, cb * 128:(cb + 1) * 128])
            pb.step(vector=trhe)

    wo_ln(h1sp, x3sp, gb=True)

    if LIMITB < 6:
        return _finishB(pb)

    hidT = ctxT

    def emit_ffn(W1, W2d, b1_slice, nsup, e_idx, accF, init_mode):
        for s in range(nsup):
            def ldw1(e, s=s):
                for cb in range(NCB):
                    if e_idx is None:
                        pb.dma(e, wts2[:, cb * D:(cb + 1) * D],
                               W1[cb * 128:(cb + 1) * 128, s * 1024:(s + 1) * 1024], "g")
                    else:
                        pb.dma(e, wts2[:, cb * D:(cb + 1) * D],
                               W1[e_idx, cb * 128:(cb + 1) * 128, s * 1024:(s + 1) * 1024], "g")
            pb.step(gpsimd=ldw1)

            def bload(e, s=s):
                off0 = (0 if e_idx is None else e_idx * FF) + s * 1024
                for fb in range(8):
                    pb.dma(e, bcol[:, fb:fb + 1], b1_slice[off0 + fb * 128: off0 + (fb + 1) * 128, :], "h")
            pb.step(sync=bload)

            PAIRS = ((0, 1), (2, 3), (4, 5))

            def mk_f1mm(fb, hh, pi):
                def f1mm(e, fb=fb, hh=hh, pi=pi):
                    for ci in range(2):
                        ch = hh * 2 + ci
                        for r in range(NCB):
                            e.matmul(P[PAIRS[pi][ci]][:, :],
                                     wts2[:, r * D + fb * 128: r * D + (fb + 1) * 128],
                                     hT[:, r * T + ch * 512: r * T + (ch + 1) * 512],
                                     start=(r == 0), stop=(r == NCB - 1))
                return f1mm

            def mk_f1ev(fb, hh, pi):
                def f1ev(e, fb=fb, hh=hh, pi=pi):
                    for ci in range(2):
                        ch = hh * 2 + ci
                        e.activation(hidT[:, fb * T + ch * 512: fb * T + (ch + 1) * 512],
                                     P[PAIRS[pi][ci]][:, :], AF.Relu, bias=bcol[:, fb:fb + 1])
                return f1ev

            units = [(fb, hh, (fb * 2 + hh) % 3) for fb in range(8) for hh in range(2)]

            def mk_f1ev2(batch):
                def f(e, batch=tuple(batch)):
                    for (fb, hh, pi) in batch:
                        for ci in range(2):
                            ch = hh * 2 + ci
                            e.activation(hidT[:, fb * T + ch * 512: fb * T + (ch + 1) * 512],
                                         P[PAIRS[pi][ci]][:, :], AF.Relu,
                                         bias=bcol[:, fb:fb + 1])
                return f

            pend = []
            for u, unit in enumerate(units):
                pb.step(tensor=mk_f1mm(*unit))
                if len(pend) == 2:
                    pb.step(scalar=mk_f1ev2(pend))
                    pend = []
                pend.append(unit)
            if pend:
                pb.step(scalar=mk_f1ev2(pend))

            def ldw2all(e, s=s):
                for fb in range(8):
                    rw = s * 1024 + fb * 128
                    if e_idx is None:
                        pb.dma(e, wts2[:, fb * D:(fb + 1) * D], W2d[rw:rw + 128, :], "g")
                    else:
                        pb.dma(e, wts2[:, fb * D:(fb + 1) * D], W2d[e_idx, rw:rw + 128, :], "g")
            pb.step(gpsimd=ldw2all)

            def mk_f2mm(qt, pi):
                def f2mm(e, qt=qt, pi=pi):
                    for oc in range(2):
                        for fb in range(8):
                            e.matmul(P[PAIRS[pi][oc]][:, :],
                                     hidT[:, fb * T + qt * 128: fb * T + (qt + 1) * 128],
                                     wts2[:, fb * D + oc * 512: fb * D + (oc + 1) * 512],
                                     start=(fb == 0), stop=(fb == 7))
                return f2mm

            def mk_f2ev(qt, pi):
                def f2ev(e, qt=qt, pi=pi):
                    for oc in range(2):
                        dst = accF[:, qt * D + oc * 512: qt * D + (oc + 1) * 512]
                        psrc = P[PAIRS[pi][oc]][:, :]
                        if init_mode == "add":
                            e.tensor_tensor(dst, psrc, dst, ALU.add)
                        else:
                            e.scalar_tensor_tensor(dst, psrc,
                                                   msks[:, qt * 8 + e_idx: qt * 8 + e_idx + 1],
                                                   dst, ALU.mult, ALU.add)
                return f2ev

            prev = None
            for qt in range(NQT):
                pb.step(tensor=mk_f2mm(qt, qt % 3))
                if prev is not None:
                    pb.step(vector=mk_f2ev(*prev))
                prev = (qt, qt % 3)
            pb.step(vector=mk_f2ev(*prev))

    def ldb2(e):
        pb.dma(e, lnrow[0:1, 0:D], bff2r, "h")
    pb.step(sync=ldb2)
    for qt in range(NQT):
        def b2mm(e, qt=qt):
            e.matmul(P[0][:, :], ones_f[0:1, 0:128], lnrow[0:1, 0:512], start=True, stop=True)
            e.matmul(P[1][:, :], ones_f[0:1, 0:128], lnrow[0:1, 512:1024], start=True, stop=True)
        pb.step(tensor=b2mm)

        def b2ev(e, qt=qt):
            e.tensor_copy(accF[:, qt * D: qt * D + 512], P[0][:, :])
            e.drain()
            e.tensor_copy(accF[:, qt * D + 512: (qt + 1) * D], P[1][:, :])
            e.drain()
        pb.step(vector=b2ev)
    emit_ffn(Wff1, Wff2, bff1c, 4, None, accF, "add")

    def ldlp2(e):
        pb.dma(e, lnrow[0:1, 0:D], ln2g, "h")
        pb.dma(e, lnrow[0:1, D:2 * D], ln2b, "h")
    pb.step(sync=ldlp2)
    tfn2, vfn2 = gb_build("ln2")
    pb.step(tensor=tfn2)
    pb.step(vector=vfn2)

    for qt in range(NQT):
        def ldh1(e, qt=qt):
            pb.dma(e, xt_l[:, :], h1sp[qt * 128:(qt + 1) * 128, :], "h")
        pb.step(sync=ldh1)

        def ev2(e, qt=qt):
            e.scalar_tensor_tensor(scrA[:, 0:512], accF[:, qt * D: qt * D + 512],
                                   1.0, xt_l[:, 0:512], ALU.mult, ALU.add,
                                   accum_out=acol[:, 6:7])
            e.drain()
            e.scalar_tensor_tensor(scrA[:, 512:1024], accF[:, qt * D + 512:(qt + 1) * D],
                                   1.0, xt_l[:, 512:1024], ALU.mult, ALU.add,
                                   accum_out=acol[:, 7:8])
            e.drain()
            e.tensor_tensor(acol[:, 8:9], acol[:, 6:7], acol[:, 7:8], ALU.add)
        pb.step(vector=ev2)

        _ln_emit(pb, scrA, acol, xt_l, xbf, None, None)

        def gbap2(e):
            e.tensor_tensor(xt_l[:, 0:D], xt_l[:, 0:D], gbG[:, 0:D], ALU.mult)
            e.drain()
            e.tensor_tensor(xt_l[:, 0:D], xt_l[:, 0:D], gbB[:, 0:D], ALU.add)
            e.drain()
            e.tensor_copy(xbf[:, 0:D], xt_l[:, 0:D])
        pb.step(vector=gbap2)

        def spl2(e, qt=qt):
            pb.dma(e, h2sp[qt * 128:(qt + 1) * 128, :], xt_l[:, 0:D], "h")
        pb.step(sync=spl2)

        def trh2(e):
            for cb in range(NCB):
                e.transpose(PT[:, cb * 128:(cb + 1) * 128],
                            xbf[:, cb * 128:(cb + 1) * 128], ident_b[:, :])
        pb.step(tensor=trh2)

        def trhe2(e, qt=qt):
            for cb in range(NCB):
                e.tensor_copy(hT[:, cb * T + qt * 128: cb * T + (qt + 1) * 128],
                              PT[:, cb * 128:(cb + 1) * 128])
                e.drain()
        pb.step(vector=trhe2)

    if LIMITB < 7:
        return _finishB(pb)

    mskb = pb.sbm("mskb", 128, 8, BF16, side="right")
    for qt in range(NQT):
        def gmm(e, qt=qt):
            for r in range(NCB):
                e.matmul(P[0][:, 0:8],
                         hT[:, r * T + qt * 128: r * T + (qt + 1) * 128],
                         wgsb[:, r * 8:(r + 1) * 8],
                         start=(r == 0), stop=(r == NCB - 1))
        pb.step(tensor=gmm)

        def gred(e, qt=qt):
            e.reduce_max(acol[:, 0:1], P[0][:, 0:8], AX.X)
            e.drain()
            e.tensor_scalar(acol[:, 1:2], acol[:, 0:1], -1.0, None, ALU.mult)
            e.drain()
            e.tensor_scalar(msks[:, qt * 8:(qt + 1) * 8], P[0][:, 0:8],
                            acol[:, 0:1], None, ALU.is_ge)
            e.drain()
            e.tensor_copy(mskb[:, 0:8], msks[:, qt * 8:(qt + 1) * 8])
            e.drain()
        pb.step(vector=gred)

        def gexp(e, qt=qt):
            e.activation(scrA[:, 0:8], P[0][:, 0:8], AF.Exp, bias=acol[:, 1:2],
                         accum_out=acol[:, 2:3])
        pb.step(scalar=gexp)

        def ginv(e, qt=qt):
            e.reciprocal(gate_c[:, qt:qt + 1], acol[:, 2:3])
        pb.step(vector=ginv)

        def gtp(e, qt=qt):
            e.transpose(PT[0:8, 0:128], mskb[:, 0:8], ident_b[:, :])
        pb.step(tensor=gtp)

        def gte(e, qt=qt):
            e.tensor_copy(mskT[0:8, qt * 128:(qt + 1) * 128], PT[0:8, 0:128])
        pb.step(vector=gte)

    for qt in range(NQT):
        def bemm(e, qt=qt):
            for oc in range(2):
                e.matmul(P[oc][:, :], mskT[0:8, qt * 128:(qt + 1) * 128],
                         be2s[0:8, oc * 512:(oc + 1) * 512], start=True, stop=True)
        pb.step(tensor=bemm)

        def beev(e, qt=qt):
            e.tensor_copy(accF[:, qt * D: qt * D + 512], P[0][:, :])
            e.drain()
            e.tensor_copy(accF[:, qt * D + 512:(qt + 1) * D], P[1][:, :])
            e.drain()
        pb.step(vector=beev)

    for ei in range(NE):
        emit_ffn(We1, We2, be1c, 4, ei, accF, "mask")

    def ldlp3(e):
        pb.dma(e, lnrow[0:1, 0:D], ln3g, "h")
        pb.dma(e, lnrow[0:1, D:2 * D], ln3b, "h")
    pb.step(sync=ldlp3)
    tfn3, vfn3 = gb_build("ln3")
    pb.step(tensor=tfn3)
    pb.step(vector=vfn3)

    for qt in range(NQT):
        def ldh2(e, qt=qt):
            pb.dma(e, xt_l[:, :], h2sp[qt * 128:(qt + 1) * 128, :], "h")
        pb.step(sync=ldh2)

        def ev3(e, qt=qt):
            e.tensor_scalar(accF[:, qt * D:(qt + 1) * D], accF[:, qt * D:(qt + 1) * D],
                            gate_c[:, qt:qt + 1], None, ALU.mult)
            e.drain()
            e.scalar_tensor_tensor(scrA[:, 0:D], accF[:, qt * D:(qt + 1) * D],
                                   1.0, xt_l[:, 0:D], ALU.mult, ALU.add,
                                   accum_out=acol[:, 8:9])
            e.drain()
        pb.step(vector=ev3)

        _ln_emit(pb, scrA, acol, xt_l, xbf, None, None)

        def gbap3(e):
            e.tensor_tensor(xt_l[:, 0:D], xt_l[:, 0:D], gbG[:, 0:D], ALU.mult)
            e.drain()
            e.tensor_tensor(xt_l[:, 0:D], xt_l[:, 0:D], gbB[:, 0:D], ALU.add)
            e.drain()
            e.tensor_copy(xbf[:, 0:D], xt_l[:, 0:D])
        pb.step(vector=gbap3)

        def trh3(e):
            for cb in range(NCB):
                e.transpose(PT[:, cb * 128:(cb + 1) * 128],
                            xbf[:, cb * 128:(cb + 1) * 128], ident_b[:, :])
        pb.step(tensor=trh3)

        def trhe3(e, qt=qt):
            for cb in range(NCB):
                e.tensor_copy(hT[:, cb * T + qt * 128: cb * T + (qt + 1) * 128],
                              PT[:, cb * 128:(cb + 1) * 128])
                e.drain()
        pb.step(vector=trhe3)

    def ldwout(e):
        for cb in range(NCB):
            pb.dma(e, wts2[:, cb * D:(cb + 1) * D], Wout[cb * 128:(cb + 1) * 128, :], "g")
    pb.step(gpsimd=ldwout)

    for qt in range(NQT):
        def ldx3(e, qt=qt):
            pb.dma(e, xt_l[:, :], x3sp[qt * 128:(qt + 1) * 128, :], "h")
        pb.step(sync=ldx3)

        def omm(e, qt=qt):
            for oc in range(2):
                for r in range(NCB):
                    e.matmul(P[oc][:, :],
                             hT[:, r * T + qt * 128: r * T + (qt + 1) * 128],
                             wts2[:, r * D + oc * 512: r * D + (oc + 1) * 512],
                             start=(r == 0), stop=(r == NCB - 1))
        pb.step(tensor=omm)

        def evo(e):
            e.scalar_tensor_tensor(scrA[:, 0:512], P[0][:, :], 1.0, xt_l[:, 0:512],
                                   ALU.mult, ALU.add, accum_out=acol[:, 6:7])
            e.drain()
            e.scalar_tensor_tensor(scrA[:, 512:1024], P[1][:, :], 1.0, xt_l[:, 512:1024],
                                   ALU.mult, ALU.add, accum_out=acol[:, 7:8])
            e.drain()
            e.tensor_tensor(acol[:, 8:9], acol[:, 6:7], acol[:, 7:8], ALU.add)
        pb.step(vector=evo)

        _ln_emit(pb, scrA, acol, xt_l, xbf, None, None)

        def sto(e, qt=qt):
            pb.dma(e, xo[qt * 128:(qt + 1) * 128, :], xt_l[:, 0:D], "h")
        pb.step(sync=sto)

    return _finishB(pb)


def _finishB(pb):
    hf = pb.c["h"]; gf = pb.c["g"]

    def fence(e):
        if hf:
            e.wait_ge(pb._sems["h"], hf)
        if gf:
            e.wait_ge(pb._sems["g"], gf)
    pb.block.sync(fence)
    return pb.done()


def _ln_emit(pb, xr_sb, acol, out_f32, out_bf, gcol, bcol):
    AFx = AF

    def sq(e):
        e.scalar_tensor_tensor(out_bf[:, 0:D], xr_sb[:, 0:D], 1.0, xr_sb[:, 0:D],
                               ALU.mult, ALU.mult, accum_out=acol[:, 9:10])
    pb.step(vector=sq)

    def stats(e):
        e.tensor_scalar(acol[:, 10:11], acol[:, 8:9], 1.0 / D, None, ALU.mult)
        e.drain()
        e.tensor_tensor(acol[:, 11:12], acol[:, 10:11], acol[:, 10:11], ALU.mult)
        e.drain()
        e.scalar_tensor_tensor(acol[:, 12:13], acol[:, 9:10], 1.0 / D, acol[:, 11:12],
                               ALU.mult, ALU.subtract)
        e.drain()
        e.tensor_scalar(acol[:, 12:13], acol[:, 12:13], 1e-5, None, ALU.add)
    pb.step(vector=stats)

    def rstd(e):
        e.activation(acol[:, 13:14], acol[:, 12:13], AF.Ln)
        e.drain()
        e.activation(acol[:, 14:15], acol[:, 13:14], AF.Exp, scale=-0.5)
    pb.step(scalar=rstd)

    def nmr(e):
        e.scalar_tensor_tensor(acol[:, 15:16], acol[:, 10:11], -1.0, acol[:, 14:15],
                               ALU.mult, ALU.mult)
    pb.step(vector=nmr)

    def norm(e):
        e.tensor_scalar(out_f32[:, 0:D], xr_sb[:, 0:D], acol[:, 14:15], acol[:, 15:16],
                        ALU.mult, ALU.add)
        e.tensor_scalar(out_bf[:, 0:D], xr_sb[:, 0:D], acol[:, 14:15], acol[:, 15:16],
                        ALU.mult, ALU.add)
    pb.step(vector=norm)


def _s4_consts():
    L = 8; M = 5
    om = np.exp(np.complex64(-2j * np.pi / L) * np.arange(M, dtype=np.complex64))
    om = om.astype(np.complex64)
    z = (2 * (1 - om) / (1 + om)).astype(np.complex64)
    w2 = (2 / (1 + om)).astype(np.complex64)
    Z = [(float(np.float32(z[m].real)), float(np.float32(z[m].imag))) for m in range(M)]
    W2 = [(float(np.float32(w2[m].real)), float(np.float32(w2[m].imag))) for m in range(M)]
    IRW = []
    for m in range(M):
        row = []
        for l in range(L):
            if m == 0:
                wr_, wi_ = 1.0 / L, 0.0
            elif m == 4:
                wr_, wi_ = ((-1.0) ** l) / L, 0.0
            else:
                wr_ = 2.0 * np.cos(2 * np.pi * m * l / L) / L
                wi_ = -2.0 * np.sin(2 * np.pi * m * l / L) / L
            row.append((float(np.float32(wr_)), float(np.float32(wi_))))
        IRW.append(row)
    return Z, W2, IRW


def prep_A(inputs, b):
    x = np.asarray(inputs["L0_attn_x"])[b]
    rm = np.asarray(inputs["L0_ffn_rail_maker"])
    up = np.asarray(inputs["L0_ffn_rail_up_proj"])
    bdm = np.zeros((32, 128, 128), np.float32)
    for hh in range(4):
        for g8 in range(8):
            for rr in range(8):
                r = g8 * 8 + rr
                bdm[hh * 8 + g8, 16 * rr:16 * rr + 16, 16 * rr:16 * rr + 16] = rm[hh, r]
    uts = np.zeros((128, 128, 128), np.float32)
    for s in range(8):
        hh = s // 2
        rbase = (s % 2) * 32
        for m in range(16):
            gl = m // 4
            gs = rbase + 8 * gl
            for pi in range(2):
                r = rbase + 2 * m + pi
                uts[s * 16 + m, 16 * (r - gs):16 * (r - gs) + 16,
                    64 * pi:64 * pi + 64] = up[hh, r].T
    pm = np.concatenate([np.ones(64, np.float32), -np.ones(64, np.float32)])
    bdo = np.zeros((128, 128), np.float32)
    bdo[0:64, 0:64] = 1.0
    bdo[64:128, 64:128] = 1.0
    return {
        "x0": np.ascontiguousarray(x, np.float32),
        "x0Td": np.ascontiguousarray(x.T, np.float32),
        "WqT": np.ascontiguousarray(np.asarray(inputs["L0_attn_Wq"]).T, np.float32),
        "WkT": np.ascontiguousarray(np.asarray(inputs["L0_attn_Wk"]).T, np.float32),
        "WvT": np.ascontiguousarray(np.asarray(inputs["L0_attn_Wv"]).T, np.float32),
        "WoT": np.ascontiguousarray(np.asarray(inputs["L0_attn_Wo"]).T, np.float32),
        "bo": np.asarray(inputs["L0_attn_bo"], np.float32).reshape(1, D),
        "lq1": np.asarray(inputs["L0_attn_lambda_q1"], np.float32).reshape(1, 64),
        "lk1": np.asarray(inputs["L0_attn_lambda_k1"], np.float32).reshape(1, 64),
        "lq2": np.asarray(inputs["L0_attn_lambda_q2"], np.float32).reshape(1, 64),
        "lk2": np.asarray(inputs["L0_attn_lambda_k2"], np.float32).reshape(1, 64),
        "dgr": np.asarray(inputs["L0_attn_delta_gain"], np.float32).reshape(1, 8),
        "chd": np.asarray(inputs["L0_attn_cos_head_delta"], np.float32).reshape(1, 8),
        "raw": np.asarray(inputs["L0_attn_cos_logit_scale_raw"], np.float32).reshape(1, 1),
        "subw": np.asarray(inputs["L0_attn_subln_weight"], np.float32).reshape(128, 1),
        "pm64": pm.reshape(128, 1),
        "bdo": bdo,
        "bdm": bdm,
        "uts": uts,
        "wcT": np.ascontiguousarray(np.asarray(inputs["L0_ffn_w_cproj"]).T, np.float32),
    }



def prep_B(inputs, x2_full, b):
    g = lambda k: np.asarray(inputs[k], np.float32)

    def arr_pbn(a):
        return np.ascontiguousarray(a.reshape(8, 128, 16).transpose(1, 0, 2).reshape(128, 128))

    x2 = np.ascontiguousarray(x2_full[b], np.float32)
    x2rT = np.stack([np.ascontiguousarray(x2_full[(b - j) % 8].T) for j in range(8)], 0)
    wcvT = np.zeros((8, 8), np.float32)
    for j in range(8):
        if b - j >= 0:
            wcvT[j, j] = 1.0
    ldt = g("L1_attn_log_dt").reshape(1024)
    ldt_bc = np.repeat(ldt.reshape(8, 128).T.reshape(128, 8, 1), 16, 2).reshape(128, 128)
    B_ = g("L1_attn_B"); P_ = g("L1_attn_P")[0]; C_ = g("L1_attn_C")
    pe_pos = np.arange(T - 1, -1, -1, dtype=np.float32)
    inv_freq = (1.0 / (10000.0 ** (np.arange(0, D, 2, dtype=np.float32) / np.float32(D)))).astype(np.float32)
    sinusoid = pe_pos[:, None] * inv_freq[None, :]
    pe = np.concatenate([np.sin(sinusoid), np.cos(sinusoid)], 1).astype(np.float32)
    return {
        "x2": x2,
        "x2rT": np.ascontiguousarray(x2rT, np.float32),
        "wcvT": wcvT,
        "ldt_bc": np.ascontiguousarray(ldt_bc, np.float32),
        "iwr_a": arr_pbn(g("L1_attn_inv_w_real")),
        "wim_a": arr_pbn(g("L1_attn_w_imaginary")),
        "Br_a": arr_pbn(B_[:, :, 0]), "Bi_a": arr_pbn(B_[:, :, 1]),
        "Pr_a": arr_pbn(P_[:, :, 0]), "Pi_a": arr_pbn(P_[:, :, 1]),
        "Cr_a": arr_pbn(C_[:, :, 0]), "Ci_a": arr_pbn(C_[:, :, 1]),
        "Dd_c": np.ascontiguousarray(g("L1_attn_D").reshape(8, 128).T),
        "peT": np.ascontiguousarray(pe.T),
        "Wqkv": g("L1_ffn_Wqkv"),
        "Wo1": g("L1_ffn_Wo"),
        "Wr": g("L1_ffn_Wr"),
        "ubT": np.ascontiguousarray(g("L1_ffn_u_bias").T),
        "vbT": np.ascontiguousarray(g("L1_ffn_v_bias").T),
        "ln1g": g("L1_ffn_ln1_g").reshape(1, D), "ln1b": g("L1_ffn_ln1_b").reshape(1, D),
        "ln2g": g("L1_ffn_ln2_g").reshape(1, D), "ln2b": g("L1_ffn_ln2_b").reshape(1, D),
        "ln3g": g("L1_ffn_ln3_g").reshape(1, D), "ln3b": g("L1_ffn_ln3_b").reshape(1, D),
        "Wff1": g("L1_ffn_Wff1"), "bff1c": g("L1_ffn_bff1").reshape(FF, 1),
        "Wff2": g("L1_ffn_Wff2"), "bff2r": g("L1_ffn_bff2").reshape(1, D),
        "Wg": g("L1_ffn_Wg"),
        "We1": g("L1_ffn_We1"), "be1c": g("L1_ffn_be1").reshape(NE * FF, 1),
        "We2": g("L1_ffn_We2"), "be2m": g("L1_ffn_be2"),
        "Wout": g("L1_ffn_Wout"),
    }


_CACHE = {}


def run_A(inputs):
    if "A" not in _CACHE:
        _CACHE["A"] = build_launchA()
    ncA = _CACHE["A"]
    maps = [prep_A(inputs, b) for b in range(NCORE)]
    res = run_bass_kernel_spmd(ncA, maps, core_ids=list(range(NCORE)))
    return np.stack([res.results[b]["xout"] for b in range(NCORE)], 0)


def run_B(inputs, x2_full):
    if "B" not in _CACHE:
        _CACHE["B"] = build_launchB()
    ncB = _CACHE["B"]
    maps = [prep_B(inputs, x2_full, b) for b in range(NCORE)]
    res = run_bass_kernel_spmd(ncB, maps, core_ids=list(range(NCORE)))
    return np.stack([res.results[b]["xo"] for b in range(NCORE)], 0)


def kernel(**inputs):
    x2 = run_A(inputs)
    x4 = run_B(inputs, x2)
    return x4
